# version 19
# baseline (speedup 1.0000x reference)
import numpy as np
import ml_dtypes

import concourse.bass as bass
import concourse.mybir as mybir
from concourse.bass_utils import run_bass_kernel_spmd

F32 = mybir.dt.float32
BF16 = mybir.dt.bfloat16
BF = ml_dtypes.bfloat16
AF = mybir.ActivationFunctionType
OP = mybir.AluOpType
AX = mybir.AxisListType

D = 1024
T = 2048
DEBUG_TAPS = False
PHASE_LIM = 99
OLD_RECIPS = True


class _Stop(Exception):
    pass


def _host_consts():
    d = np.arange(D)
    f = np.arange(512)
    ang = 2 * np.pi * np.outer(d, f) / D
    CS = np.zeros((D, 1026), np.float32)
    CS[:, 0:512] = np.cos(ang)
    CS[:, 512:1024] = -np.sin(ang)
    CS[:, 1024] = np.cos(np.pi * d)
    IDFT = np.zeros((1026, D), np.float32)
    coef = np.where(f == 0, 1.0, 2.0) / D
    IDFT[0:512] = coef[:, None] * np.cos(ang.T)
    IDFT[512:1024] = (-2.0 / D) * np.sin(ang.T)
    IDFT[1024] = np.cos(np.pi * d) / D
    be = np.arange(512)
    RfT = np.zeros((32, 512), np.float32); RfT[be // 16, be] = 1
    RqfT = np.zeros((16, 512), np.float32); RqfT[be % 16, be] = 1
    RpT = np.zeros((32, 512), np.float32)
    RpT[(be // 128) * 8 + (be % 128) // 16, be] = 1
    p = np.arange(128)
    RqpT = np.zeros((16, 128), np.float32); RqpT[p % 16, p] = 1
    return {
        "CONST_CS": CS.astype(BF),
        "CONST_IDFT": IDFT.astype(np.float32),
        "CONST_RfT": RfT.astype(BF),
        "CONST_RqfT": RqfT.astype(BF),
        "CONST_RpT": RpT.astype(BF),
        "CONST_RqpT": RqpT.astype(BF),
    }


def build_nc():
    from contextlib import ExitStack
    nc = bass.Bass("TRN2", target_bir_lowering=False, detect_race_conditions=False)

    def din(name, shape, dt=F32):
        return nc.dram_tensor(name, shape, dt, kind="ExternalInput").ap()

    xT = din("xT", [D, T])
    Wqkv = din("Wqkv", [D, 3 * D])
    bqkv = din("bqkv", [3 * D])
    Wout = din("Wout", [D, D])
    bout = din("bout", [D])
    Wq_b = din("Wq_b", [D, D])
    bq_b = din("bq_b", [D])
    WkT = din("WkT", [D, D])
    bk_b = din("bk_b", [D])
    WvT = din("WvT", [D, D])
    bv_b = din("bv_b", [D])
    G1R = din("G1R", [1, 32, 4, 32])
    G2R = din("G2R", [4, 16, 1, 16])
    G1I = din("G1I", [1, 32, 4, 32])
    G2I = din("G2I", [4, 16, 1, 16])
    fbias = din("fbias", [D])
    CS_h = din("CONST_CS", [D, 1026], BF16)
    IDFT_h = din("CONST_IDFT", [1026, D])
    RfT_h = din("CONST_RfT", [32, 512], BF16)
    RqfT_h = din("CONST_RqfT", [16, 512], BF16)
    RpT_h = din("CONST_RpT", [32, 512], BF16)
    RqpT_h = din("CONST_RqpT", [16, 128], BF16)

    outT = nc.dram_tensor("outT", [D, T], F32, kind="ExternalOutput").ap()
    s2h = nc.dram_tensor("s2h", [2 * D], BF16, kind="Internal").ap()
    wsth = nc.dram_tensor("wsth", [12 * 128, 512], BF16, kind="Internal").ap()

    import kernel as _self

    with ExitStack() as ctx:
        def sbuf(name, shape, dt):
            return ctx.enter_context(nc.sbuf_tensor(name, shape, dt))

        X = [sbuf(f"X{i}", [128, T], F32) for i in range(8)]
        A = [sbuf(f"Ax{i}", [128, T], BF16) for i in range(8)]
        B = [sbuf(f"Bx{i}", [128, T], BF16) for i in range(8)]
        C = [sbuf(f"Cx{i}", [128, T], BF16) for i in range(8)]
        Dv = [sbuf(f"Dx{i}", [128, 1040], BF16) for i in range(16)]
        SBF0 = sbuf("SBF0", [128, T], BF16)
        PT = [sbuf("PT0", [128, 1024], BF16), sbuf("PT1", [128, 1024], BF16)]
        SF0 = sbuf("SF0", [128, 1024], F32)
        WVH = PT
        ones_bf = sbuf("ones_bf", [128, 128], BF16)
        ones_f = sbuf("ones_f", [128, 128], F32)
        ones1f = sbuf("ones1f", [1, 64], F32)
        colC = sbuf("colC", [128, 72], F32)
        colB = sbuf("colB", [128, 16], BF16)
        negc = sbuf("negc", [128, 8], F32)
        stq = sbuf("stq", [128, 8], F32)
        Shat = sbuf("Shat", [128, 48], F32)
        RED = sbuf("RED", [128, 68], F32)
        WNY = sbuf("WNY", [128, 16], BF16)
        recips = SF0
        s_bf = SBF0
        bvrow = SBF0
        onesr_bf = ones_bf
        RfT_s = B[0]
        RqfT_s = B[0]
        RpT_s = B[0]
        RqpT_s = B[0]
        g1s = B[2]
        g2s = B[2]
        T1 = B[1]
        T1q = B[3]
        brep = B[2]

        ps0 = ctx.enter_context(nc.psum_tensor("ps0", [128, T], F32))
        ps1 = ctx.enter_context(nc.psum_tensor("ps1", [128, T], F32))
        PS = [ps0, ps1]

        semn = ["dS", "dG", "sT", "sC", "sV", "sS", "sPV", "sE", "sRaw",
                "sDR", "sNm", "sStab", "sWc", "sST", "sPS0", "sPS1", "sRc"]
        sems = {n: ctx.enter_context(nc.semaphore(n)) for n in semn}
        cnt = {n: 0 for n in semn}

        blk_real = ctx.enter_context(nc.Block())

        class _BlkProxy:
            stop = False
            def __getattr__(self, eng):
                if self.stop:
                    return lambda f: None
                return getattr(blk_real, eng)
        blk = _BlkProxy()

        def ck(n):
            if _self.PHASE_LIM <= n:
                blk.__class__.stop = True

        def inc(ins_, name, amt=1):
            cnt[name] += amt
            ins_.then_inc(sems[name], amt)
            return cnt[name]

        def wt(e, name, val):
            if val > 0:
                e.wait_ge(sems[name], val)

        psw = [0, 0]

        def ps_acq(e, a):
            if psw[a] > 0:
                e.wait_ge(sems["sPS%d" % a], psw[a])
            psw[a] += 1

        def ps_rel(ins_, a):
            inc(ins_, "sPS%d" % a)

        tap_seen = {}
        def tap(name, aps, waits):
            if not DEBUG_TAPS:
                return
            if name in tap_seen:
                name = name + str(tap_seen[name])
            tap_seen[name] = tap_seen.get(name, 0) + 1
            rows = aps[0].shape[0]
            cols = aps[0].shape[1]
            dt_ = aps[0].dtype
            h = nc.dram_tensor(name, [len(aps), rows, cols], dt_,
                               kind="ExternalOutput").ap()
            @blk.sync
            def _(sy):
                for (nm, val) in waits:
                    wt(sy, nm, val)
                for i, ap_ in enumerate(aps):
                    inc(sy.dma_start(out=h[i], in_=ap_), "dS", 16)

        @blk.sync
        def _(sy):
            for i in range(8):
                inc(sy.dma_start(out=X[i][:, :],
                                 in_=xT[i * 128:(i + 1) * 128, :]), "dS", 16)
            for c in range(8):
                inc(sy.dma_start(out=colC[:, c:c + 1],
                                 in_=bqkv[c * 128:(c + 1) * 128][:, None]), "dS", 16)
            for c in range(8):
                inc(sy.dma_start(out=colC[:, 8 + c:9 + c],
                                 in_=bqkv[1024 + c * 128:1024 + (c + 1) * 128][:, None]), "dS", 16)
            for c in range(8):
                inc(sy.dma_start(out=colC[:, 16 + c:17 + c],
                                 in_=bout[c * 128:(c + 1) * 128][:, None]), "dS", 16)
            for c in range(8):
                inc(sy.dma_start(out=colC[:, 24 + c:25 + c],
                                 in_=fbias[c * 128:(c + 1) * 128][:, None]), "dS", 16)
            for c in range(8):
                inc(sy.dma_start(out=colC[:, 32 + c:33 + c],
                                 in_=bq_b[c * 128:(c + 1) * 128][:, None]), "dS", 16)
        dS_x = [16 * (i + 1) for i in range(8)]
        dS_prolog = cnt["dS"]

        @blk.gpsimd
        def _(g):
            for c in range(8):
                inc(g.dma_start(out=colB[:, c:c + 1],
                                in_=bk_b[c * 128:(c + 1) * 128][:, None]), "dG", 16)
            for c in range(8):
                inc(g.dma_start(out=colB[:, 8 + c:9 + c],
                                in_=bv_b[c * 128:(c + 1) * 128][:, None]), "dG", 16)
        dG_prolog = cnt["dG"]

        @blk.vector
        def _(v):
            ins_ = v.memset(ones_bf[:, :], 1.0)
            ins_ = v.memset(ones_f[:, :], 1.0)
            ins_ = v.memset(ones1f[:, :], 1.0)
            ins_ = v.memset(colC[:, :], 0.0)
            ins_ = v.memset(stq[:, :], 0.0)
            ins_ = v.memset(Shat[:, :], 0.0)
            ins_ = v.memset(negc[:, :], 0.0)
            for t in range(16):
                dview = Dv[t][:, :].rearrange("p (h f) -> p h f", f=65)
                ins_ = v.memset(dview[:, :, 64:65], 1.0)
            inc(ins_, "sV")
        ones_ready = cnt["sV"]

        def emit_ln(src, dst, chunk_ready, dst_free=None, db=None):
            sq_done = []
            st_done = []
            for i in range(8):
                use_pt = db is not None and i % 2 == 1
                @blk.vector
                def _(v, i=i, use_pt=use_pt):
                    for (nm, val) in chunk_ready[i]:
                        wt(v, nm, val)
                    if i == 1 and db is not None:
                        wt(v, db[0], db[1])
                    lim = 2 if db is not None else 1
                    if i >= lim:
                        wt(v, "sT", st_done[i - lim])
                    if use_pt:
                        ins_ = v.tensor_tensor(PT[0][:, :], src[i][:, 0:1024],
                                               src[i][:, 0:1024], OP.mult)
                        inc(v.tensor_tensor(PT[1][:, :], src[i][:, 1024:2048],
                                            src[i][:, 1024:2048], OP.mult), "sV")
                    else:
                        inc(v.tensor_tensor(SBF0[:, :], src[i][:, :], src[i][:, :],
                                            OP.mult), "sV")
                sq_done.append(cnt["sV"])

                @blk.tensor
                def _(t, i=i, use_pt=use_pt):
                    if i == 0:
                        wt(t, "sV", ones_ready)
                        ps_acq(t, 0)
                        ps_acq(t, 1)
                    wt(t, "sV", sq_done[i])
                    st = (i == 0); sp = (i == 7)
                    for h in range(2):
                        for p in range(2):
                            ins_ = t.matmul(
                                PS[h][:, p * 512:(p + 1) * 512], ones_f[:, :],
                                src[i][:, h * 1024 + p * 512:h * 1024 + (p + 1) * 512],
                                start=st, stop=sp)
                    for h in range(2):
                        for p in range(2):
                            sqs = (PT[h][:, p * 512:(p + 1) * 512] if use_pt else
                                   SBF0[:, h * 1024 + p * 512:h * 1024 + (p + 1) * 512])
                            ins_ = t.matmul(
                                PS[h][:, 1024 + p * 512:1024 + (p + 1) * 512],
                                ones_bf[:, :], sqs,
                                start=st, stop=sp)
                    inc(ins_, "sT")
                st_done.append(cnt["sT"])

            qmarks = []
            for h in range(2):
                for q in range(2):
                    @blk.vector
                    def _(v, h=h, q=q):
                        if h == 0 and q == 0:
                            wt(v, "sT", st_done[7])
                        pa = PS[h]
                        ins_ = v.tensor_scalar(SF0[:, 0:512],
                                               pa[:, 512 * q:512 * q + 512],
                                               1.0 / 1024, None, OP.mult)
                        ins_ = v.tensor_scalar(SF0[:, 512:1024],
                                               pa[:, 1024 + 512 * q:1536 + 512 * q],
                                               1.0 / 1024, 1e-5, OP.mult, OP.add)
                        ins_ = v.tensor_tensor(pa[:, 512 * q:512 * q + 512],
                                               SF0[:, 0:512], SF0[:, 0:512], OP.mult)
                        ins_ = v.tensor_tensor(SF0[:, 512:1024], SF0[:, 512:1024],
                                               pa[:, 512 * q:512 * q + 512],
                                               OP.subtract)
                        inc(v.memset(stq[:, 6:7], 0.0), "sV")
                        if q == 1:
                            ps_rel(ins_, h)
                    rec_done = cnt["sV"]

                    @blk.scalar
                    def _(s, h=h, q=q, rec_done=rec_done):
                        wt(s, "sV", rec_done)
                        ins_ = s.activation(SF0[:, 512:1024],
                                            SF0[:, 512:1024], AF.Ln)
                        ins_ = s.activation(SF0[:, 512:1024],
                                            SF0[:, 512:1024],
                                            AF.Exp, scale=-0.5)
                        ins_ = s.activation(SBF0[:, 512:1024],
                                            SF0[:, 512:1024], AF.Identity)
                        inc(s.activation(stq[:, 7:8], stq[:, 6:7], AF.Identity),
                            "sC")
                    sqrt_done = cnt["sC"]

                    @blk.vector
                    def _(v, h=h, q=q, sqrt_done=sqrt_done):
                        wt(v, "sC", sqrt_done)
                        if h == 0 and q == 0 and dst_free is not None:
                            wt(v, dst_free[0], dst_free[1])
                        c0 = h * 1024 + 512 * q
                        for i in range(8):
                            ins_ = v.tensor_tensor(SBF0[:, 0:512],
                                                   src[i][:, c0:c0 + 512],
                                                   SF0[:, 0:512], OP.subtract)
                            ins_ = v.tensor_tensor(dst[i][:, c0:c0 + 512],
                                                   SBF0[:, 0:512],
                                                   SBF0[:, 512:1024], OP.mult)
                        inc(v.memset(stq[:, 7:8], 0.0), "sV")
                    qmarks.append(cnt["sV"])
            return cnt["sV"], qmarks

        xn1_ready, xn1_q = emit_ln(X, A, [[("dS", dS_x[i])] for i in range(8)])
        ln1_done_T = cnt["sT"]
        ck(1)
        tap("t_xn1", [a[:, :] for a in A], [("sV", xn1_ready)])

        @blk.gpsimd
        def _(g):
            for j in range(8):
                inc(g.dma_start(out=C[j][:, 0:1024],
                                in_=Wqkv[j * 128:(j + 1) * 128, 2048:3072]),
                    "dG", 16)

        @blk.gpsimd
        def _(g):
            wt(g, "sT", ln1_done_T)
            inc(g.dma_start(out=bvrow[0:1, 1024:2048],
                            in_=bqkv[2 * 1024:3 * 1024][None, :]), "dG", 16)
        dg_v = cnt["dG"]

        vt_done = []
        qk_cons = []
        stab_ready = []
        uc = [0]

        def emit_vtile(tt_):
            a = uc[0] % 2
            uc[0] += 1
            @blk.tensor
            def _(t, tt_=tt_, a=a):
                if tt_ == 0:
                    wt(t, "sV", ones_ready)
                    wt(t, "dG", dG_prolog)
                    wt(t, "dG", dg_v)
                wt(t, "sV", xn1_q[tt_ // 4])
                ps_acq(t, a)
                for j in range(8):
                    for pc in range(2):
                        ins_ = t.matmul(PS[a][:, pc * 512:(pc + 1) * 512],
                                        A[j][:, tt_ * 128:(tt_ + 1) * 128],
                                        C[j][:, pc * 512:(pc + 1) * 512],
                                        start=(j == 0), stop=False)
                for pc in range(2):
                    ins_ = t.matmul(PS[a][:, pc * 512:(pc + 1) * 512],
                                    onesr_bf[0:1, 0:128],
                                    bvrow[0:1, 1024 + pc * 512:1024 + (pc + 1) * 512],
                                    start=False, stop=True)
                inc(ins_, "sT")
            vt_done.append(cnt["sT"])

            @blk.scalar
            def _(s, tt_=tt_, a=a, m=cnt["sT"]):
                wt(s, "sT", m)
                pv = PS[a][:, 0:1024].rearrange("p (h e) -> p h e", e=64)
                dvw = Dv[tt_][:, :].rearrange("p (h f) -> p h f", f=65)[:, :, 0:64]
                ins_ = s.activation(dvw, pv, AF.Identity)
                ps_rel(ins_, a)

        def emit_qk(n):
            which, hp = divmod(n, 8)
            a = uc[0] % 2
            uc[0] += 1
            @blk.gpsimd
            def _(g, n=n, hp=hp, which=which):
                if n >= 2:
                    wt(g, "sWc", qk_cons[n - 2])
                col0 = hp * 128 if which == 0 else 1024 + hp * 128
                inc(g.dma_start(
                    out=PT[n % 2][:, :].rearrange("p (c m) -> p c m", m=128),
                    in_=Wqkv[:, col0:col0 + 128].rearrange("(c p) m -> p c m", p=128)),
                    "dG", 16)
            dg_n = cnt["dG"]

            @blk.tensor
            def _(t, n=n, a=a, dg_n=dg_n):
                wt(t, "dG", dg_n)
                ps_acq(t, a)
                for pc in range(4):
                    if n < 2:
                        wt(t, "sV", xn1_q[pc])
                    for j in range(8):
                        ins_ = t.matmul(PS[a][:, pc * 512:(pc + 1) * 512],
                                        PT[n % 2][:, j * 128:(j + 1) * 128],
                                        A[j][:, pc * 512:(pc + 1) * 512],
                                        start=(j == 0), stop=(j == 7))
                inc(ins_, "sWc")
            qk_cons.append(cnt["sWc"])
            mm_done = cnt["sWc"]

            @blk.scalar
            def _(s, n=n, hp=hp, which=which, a=a, mm_done=mm_done):
                if n == 0:
                    wt(s, "dS", dS_prolog)
                wt(s, "sWc", mm_done)
                dst = B[hp] if which == 0 else C[hp]
                bias_col = hp if which == 0 else 8 + hp
                ins_ = s.activation(dst[:, :], PS[a][:, :], AF.Identity,
                                    bias=colC[:, bias_col:bias_col + 1])
                ps_rel(ins_, a)
                inc(s.activation(stq[:, 7:8], stq[:, 6:7], AF.Identity), "sC")

        for tt_ in range(16):
            emit_vtile(tt_)
            if tt_ < 8:
                emit_qk(tt_)
        v_done_T = cnt["sT"]

        for hp in range(8):
            emit_qk(8 + hp)
            qk_evac_done = cnt["sC"]

            @blk.vector
            def _(v, hp=hp, qk_evac_done=qk_evac_done):
                wt(v, "sC", qk_evac_done)
                inc(v.tensor_tensor(SBF0[:, :], B[hp][:, :], B[hp][:, :], OP.mult),
                    "sV")
            sqq = cnt["sV"]

            @blk.tensor
            def _(t, sqq=sqq):
                wt(t, "sV", sqq)
                ps_acq(t, 0)
                for pc in range(4):
                    ins_ = t.matmul(PS[0][:, pc * 512:(pc + 1) * 512], ones_bf[:, :],
                                    SBF0[:, pc * 512:(pc + 1) * 512],
                                    start=True, stop=True)
                inc(ins_, "sT")
            mmq = cnt["sT"]

            @blk.vector
            def _(v, hp=hp, mmq=mmq):
                wt(v, "sT", mmq)
                inc(v.tensor_tensor(SBF0[:, :], C[hp][:, :], C[hp][:, :], OP.mult),
                    "sV")
            sqk = cnt["sV"]

            @blk.tensor
            def _(t, sqk=sqk):
                wt(t, "sV", sqk)
                ps_acq(t, 1)
                for pc in range(4):
                    ins_ = t.matmul(PS[1][:, pc * 512:(pc + 1) * 512], ones_bf[:, :],
                                    SBF0[:, pc * 512:(pc + 1) * 512],
                                    start=True, stop=True)
                inc(ins_, "sT")
            mmk = cnt["sT"]

            @blk.vector
            def _(v, hp=hp, mmq=mmq, mmk=mmk):
                wt(v, "sT", mmq)
                ins_ = v.reduce_max(stq[:, 0:1], PS[0][:, :], axis=AX.X)
                ps_rel(ins_, 0)
                wt(v, "sT", mmk)
                ins_ = v.reduce_max(stq[:, 1:2], PS[1][:, :], axis=AX.X)
                ps_rel(ins_, 1)
                inc(v.memset(stq[:, 6:7], 0.0), "sV")
            mul_done = cnt["sV"]

            @blk.scalar
            def _(s, hp=hp, mul_done=mul_done):
                wt(s, "sV", mul_done)
                ins_ = s.activation(stq[:, 4:5], stq[:, 0:1], AF.Ln)
                ins_ = s.activation(stq[:, 5:6], stq[:, 1:2], AF.Ln)
                ins_ = s.activation(stq[:, 2:3], stq[:, 4:5], AF.Identity,
                                    bias=stq[:, 5:6])
                ins_ = s.activation(stq[:, 3:4], stq[:, 2:3], AF.Exp, scale=0.5)
                inc(s.activation(stq[:, 7:8], stq[:, 6:7], AF.Identity), "sC")
            sqrt_done = cnt["sC"]

            @blk.vector
            def _(v, hp=hp, sqrt_done=sqrt_done):
                wt(v, "sC", sqrt_done)
                inc(v.tensor_scalar_mul(negc[:, hp:hp + 1], stq[:, 3:4], -0.125),
                    "sStab")
            stab_ready.append(cnt["sStab"])

        wc_after_qk = cnt["sWc"]
        tap("t_q", [b_[:, :] for b_ in B], [("sStab", stab_ready[7])])
        tap("t_k", [c_[:, :] for c_ in C], [("sStab", stab_ready[7])])
        tap("t_v", [d_[:, :] for d_ in Dv], [("sStab", stab_ready[7]),
                                             ("sPS0", cnt["sPS0"]),
                                             ("sPS1", cnt["sPS1"])])
        tap("t_negc", [negc[:, :]], [("sStab", stab_ready[7])])

        ck(2)
        sS0 = cnt["sS"]; sPV0 = cnt["sPV"]; sE0 = cnt["sE"]
        sRaw0 = cnt["sRaw"]; sDR0 = cnt["sDR"]; sNm0 = cnt["sNm"]

        @blk.tensor
        def _(t):
            ps_acq(t, 0)
            ps_acq(t, 1)

            def emit_pv(m):
                gp, kk = divmod(m, 16)
                hpp = gp % 8
                pp = gp % 2
                wt(t, "sE", sE0 + m + 1)
                if kk == 0:
                    wt(t, "sNm", sNm0 + gp - 1)
                ins_ = t.matmul(ps1[0:65, pp * 1024:pp * 1024 + 512],
                                Dv[kk][:, 130 * hpp:130 * hpp + 65],
                                PT[kk % 2][:, 0:512],
                                start=(kk == 0), stop=(kk == 15))
                ins_ = t.matmul(ps1[0:65, pp * 1024 + 512:pp * 1024 + 1024],
                                Dv[kk][:, 130 * hpp + 65:130 * hpp + 130],
                                PT[kk % 2][:, 512:1024],
                                start=(kk == 0), stop=(kk == 15))
                inc(ins_, "sPV")

            def emit_denrep(gp):
                pp = gp % 2
                wt(t, "sRaw", sRaw0 + gp + 1)
                ins_ = t.matmul(ps1[0:64, pp * 1024:pp * 1024 + 512],
                                ones_bf[64:65, 0:64],
                                SBF0[64:65, pp * 1024:pp * 1024 + 512],
                                start=True, stop=True)
                ins_ = t.matmul(ps1[0:64, pp * 1024 + 512:pp * 1024 + 1024],
                                ones_bf[64:65, 0:64],
                                SBF0[64:65, pp * 1024 + 512:pp * 1024 + 1024],
                                start=True, stop=True)
                inc(ins_, "sDR")

            for n in range(512):
                g, kt = divmod(n, 16)
                st_, hp = g // 8, g % 8
                wt(t, "sE", sE0 + n - 1)
                slot = kt % 2
                ins_ = t.matmul(ps0[:, slot * 1024:slot * 1024 + 512],
                                C[hp][0:64, kt * 128:(kt + 1) * 128],
                                B[hp][0:64, st_ * 512:(st_ + 1) * 512],
                                start=True, stop=True, tile_position=(0, 0))
                ins_ = t.matmul(ps0[:, slot * 1024 + 512:slot * 1024 + 1024],
                                C[hp][64:128, kt * 128:(kt + 1) * 128],
                                B[hp][64:128, st_ * 512:(st_ + 1) * 512],
                                start=True, stop=True, tile_position=(64, 0))
                inc(ins_, "sS")
                if kt == 13 and g >= 1:
                    emit_denrep(g - 1)
                if n >= 2:
                    emit_pv(n - 2)
            emit_pv(510)
            emit_pv(511)
            emit_denrep(31)

        @blk.scalar
        def _(s):
            wt(s, "sWc", wc_after_qk)
            for g in range(32):
                st_, hp = g // 8, g % 8
                for kt in range(16):
                    wt(s, "sS", sS0 + 16 * g + kt + 1)
                    wt(s, "sPV", sPV0 + 16 * g + kt - 1)
                    if g < 8 and kt == 0:
                        wt(s, "sStab", stab_ready[hp])
                    slot = kt % 2
                    ins_ = s.activation(PT[slot][:, :],
                                        ps0[:, slot * 1024:slot * 1024 + 1024],
                                        AF.Exp, bias=negc[:, hp:hp + 1], scale=0.125)
                    inc(ins_, "sE")

        @blk.vector
        def _(v):
            for g in range(32):
                p2 = g % 2
                if g >= 1:
                    gp = g - 1
                    stp, hpp = gp // 8, gp % 8
                    pp = gp % 2
                    wt(v, "sDR", sDR0 + gp + 1)
                    ins_ = v.tensor_tensor(A[hpp][0:64, stp * 512:(stp + 1) * 512],
                                           SBF0[0:64, pp * 1024:pp * 1024 + 512],
                                           ps1[0:64, pp * 1024:pp * 1024 + 512],
                                           OP.mult)
                    ins_ = v.tensor_tensor(A[hpp][64:128, stp * 512:(stp + 1) * 512],
                                           SBF0[0:64, pp * 1024 + 512:pp * 1024 + 1024],
                                           ps1[0:64, pp * 1024 + 512:pp * 1024 + 1024],
                                           OP.mult)
                    inc(ins_, "sNm")
                wt(v, "sPV", sPV0 + 16 * g + 16)
                ins_ = v.tensor_copy(SBF0[0:64, p2 * 1024:p2 * 1024 + 512],
                                     ps1[0:64, p2 * 1024:p2 * 1024 + 512])
                ins_ = v.tensor_copy(SBF0[0:64, p2 * 1024 + 512:p2 * 1024 + 1024],
                                     ps1[0:64, p2 * 1024 + 512:p2 * 1024 + 1024])
                with nc.allow_low_precision(reason="softmax denominator recip "
                                            "stored bf16 as matmul operand"):
                    ins_ = v.reciprocal(SBF0[64:65, p2 * 1024:p2 * 1024 + 512],
                                        ps1[64:65, p2 * 1024:p2 * 1024 + 512])
                    ins_ = v.reciprocal(SBF0[64:65, p2 * 1024 + 512:p2 * 1024 + 1024],
                                        ps1[64:65, p2 * 1024 + 512:p2 * 1024 + 1024])
                inc(v.memset(stq[:, 6:7], 0.0), "sRaw")
            gp = 31
            stp, hpp = gp // 8, gp % 8
            pp = gp % 2
            wt(v, "sDR", sDR0 + 32)
            ins_ = v.tensor_tensor(A[hpp][0:64, stp * 512:(stp + 1) * 512],
                                   SBF0[0:64, pp * 1024:pp * 1024 + 512],
                                   ps1[0:64, pp * 1024:pp * 1024 + 512], OP.mult)
            ins_ = v.tensor_tensor(A[hpp][64:128, stp * 512:(stp + 1) * 512],
                                   SBF0[0:64, pp * 1024 + 512:pp * 1024 + 1024],
                                   ps1[0:64, pp * 1024 + 512:pp * 1024 + 1024],
                                   OP.mult)
            inc(ins_, "sNm")
            ps_rel(v.memset(stq[:, 2:3], 0.0), 0)
            ps_rel(v.memset(stq[:, 2:3], 0.0), 1)
        ck(3)
        attn_done = cnt["sNm"]
        tap("t_attn", [a[:, :] for a in A], [("sNm", attn_done)])

        def emit_proj_residual(lhs_fn, rhs, bias_base, w_stream, first_waits,
                               rhs_qmarks=None):
            done = []
            wc_marks = []
            for i in range(8):
                if w_stream is not None:
                    @blk.gpsimd
                    def _(g, i=i):
                        if i == 0:
                            for (nm, val) in first_waits:
                                wt(g, nm, val)
                        if i >= 2:
                            wt(g, "sWc", wc_marks[i - 2])
                        inc(g.dma_start(
                            out=PT[i % 2][:, :].rearrange("p (c m) -> p c m", m=128),
                            in_=w_stream(i).rearrange("(c p) m -> p c m", p=128)),
                            "dG", 16)
                    dg_i = cnt["dG"]
                else:
                    dg_i = 0

                @blk.tensor
                def _(t, i=i, dg_i=dg_i):
                    if i == 0:
                        for (nm, val) in first_waits:
                            wt(t, nm, val)
                    if w_stream is not None:
                        wt(t, "dG", dg_i)
                    a = i % 2
                    ps_acq(t, a)
                    for pc in range(4):
                        if rhs_qmarks is not None and i < 2:
                            wt(t, "sV", rhs_qmarks[pc])
                        for j in range(8):
                            ins_ = t.matmul(PS[a][:, pc * 512:(pc + 1) * 512],
                                            lhs_fn(i, j),
                                            rhs[j][:, pc * 512:(pc + 1) * 512],
                                            start=(j == 0), stop=(j == 7))
                    inc(ins_, "sWc")
                wc_marks.append(cnt["sWc"])
                mm_i = cnt["sWc"]

                @blk.vector
                def _(v, i=i, mm_i=mm_i):
                    wt(v, "sWc", mm_i)
                    a = i % 2
                    if bias_base is not None:
                        ins_ = v.scalar_tensor_tensor(
                            X[i][:, :], PS[a][:, :],
                            colC[:, bias_base + i:bias_base + i + 1],
                            X[i][:, :], OP.add, OP.add)
                        ins_ = v.memset(stq[:, 6:7], 0.0)
                    else:
                        ins_ = v.tensor_tensor(X[i][:, :], PS[a][:, :],
                                               X[i][:, :], OP.add)
                    ps_rel(ins_, a)
                done.append(("sPS%d" % (i % 2), cnt["sPS%d" % (i % 2)]))
            return done, ("sWc", wc_marks[-1])

        xdone, pj_mark = emit_proj_residual(
            lhs_fn=lambda i, j: PT[i % 2][:, j * 128:(j + 1) * 128],
            rhs=A, bias_base=16,
            w_stream=lambda i: Wout[:, i * 128:(i + 1) * 128],
            first_waits=[("sNm", attn_done), ("sPV", sPV0 + 512)])

        ck(4)
        tap("t_x4", [x[:, :] for x in X], xdone)
        @blk.sync
        def _(sy):
            wt(sy, "sNm", attn_done)
            inc(sy.dma_start(out=B[0][0:32, 0:512], in_=RfT_h), "dS", 16)
            inc(sy.dma_start(out=B[0][0:16, 512:1024], in_=RqfT_h), "dS", 16)
            inc(sy.dma_start(out=B[2][0:32, 0:512], in_=RpT_h), "dS", 16)
            inc(sy.dma_start(out=B[2][0:16, 896:1024], in_=RqpT_h), "dS", 16)
        dS_Rm = cnt["dS"]

        @blk.gpsimd
        def _(g):
            wt(g, "sNm", attn_done)
            for rr, G1h, G2h in ((0, G1R, G2R), (1, G1I, G2I)):
                for c in range(4):
                    k = 4 * rr + c
                    inc(g.dma_start(out=B[2][0:32, 512 + 32 * k:544 + 32 * k],
                                    in_=G1h[0, :, c, :]), "dG", 16)
                    inc(g.dma_start(out=B[2][0:16, 768 + 16 * k:784 + 16 * k],
                                    in_=G2h[c, :, 0, :]), "dG", 16)
        dg_g = cnt["dG"]

        def t1ap(c):
            return (B[1] if c < 2 else B[3])[0:32, (c % 2) * 512:(c % 2) * 512 + 512]

        def t1qap(c):
            return (B[5] if c < 2 else B[7])[0:16, (c % 2) * 512:(c % 2) * 512 + 512]

        for rr in range(2):
            for c in range(4):
                @blk.tensor
                def _(t, rr=rr, c=c):
                    if rr == 0 and c == 0:
                        wt(t, "dS", dS_Rm)
                        wt(t, "dG", dg_g)
                    k = 4 * rr + c
                    ps_acq(t, 0)
                    ins_ = t.matmul(ps0[0:32, 0:512],
                                    B[2][0:32, 512 + 32 * k:544 + 32 * k],
                                    B[0][0:32, 0:512], start=True, stop=True)
                    ins_ = t.matmul(ps0[0:16, 512:1024],
                                    B[2][0:16, 768 + 16 * k:784 + 16 * k],
                                    B[0][0:16, 512:1024], start=True, stop=True)
                    inc(ins_, "sWc")
                t1_done = cnt["sWc"]

                @blk.vector
                def _(v, c=c, t1_done=t1_done):
                    wt(v, "sWc", t1_done)
                    ins_ = v.tensor_copy(t1ap(c), ps0[0:32, 0:512])
                    ins_ = v.tensor_copy(t1qap(c), ps0[0:16, 512:1024])
                    ps_rel(ins_, 0)
            t1_all = cnt["sPS0"]

            for ic in range(4):
                for c in range(4):
                    a2 = (4 * ic + c) % 2
                    @blk.tensor
                    def _(t, ic=ic, c=c, a2=a2, t1_all=t1_all):
                        if c == 0:
                            wt(t, "sPS0", t1_all)
                        ps_acq(t, a2)
                        ins_ = t.matmul(PS[a2][:, 0:512],
                                        B[2][0:32, ic * 128:(ic + 1) * 128],
                                        t1ap(c), start=True, stop=True)
                        ins_ = t.matmul(PS[a2][:, 512:1024], B[2][0:16, 896:1024],
                                        t1qap(c), start=True, stop=True)
                        inc(ins_, "sT")
                    ab_done = cnt["sT"]

                    @blk.vector
                    def _(v, rr=rr, ic=ic, c=c, a2=a2, ab_done=ab_done):
                        wt(v, "sT", ab_done)
                        ins_ = v.tensor_copy(B[4][:, 0:512], PS[a2][:, 512:1024])
                        if c == 0:
                            ins_ = v.tensor_tensor(B[6][:, 0:512], PS[a2][:, 0:512],
                                                   B[4][:, 0:512], OP.mult)
                        else:
                            ins_ = v.tensor_tensor(B[6][:, 512:1024],
                                                   PS[a2][:, 0:512],
                                                   B[4][:, 0:512], OP.mult)
                            dst = (Dv[4 * rr + ic][:, 0:512] if c == 3 else
                                   B[6][:, 0:512])
                            ins_ = v.tensor_tensor(dst, B[6][:, 0:512],
                                                   B[6][:, 512:1024], OP.add)
                        ps_rel(ins_, a2)
                        if c == 3:
                            inc(v.memset(stq[:, 7:8], 0.0), "sV")
                acc_done = cnt["sV"]

                @blk.scalar
                def _(s, rr=rr, ic=ic, acc_done=acc_done):
                    wt(s, "sV", acc_done)
                    inc(s.activation(stq[:, 7:8], stq[:, 6:7], AF.Identity), "sC")
        wt_cast = cnt["sC"]

        @blk.vector
        def _(v, wt_cast=wt_cast):
            wt(v, "sC", wt_cast)
            for ic in range(4):
                ins_ = v.tensor_scalar_mul(Dv[8 + ic][:, 0:512],
                                           Dv[4 + ic][:, 0:512], -1.0)
            inc(ins_, "sV")
        ffn1_wbuilt = cnt["sV"]
        ck(41)

        def ffn_lhs(i, j):
            if j < 4:
                if i < 4:
                    return Dv[j][:, i * 128:(i + 1) * 128]
                return Dv[4 + j][:, (i - 4) * 128:(i - 3) * 128]
            if i < 4:
                return Dv[8 + (j - 4)][:, i * 128:(i + 1) * 128]
            return Dv[j - 4][:, (i - 4) * 128:(i - 3) * 128]

        tap("t_ffnw", [Dv[i][:, 0:512] for i in range(12)],
            [("sV", ffn1_wbuilt)])

        @blk.sync
        def _(sy):
            wt(sy, "sNm", attn_done)
            for j in range(8):
                inc(sy.dma_start(out=C[j][:, 0:1026],
                                 in_=CS_h[j * 128:(j + 1) * 128, :]), "dS", 16)
        dS_cs = cnt["dS"]

        @blk.gpsimd
        def _(g):
            wt(g, "sNm", attn_done)
            for j in range(8):
                inc(g.dma_start(out=B[j][:, 1024:2048],
                                in_=WkT[j * 128:(j + 1) * 128, :]), "dG", 16)
        dg_kstage = cnt["dG"]

        xn2_ready, xn2_q = emit_ln(X, A, [[xdone[i]] for i in range(8)], dst_free=pj_mark,
                            db=pj_mark)

        ck(5)
        fold_marks = {}
        ev_marks = []
        for i in range(8):
            @blk.tensor
            def _(t, i=i):
                if i == 0:
                    wt(t, "dG", dg_kstage)
                    wt(t, "dS", dS_cs)
                a = i % 2
                ps_acq(t, a)
                for j in range(8):
                    for pc in range(2):
                        ins_ = t.matmul(PS[a][:, pc * 512:(pc + 1) * 512],
                                        B[j][:, 1024 + i * 128:1024 + (i + 1) * 128],
                                        C[j][:, pc * 512:(pc + 1) * 512],
                                        start=(j == 0), stop=(j == 7))
                    ins_ = t.matmul(PS[a][:, 1024:1026],
                                    B[j][:, 1024 + i * 128:1024 + (i + 1) * 128],
                                    C[j][:, 1024:1026],
                                    start=(j == 0), stop=(j == 7))
                inc(ins_, "sWc")
            mmf = cnt["sWc"]
            if i == 7:
                fold_marks["k"] = mmf

            @blk.scalar
            def _(s, i=i, mmf=mmf):
                wt(s, "sWc", mmf)
                a = i % 2
                ins_ = s.activation(B[i][:, 0:1024], PS[a][:, 0:1024], AF.Identity)
                ins_ = s.activation(WNY[:, i:i + 1], PS[a][:, 1024:1025],
                                    AF.Identity)
                ps_rel(ins_, a)
            ev_marks.append(cnt["sPS%d" % (i % 2)])

        xdone, ffn1_mark = emit_proj_residual(
            lhs_fn=ffn_lhs, rhs=A, bias_base=24, w_stream=None,
            first_waits=[("sV", ffn1_wbuilt)], rhs_qmarks=xn2_q)

        ck(6)
        tap("t_x6", [x[:, :] for x in X], xdone)
        xn3_ready, xn3_q = emit_ln(X, A, [[xdone[i]] for i in range(8)], dst_free=ffn1_mark,
                            db=pj_mark)

        ck(7)
        @blk.gpsimd
        def _(g):
            wt(g, "sV", ffn1_wbuilt)
            for i in range(12):
                inc(g.dma_start(out=wsth[i * 128:(i + 1) * 128, :],
                                in_=Dv[i][:, 0:512]), "dG", 16)
        w_stash = cnt["dG"]

        @blk.gpsimd
        def _(g):
            wt(g, ffn1_mark[0], ffn1_mark[1])
            for j in range(8):
                inc(g.dma_start(out=Dv[8 + j][:, 0:1024],
                                in_=WvT[j * 128:(j + 1) * 128, :]), "dG", 16)
        dg_vstage = cnt["dG"]

        for i in range(8):
            @blk.tensor
            def _(t, i=i):
                if i == 0:
                    wt(t, "dG", dg_vstage)
                a = i % 2
                ps_acq(t, a)
                for j in range(8):
                    for pc in range(2):
                        ins_ = t.matmul(PS[a][:, pc * 512:(pc + 1) * 512],
                                        Dv[8 + j][:, i * 128:(i + 1) * 128],
                                        C[j][:, pc * 512:(pc + 1) * 512],
                                        start=(j == 0), stop=(j == 7))
                    ins_ = t.matmul(PS[a][:, 1024:1026],
                                    Dv[8 + j][:, i * 128:(i + 1) * 128],
                                    C[j][:, 1024:1026],
                                    start=(j == 0), stop=(j == 7))
                inc(ins_, "sWc")
            mmf = cnt["sWc"]
            if i == 7:
                fold_marks["v"] = mmf

            @blk.scalar
            def _(s, i=i, mmf=mmf):
                wt(s, "sWc", mmf)
                if i == 0:
                    wt(s, "dG", w_stash)
                    wt(s, ffn1_mark[0], ffn1_mark[1])
                a = i % 2
                ins_ = s.activation(Dv[i][:, 0:1024], PS[a][:, 0:1024], AF.Identity)
                ins_ = s.activation(WNY[:, 8 + i:9 + i], PS[a][:, 1024:1025],
                                    AF.Identity)
                ps_rel(ins_, a)
            ev_marks.append(cnt["sPS%d" % (i % 2)])

        @blk.tensor
        def _(t):
            wt(t, "dG", dG_prolog)
            for si, (cb0, sl0) in enumerate(((0, 40), (8, 49))):
                a = si % 2
                ps_acq(t, a)
                for cchunk in range(8):
                    for j in range(8):
                        ins_ = t.matmul(PS[a][:, cchunk:cchunk + 1],
                                        C[j][:, cchunk * 128:(cchunk + 1) * 128],
                                        colB[:, cb0 + j:cb0 + j + 1],
                                        start=(j == 0), stop=(j == 7))
                for j in range(8):
                    ins_ = t.matmul(PS[a][0:1, 8:9],
                                    C[j][:, 1024:1025],
                                    colB[:, cb0 + j:cb0 + j + 1],
                                    start=(j == 0), stop=(j == 7))
                inc(ins_, "sT")
        bias_mm = cnt["sT"]

        @blk.vector
        def _(v):
            wt(v, "sT", bias_mm)
            for si, sl0 in ((0, 40), (1, 49)):
                a = si % 2
                for cchunk in range(8):
                    ins_ = v.tensor_copy(colC[:, sl0 + cchunk:sl0 + cchunk + 1],
                                         PS[a][:, cchunk:cchunk + 1])
                ins_ = v.tensor_copy(colC[0:1, sl0 + 8:sl0 + 9], PS[a][0:1, 8:9])
                ps_rel(ins_, a)
        bias_cols_done = cnt["sPS1"]
        bias_cols_done0 = cnt["sPS0"]

        tap("t_wkcs", [b_[:, 0:1024] for b_ in B],
            [("sPS0", cnt["sPS0"]), ("sPS1", cnt["sPS1"])])
        tap("t_wvcs", [d_[:, 0:1024] for d_ in Dv[0:8]],
            [("sPS0", cnt["sPS0"]), ("sPS1", cnt["sPS1"])])
        tap("t_wny", [WNY[:, :]], [("sPS0", cnt["sPS0"]), ("sPS1", cnt["sPS1"])])
        tap("t_bcol", [colC[:, :]], [("sPS0", cnt["sPS0"]), ("sPS1", cnt["sPS1"])])
        q_marks = []
        wcq = []
        for i in range(8):
            @blk.gpsimd
            def _(g, i=i):
                if i == 0:
                    wt(g, "sPV", sPV0 + 512)
                    wt(g, "sPS0", bias_cols_done0)
                if i >= 2:
                    wt(g, "sWc", wcq[i - 2])
                inc(g.dma_start(
                    out=PT[i % 2][:, :].rearrange("p (c m) -> p c m", m=128),
                    in_=Wq_b[:, i * 128:(i + 1) * 128].rearrange("(c p) m -> p c m", p=128)),
                    "dG", 16)
            dg_q = cnt["dG"]

            @blk.tensor
            def _(t, i=i, dg_q=dg_q):
                wt(t, "dG", dg_q)
                a = i % 2
                ps_acq(t, a)
                for pc in range(4):
                    if i < 2:
                        wt(t, "sV", xn3_q[pc])
                    for j in range(8):
                        ins_ = t.matmul(PS[a][:, pc * 512:(pc + 1) * 512],
                                        PT[i % 2][:, j * 128:(j + 1) * 128],
                                        A[j][:, pc * 512:(pc + 1) * 512],
                                        start=(j == 0), stop=(j == 7))
                inc(ins_, "sWc")
            wcq.append(cnt["sWc"])
            mm_q = cnt["sWc"]

            @blk.scalar
            def _(s, i=i, mm_q=mm_q):
                wt(s, "sWc", mm_q)
                a = i % 2
                ins_ = s.activation(C[i][:, :], PS[a][:, :], AF.Identity,
                                    bias=colC[:, 32 + i:33 + i])
                ps_rel(ins_, a)
            q_marks.append(cnt["sPS%d" % (i % 2)])
        q_done_marks = list(q_marks)

        @blk.gpsimd
        def _(g):
            wt(g, "sWc", fold_marks["k"])
            for r in range(8):
                for pc in range(2):
                    inc(g.dma_start(
                        out=B[r][:, 1024 + pc * 512:1536 + pc * 512],
                        in_=IDFT_h[r * 128:(r + 1) * 128, pc * 512:(pc + 1) * 512]),
                        "dG", 16)

        tap("t_qb", [c_[:, :] for c_ in C],
            [("sPS0", cnt["sPS0"]), ("sPS1", cnt["sPS1"])])
        prod_marks = []
        sf0_free_marks = []
        _dbg2 = [0]
        for fc in range(4):
            for tq in range(4):
                it = 4 * fc + tq
                @blk.tensor
                def _(t, fc=fc, tq=tq, it=it):
                    if it < 4:
                        wt(t, "sV", xn3_q[tq])
                    ps_acq(t, 1)
                    for half, f0 in ((0, fc), (1, 4 + fc)):
                        for j in range(8):
                            ins_ = t.matmul(
                                ps1[:, half * 512:(half + 1) * 512],
                                Dv[j][:, f0 * 128:(f0 + 1) * 128],
                                A[j][:, tq * 512:(tq + 1) * 512],
                                start=(j == 0), stop=(j == 7))
                    inc(ins_, "sT")
                    ps_acq(t, 0)
                    for half, f0 in ((0, fc), (1, 4 + fc)):
                        for j in range(8):
                            ins_ = t.matmul(
                                ps0[:, half * 512:(half + 1) * 512],
                                B[j][:, f0 * 128:(f0 + 1) * 128],
                                A[j][:, tq * 512:(tq + 1) * 512],
                                start=(j == 0), stop=(j == 7))
                    inc(ins_, "sT")
                vmm = cnt["sT"] - 1
                kmm = cnt["sT"]

                @blk.scalar
                def _(s, fc=fc, vmm=vmm, it=it):
                    wt(s, "sT", vmm)
                    if it >= 1:
                        wt(s, "sV", sf0_free_marks[it - 1])
                    ins_ = s.activation(SF0[:, 0:512], ps1[:, 0:512], AF.Identity,
                                        bias=colC[:, 49 + fc:50 + fc])
                    ins_ = s.activation(SF0[:, 512:1024], ps1[:, 512:1024],
                                        AF.Identity, bias=colC[:, 53 + fc:54 + fc])
                    ps_rel(ins_, 1)
                vev = cnt["sPS1"]

                if DEBUG_TAPS and fc == 0 and tq == 0:
                    @blk.vector
                    def _(v, vev=vev, kmm=kmm):
                        wt(v, "sPS1", vev)
                        wt(v, "sT", kmm)
                        ins_ = v.tensor_copy(PT[0][:, :], ps0[:, 0:1024])
                        ins_ = v.tensor_copy(PT[1][:, :], SF0[:, 0:1024])
                        inc(ins_, "sV")
                    dbg_mark = cnt["sV"]
                    tap("t_k0", [PT[0][:, :]], [("sV", dbg_mark)])
                    tap("t_v0", [PT[1][:, :]], [("sV", dbg_mark)])
                    dbg_ds = cnt["dS"]

                @blk.vector
                def _(v, fc=fc, tq=tq, vev=vev, kmm=kmm):
                    if DEBUG_TAPS and fc == 0 and tq == 0:
                        wt(v, "dS", dbg_ds)
                    if DEBUG_TAPS and it >= 1:
                        wt(v, "dS", _dbg2[0])
                    wt(v, "sPS1", vev)
                    wt(v, "sT", kmm)
                    combos = [(0, 0, 0), (1, 1, 1), (0, 1, 2), (1, 0, 3)]
                    tmps = [ps0[:, 1024:1536], ps0[:, 1536:2048]]
                    red_q = []
                    for idx, (kh, vh, k) in enumerate(combos):
                        bcol = 40 + fc if kh == 0 else 44 + fc
                        if idx >= 2:
                            pk, ptmp = red_q.pop(0)
                            v.reduce_sum(RED[:, it * 4 + pk:it * 4 + pk + 1],
                                         ptmp, axis=AX.X)
                        v.scalar_tensor_tensor(
                            tmps[idx % 2], ps0[:, kh * 512:(kh + 1) * 512],
                            colC[:, bcol:bcol + 1],
                            SF0[:, vh * 512:(vh + 1) * 512],
                            OP.add, OP.mult)
                        red_q.append((k, tmps[idx % 2]))
                    inc(v.memset(stq[:, 6:7], 0.0), "sV")
                    for pk, ptmp in red_q:
                        ins_ = v.reduce_sum(RED[:, it * 4 + pk:it * 4 + pk + 1],
                                            ptmp, axis=AX.X)
                    ps_rel(ins_, 0)
                    inc(v.memset(stq[:, 7:8], 0.0), "sV")
                sf0_free_marks.append(cnt["sV"] - 1)
                prod_marks.append(cnt["sV"])
                if DEBUG_TAPS:
                    tap("t_sh%d" % it, [Shat[:, 0:12]], [("sV", prod_marks[it])])
                    _dbg2[0] = cnt["dS"]
        shat_main = cnt["sV"]
        last_vmm = vmm

        @blk.tensor
        def _(t):
            ps_acq(t, 1)
            for j in range(8):
                for pc in range(4):
                    ins_ = t.matmul(ps1[0:1, pc * 512:(pc + 1) * 512],
                                    WNY[:, 8 + j:9 + j],
                                    A[j][:, pc * 512:(pc + 1) * 512],
                                    start=(j == 0), stop=(j == 7))
            inc(ins_, "sT")
            ps_acq(t, 0)
            for j in range(8):
                for pc in range(4):
                    ins_ = t.matmul(ps0[0:1, pc * 512:(pc + 1) * 512],
                                    WNY[:, j:j + 1],
                                    A[j][:, pc * 512:(pc + 1) * 512],
                                    start=(j == 0), stop=(j == 7))
            inc(ins_, "sT")
        ny_v = cnt["sT"] - 1
        ny_k = cnt["sT"]

        @blk.scalar
        def _(s):
            wt(s, "sT", ny_v)
            ins_ = s.activation(SBF0[0:1, 0:2048], ps1[0:1, 0:2048], AF.Identity,
                                bias=colC[0:1, 57:58])
            ps_rel(s.activation(stq[:, 7:8], stq[:, 6:7], AF.Identity), 1)
        ny_vev = cnt["sPS1"]

        @blk.vector
        def _(v):
            wt(v, "sPS1", ny_vev)
            wt(v, "sT", ny_k)
            ins_ = v.scalar_tensor_tensor(ps1[0:1, 1024:2048], ps0[0:1, 0:1024],
                                          colC[0:1, 48:49], SBF0[0:1, 0:1024],
                                          OP.add, OP.mult)
            ins_ = v.reduce_sum(RED[0:1, 64:65], ps1[0:1, 1024:2048], axis=AX.X)
            ins_ = v.scalar_tensor_tensor(ps1[0:1, 1024:2048], ps0[0:1, 1024:2048],
                                          colC[0:1, 48:49], SBF0[0:1, 1024:2048],
                                          OP.add, OP.mult)
            ins_ = v.reduce_sum(RED[0:1, 65:66], ps1[0:1, 1024:2048], axis=AX.X)
            ps_rel(ins_, 0)
            inc(v.memset(stq[:, 6:7], 0.0), "sV")
        ny_red = cnt["sV"]

        @blk.gpsimd
        def _(g):
            wt(g, "sV", ny_red)
            inc(g.dma_start(out=SBF0[0:1, 0:1024], in_=IDFT_h[1024:1025, :]),
                "dG", 16)
        idft_pieces_dg = cnt["dG"]

        @blk.scalar
        def _(s):
            wt(s, "sV", ny_red)
            inc(s.activation(stq[:, 7:8], stq[:, 6:7], AF.Identity), "sC")
        hop1 = cnt["sC"]

        @blk.vector
        def _(v):
            wt(v, "sC", hop1)
            rv = RED[:, 0:64].rearrange("p (i k) -> p i k", k=4)
            ins_ = v.tensor_tensor(Shat[:, 8:24], rv[:, :, 0:1], rv[:, :, 1:2],
                                   OP.subtract)
            ins_ = v.tensor_tensor(Shat[:, 24:40], rv[:, :, 2:3], rv[:, :, 3:4],
                                   OP.add)
            inc(v.memset(stq[:, 6:7], 0.0), "sV")
        bulk1 = cnt["sV"]

        @blk.scalar
        def _(s):
            wt(s, "sV", bulk1)
            inc(s.activation(stq[:, 7:8], stq[:, 6:7], AF.Identity), "sC")
        hop2 = cnt["sC"]

        @blk.vector
        def _(v):
            wt(v, "sC", hop2)
            ins_ = v.reduce_sum(Shat[:, 0:4],
                                Shat[:, 8:24].rearrange("p (f t) -> p f t", t=4),
                                axis=AX.X)
            ins_ = v.reduce_sum(Shat[:, 4:8],
                                Shat[:, 24:40].rearrange("p (f t) -> p f t", t=4),
                                axis=AX.X)
            inc(v.memset(stq[:, 6:7], 0.0), "sV")
        shat_done = cnt["sV"]

        @blk.scalar
        def _(s):
            wt(s, "sV", shat_done)
            ins_ = s.activation(Shat[0:1, 8:9], RED[0:1, 64:65], AF.Identity,
                                bias=RED[0:1, 65:66])
            inc(s.activation(stq[:, 7:8], stq[:, 6:7], AF.Identity), "sC")
        sny_done = cnt["sC"]

        tap("t_shat", [Shat[:, :]], [("sV", shat_done), ("sC", sny_done)])
        @blk.vector
        def _(v):
            wt(v, "sV", shat_done)
            wt(v, "sC", sny_done)
            inc(v.tensor_copy(colB[:, 0:9], Shat[:, 0:9]), "sV")
        shatbf_done = cnt["sV"]

        @blk.tensor
        def _(t):
            wt(t, "sV", shatbf_done)
            wt(t, "dG", idft_pieces_dg)
            ps_acq(t, 1)
            for pc in range(2):
                for r in range(9):
                    if r < 8:
                        lhs = colB[:, r:r + 1]
                        rhs = B[r][:, 1024 + pc * 512:1536 + pc * 512]
                    else:
                        lhs = colB[0:1, 8:9]
                        rhs = SBF0[0:1, pc * 512:(pc + 1) * 512]
                    ins_ = t.matmul(ps1[0:1, pc * 512:(pc + 1) * 512],
                                    lhs, rhs, start=(r == 0), stop=(r == 8))
            inc(ins_, "sT")
        s_time_done = cnt["sT"]

        @blk.scalar
        def _(s):
            wt(s, "sT", s_time_done)
            ins_ = s.activation(s_bf[0:1, 0:1024], ps1[0:1, 0:1024], AF.Identity)
            ins_ = s.activation(s_bf[0:1, 1024:2048], ps1[0:1, 0:1024], AF.Identity)
            inc(s.activation(stq[:, 7:8], stq[:, 6:7], AF.Identity), "sC")
            ps_rel(s.activation(stq[:, 7:8], stq[:, 6:7], AF.Identity), 1)
        sbf_done = cnt["sC"]

        @blk.sync
        def _(sy):
            wt(sy, "sC", sbf_done)
            wt(sy, "sWc", fold_marks["v"])
            inc(sy.dma_start(out=s2h, in_=s_bf[0:1, :]), "dS", 16)
        s2_w = cnt["dS"]

        @blk.sync
        def _(sy):
            wt(sy, "dS", s2_w)
            for j in range(8):
                win = bass.AP(s2h.tensor, j * 128, [[1, 128], [1, 1024]])
                inc(sy.dma_start(out=B[j][:, 1024:2048], in_=win), "dS", 16)
        circ_done = cnt["dS"]

        tap("t_sbf", [s_bf[0:1, :]], [("dS", circ_done)])
        tap("t_circ", [b_[:, 1024:2048] for b_ in B], [("dS", circ_done)])
        fin_marks = []
        for mc in range(8):
            @blk.tensor
            def _(t, mc=mc):
                if mc == 0:
                    wt(t, "dS", circ_done)
                a = mc % 2
                ps_acq(t, a)
                for j in range(8):
                    for pc in range(4):
                        ins_ = t.matmul(PS[a][:, pc * 512:(pc + 1) * 512],
                                        B[j][:, 1024 + mc * 128:1024 + (mc + 1) * 128],
                                        C[j][:, pc * 512:(pc + 1) * 512],
                                        start=(j == 0), stop=(j == 7))
                inc(ins_, "sT")
            mmf = cnt["sT"]

            @blk.vector
            def _(v, mc=mc, mmf=mmf):
                wt(v, "sT", mmf)
                a = mc % 2
                ins_ = v.tensor_tensor(X[mc][:, :], PS[a][:, :], X[mc][:, :], OP.add)
                ps_rel(ins_, a)
            fin_marks.append(("sPS%d" % (mc % 2), cnt["sPS%d" % (mc % 2)]))
        final_mm_done = cnt["sT"]

        @blk.gpsimd
        def _(g):
            wt(g, "dG", w_stash)
            wt(g, "sT", last_vmm)
            for i in range(8):
                inc(g.dma_start(out=Dv[i][:, 0:512],
                                in_=wsth[i * 128:(i + 1) * 128, :]), "dG", 16)
        w_reload = cnt["dG"]

        ck(9)
        xn4_ready, xn4_q = emit_ln(X, A, [[fin_marks[i]] for i in range(8)],
                            dst_free=("sT", ny_k), db=("sWc", wcq[-1]))

        ck(95)
        def ffn_lhs2(i, j):
            if j < 4:
                if i < 4:
                    return Dv[j][:, i * 128:(i + 1) * 128]
                return Dv[4 + j][:, (i - 4) * 128:(i - 3) * 128]
            if i < 4:
                return Dv[8 + (j - 4)][:, i * 128:(i + 1) * 128]
            return Dv[j - 4][:, (i - 4) * 128:(i - 3) * 128]

        xdone_f, _mk2 = emit_proj_residual(
            lhs_fn=ffn_lhs2, rhs=A, bias_base=24, w_stream=None,
            first_waits=[("dG", w_reload)], rhs_qmarks=xn4_q)

        @blk.sync
        def _(sy):
            for i in range(8):
                wt(sy, xdone_f[i][0], xdone_f[i][1])
                inc(sy.dma_start(out=outT[i * 128:(i + 1) * 128, :],
                                 in_=X[i][:, :]), "sST", 16)
            sy.wait_ge(sems["sST"], cnt["sST"])
            if DEBUG_TAPS:
                sy.wait_ge(sems["dS"], cnt["dS"])

    return nc


_NC_CACHE = []


def kernel(**inputs):
    consts = _host_consts()
    x = np.ascontiguousarray(inputs["L0_attn_a_data"], dtype=np.float32)
    f32 = lambda a: np.ascontiguousarray(a, dtype=np.float32)
    base = {
        "Wqkv": f32(inputs["L0_attn_a_W_qkv"]),
        "bqkv": f32(inputs["L0_attn_a_b_qkv"]),
        "Wout": f32(inputs["L0_attn_a_W_out"]),
        "bout": f32(inputs["L0_attn_a_b_out"]),
        "Wq_b": f32(inputs["L0_attn_b_Wq"]),
        "bq_b": f32(inputs["L0_attn_b_bq"]),
        "WkT": np.ascontiguousarray(np.asarray(inputs["L0_attn_b_Wk"], dtype=np.float32).T),
        "bk_b": f32(inputs["L0_attn_b_bk"]),
        "WvT": np.ascontiguousarray(np.asarray(inputs["L0_attn_b_Wv"], dtype=np.float32).T),
        "bv_b": f32(inputs["L0_attn_b_bv"]),
        "G1R": f32(inputs["L0_ffn_G1_R"]),
        "G2R": f32(inputs["L0_ffn_G2_R"]),
        "G1I": f32(inputs["L0_ffn_G1_I"]),
        "G2I": f32(inputs["L0_ffn_G2_I"]),
        "fbias": np.concatenate([f32(inputs["L0_ffn_bias_real"]),
                                 f32(inputs["L0_ffn_bias_imag"])]),
    }
    base.update(consts)
    nc = build_nc()
    in_maps = []
    for b in range(8):
        m = dict(base)
        m["xT"] = np.ascontiguousarray(x[b].T)
        in_maps.append(m)
    res = run_bass_kernel_spmd(nc, in_maps, core_ids=list(range(8)))
    out = np.stack([np.asarray(res.results[b]["outT"]).T for b in range(8)])
    return np.ascontiguousarray(out, dtype=np.float32)


# version 20
# speedup vs baseline: 1.0260x; 1.0260x over previous
import numpy as np
import ml_dtypes

import concourse.bass as bass
import concourse.mybir as mybir
from concourse.bass_utils import run_bass_kernel_spmd

F32 = mybir.dt.float32
BF16 = mybir.dt.bfloat16
BF = ml_dtypes.bfloat16
AF = mybir.ActivationFunctionType
OP = mybir.AluOpType
AX = mybir.AxisListType

D = 1024
T = 2048
DEBUG_TAPS = False
PHASE_LIM = 99
OLD_RECIPS = True


class _Stop(Exception):
    pass


def _host_consts():
    d = np.arange(D)
    f = np.arange(512)
    ang = 2 * np.pi * np.outer(d, f) / D
    CS = np.zeros((D, 1026), np.float32)
    CS[:, 0:512] = np.cos(ang)
    CS[:, 512:1024] = -np.sin(ang)
    CS[:, 1024] = np.cos(np.pi * d)
    IDFT = np.zeros((1026, D), np.float32)
    coef = np.where(f == 0, 1.0, 2.0) / D
    IDFT[0:512] = coef[:, None] * np.cos(ang.T)
    IDFT[512:1024] = (-2.0 / D) * np.sin(ang.T)
    IDFT[1024] = np.cos(np.pi * d) / D
    be = np.arange(512)
    RfT = np.zeros((32, 512), np.float32); RfT[be // 16, be] = 1
    RqfT = np.zeros((16, 512), np.float32); RqfT[be % 16, be] = 1
    RpT = np.zeros((32, 512), np.float32)
    RpT[(be // 128) * 8 + (be % 128) // 16, be] = 1
    p = np.arange(128)
    RqpT = np.zeros((16, 128), np.float32); RqpT[p % 16, p] = 1
    return {
        "CONST_CS": CS.astype(BF),
        "CONST_IDFT": IDFT.astype(np.float32),
        "CONST_RfT": RfT.astype(BF),
        "CONST_RqfT": RqfT.astype(BF),
        "CONST_RpT": RpT.astype(BF),
        "CONST_RqpT": RqpT.astype(BF),
    }


def build_nc():
    from contextlib import ExitStack
    nc = bass.Bass("TRN2", target_bir_lowering=False, detect_race_conditions=False)

    def din(name, shape, dt=F32):
        return nc.dram_tensor(name, shape, dt, kind="ExternalInput").ap()

    xT = din("xT", [D, T])
    Wqkv = din("Wqkv", [D, 3 * D])
    bqkv = din("bqkv", [3 * D])
    Wout = din("Wout", [D, D])
    bout = din("bout", [D])
    Wq_b = din("Wq_b", [D, D])
    bq_b = din("bq_b", [D])
    WkT = din("WkT", [D, D])
    bk_b = din("bk_b", [D])
    WvT = din("WvT", [D, D])
    bv_b = din("bv_b", [D])
    G1R = din("G1R", [1, 32, 4, 32])
    G2R = din("G2R", [4, 16, 1, 16])
    G1I = din("G1I", [1, 32, 4, 32])
    G2I = din("G2I", [4, 16, 1, 16])
    fbias = din("fbias", [D])
    CS_h = din("CONST_CS", [D, 1026], BF16)
    IDFT_h = din("CONST_IDFT", [1026, D])
    RfT_h = din("CONST_RfT", [32, 512], BF16)
    RqfT_h = din("CONST_RqfT", [16, 512], BF16)
    RpT_h = din("CONST_RpT", [32, 512], BF16)
    RqpT_h = din("CONST_RqpT", [16, 128], BF16)

    outT = nc.dram_tensor("outT", [D, T], F32, kind="ExternalOutput").ap()
    s2h = nc.dram_tensor("s2h", [2 * D], BF16, kind="Internal").ap()
    wsth = nc.dram_tensor("wsth", [12 * 128, 512], BF16, kind="Internal").ap()

    import kernel as _self

    with ExitStack() as ctx:
        def sbuf(name, shape, dt):
            return ctx.enter_context(nc.sbuf_tensor(name, shape, dt))

        X = [sbuf(f"X{i}", [128, T], F32) for i in range(8)]
        A = [sbuf(f"Ax{i}", [128, T], BF16) for i in range(8)]
        B = [sbuf(f"Bx{i}", [128, T], BF16) for i in range(8)]
        C = [sbuf(f"Cx{i}", [128, T], BF16) for i in range(8)]
        Dv = [sbuf(f"Dx{i}", [128, 1040], BF16) for i in range(16)]
        SBF0 = sbuf("SBF0", [128, T], BF16)
        PT = [sbuf("PT0", [128, 1024], BF16), sbuf("PT1", [128, 1024], BF16)]
        SF0 = sbuf("SF0", [128, 1024], F32)
        WVH = PT
        ones_bf = sbuf("ones_bf", [128, 128], BF16)
        ones_f = sbuf("ones_f", [128, 128], F32)
        ones1f = sbuf("ones1f", [1, 64], F32)
        colC = sbuf("colC", [128, 72], F32)
        colB = sbuf("colB", [128, 16], BF16)
        negc = sbuf("negc", [128, 8], F32)
        stq = sbuf("stq", [128, 8], F32)
        Shat = sbuf("Shat", [128, 48], F32)
        RED = sbuf("RED", [128, 68], F32)
        WNY = sbuf("WNY", [128, 16], BF16)
        recips = SF0
        s_bf = SBF0
        bvrow = SBF0
        onesr_bf = ones_bf
        RfT_s = B[0]
        RqfT_s = B[0]
        RpT_s = B[0]
        RqpT_s = B[0]
        g1s = B[2]
        g2s = B[2]
        T1 = B[1]
        T1q = B[3]
        brep = B[2]

        ps0 = ctx.enter_context(nc.psum_tensor("ps0", [128, T], F32))
        ps1 = ctx.enter_context(nc.psum_tensor("ps1", [128, T], F32))
        PS = [ps0, ps1]

        semn = ["dS", "dG", "sT", "sC", "sV", "sS", "sPV", "sE", "sRaw",
                "sDR", "sNm", "sStab", "sWc", "sST", "sPS0", "sPS1", "sRc"]
        sems = {n: ctx.enter_context(nc.semaphore(n)) for n in semn}
        cnt = {n: 0 for n in semn}

        blk_real = ctx.enter_context(nc.Block())

        class _BlkProxy:
            stop = False
            def __getattr__(self, eng):
                if self.stop:
                    return lambda f: None
                return getattr(blk_real, eng)
        blk = _BlkProxy()

        def ck(n):
            if _self.PHASE_LIM <= n:
                blk.__class__.stop = True

        def inc(ins_, name, amt=1):
            cnt[name] += amt
            ins_.then_inc(sems[name], amt)
            return cnt[name]

        def wt(e, name, val):
            if val > 0:
                e.wait_ge(sems[name], val)

        psw = [0, 0]

        def ps_acq(e, a):
            if psw[a] > 0:
                e.wait_ge(sems["sPS%d" % a], psw[a])
            psw[a] += 1

        def ps_rel(ins_, a):
            inc(ins_, "sPS%d" % a)

        tap_seen = {}
        def tap(name, aps, waits):
            if not DEBUG_TAPS:
                return
            if name in tap_seen:
                name = name + str(tap_seen[name])
            tap_seen[name] = tap_seen.get(name, 0) + 1
            rows = aps[0].shape[0]
            cols = aps[0].shape[1]
            dt_ = aps[0].dtype
            h = nc.dram_tensor(name, [len(aps), rows, cols], dt_,
                               kind="ExternalOutput").ap()
            @blk.sync
            def _(sy):
                for (nm, val) in waits:
                    wt(sy, nm, val)
                for i, ap_ in enumerate(aps):
                    inc(sy.dma_start(out=h[i], in_=ap_), "dS", 16)

        @blk.sync
        def _(sy):
            for i in range(8):
                inc(sy.dma_start(out=X[i][:, :],
                                 in_=xT[i * 128:(i + 1) * 128, :]), "dS", 16)
            for c in range(8):
                inc(sy.dma_start(out=colC[:, c:c + 1],
                                 in_=bqkv[c * 128:(c + 1) * 128][:, None]), "dS", 16)
            for c in range(8):
                inc(sy.dma_start(out=colC[:, 8 + c:9 + c],
                                 in_=bqkv[1024 + c * 128:1024 + (c + 1) * 128][:, None]), "dS", 16)
            for c in range(8):
                inc(sy.dma_start(out=colC[:, 16 + c:17 + c],
                                 in_=bout[c * 128:(c + 1) * 128][:, None]), "dS", 16)
            for c in range(8):
                inc(sy.dma_start(out=colC[:, 24 + c:25 + c],
                                 in_=fbias[c * 128:(c + 1) * 128][:, None]), "dS", 16)
            for c in range(8):
                inc(sy.dma_start(out=colC[:, 32 + c:33 + c],
                                 in_=bq_b[c * 128:(c + 1) * 128][:, None]), "dS", 16)
        dS_x = [16 * (i + 1) for i in range(8)]
        dS_prolog = cnt["dS"]

        @blk.gpsimd
        def _(g):
            for c in range(8):
                inc(g.dma_start(out=colB[:, c:c + 1],
                                in_=bk_b[c * 128:(c + 1) * 128][:, None]), "dG", 16)
            for c in range(8):
                inc(g.dma_start(out=colB[:, 8 + c:9 + c],
                                in_=bv_b[c * 128:(c + 1) * 128][:, None]), "dG", 16)
        dG_prolog = cnt["dG"]

        @blk.vector
        def _(v):
            ins_ = v.memset(ones_bf[:, :], 1.0)
            ins_ = v.memset(ones_f[:, :], 1.0)
            ins_ = v.memset(ones1f[:, :], 1.0)
            ins_ = v.memset(colC[:, :], 0.0)
            ins_ = v.memset(stq[:, :], 0.0)
            ins_ = v.memset(Shat[:, :], 0.0)
            ins_ = v.memset(negc[:, :], 0.0)
            for t in range(16):
                dview = Dv[t][:, :].rearrange("p (h f) -> p h f", f=65)
                ins_ = v.memset(dview[:, :, 64:65], 1.0)
            inc(ins_, "sV")
        ones_ready = cnt["sV"]

        def emit_ln(src, dst, chunk_ready, dst_free=None, db=None):
            sq_done = []
            st_done = []
            for i in range(8):
                use_pt = db is not None and i % 2 == 1
                @blk.vector
                def _(v, i=i, use_pt=use_pt):
                    for (nm, val) in chunk_ready[i]:
                        wt(v, nm, val)
                    if i == 1 and db is not None:
                        wt(v, db[0], db[1])
                    lim = 2 if db is not None else 1
                    if i >= lim:
                        wt(v, "sT", st_done[i - lim])
                    if use_pt:
                        ins_ = v.tensor_tensor(PT[0][:, :], src[i][:, 0:1024],
                                               src[i][:, 0:1024], OP.mult)
                        inc(v.tensor_tensor(PT[1][:, :], src[i][:, 1024:2048],
                                            src[i][:, 1024:2048], OP.mult), "sV")
                    else:
                        inc(v.tensor_tensor(SBF0[:, :], src[i][:, :], src[i][:, :],
                                            OP.mult), "sV")
                sq_done.append(cnt["sV"])

                @blk.tensor
                def _(t, i=i, use_pt=use_pt):
                    if i == 0:
                        wt(t, "sV", ones_ready)
                        ps_acq(t, 0)
                        ps_acq(t, 1)
                    wt(t, "sV", sq_done[i])
                    st = (i == 0); sp = (i == 7)
                    for h in range(2):
                        for p in range(2):
                            ins_ = t.matmul(
                                PS[h][:, p * 512:(p + 1) * 512], ones_f[:, :],
                                src[i][:, h * 1024 + p * 512:h * 1024 + (p + 1) * 512],
                                start=st, stop=sp)
                    for h in range(2):
                        for p in range(2):
                            sqs = (PT[h][:, p * 512:(p + 1) * 512] if use_pt else
                                   SBF0[:, h * 1024 + p * 512:h * 1024 + (p + 1) * 512])
                            ins_ = t.matmul(
                                PS[h][:, 1024 + p * 512:1024 + (p + 1) * 512],
                                ones_bf[:, :], sqs,
                                start=st, stop=sp)
                    inc(ins_, "sT")
                st_done.append(cnt["sT"])

            qmarks = []
            for h in range(2):
                for q in range(2):
                    @blk.vector
                    def _(v, h=h, q=q):
                        if h == 0 and q == 0:
                            wt(v, "sT", st_done[7])
                        pa = PS[h]
                        ins_ = v.tensor_scalar(SF0[:, 0:512],
                                               pa[:, 512 * q:512 * q + 512],
                                               1.0 / 1024, None, OP.mult)
                        ins_ = v.tensor_scalar(SF0[:, 512:1024],
                                               pa[:, 1024 + 512 * q:1536 + 512 * q],
                                               1.0 / 1024, 1e-5, OP.mult, OP.add)
                        ins_ = v.tensor_tensor(pa[:, 512 * q:512 * q + 512],
                                               SF0[:, 0:512], SF0[:, 0:512], OP.mult)
                        ins_ = v.tensor_tensor(SF0[:, 512:1024], SF0[:, 512:1024],
                                               pa[:, 512 * q:512 * q + 512],
                                               OP.subtract)
                        inc(v.memset(stq[:, 6:7], 0.0), "sV")
                        if q == 1:
                            ps_rel(ins_, h)
                    rec_done = cnt["sV"]

                    @blk.scalar
                    def _(s, h=h, q=q, rec_done=rec_done):
                        wt(s, "sV", rec_done)
                        ins_ = s.activation(SF0[:, 512:1024],
                                            SF0[:, 512:1024], AF.Ln)
                        ins_ = s.activation(SF0[:, 512:1024],
                                            SF0[:, 512:1024],
                                            AF.Exp, scale=-0.5)
                        ins_ = s.activation(SBF0[:, 512:1024],
                                            SF0[:, 512:1024], AF.Identity)
                        inc(s.activation(stq[:, 7:8], stq[:, 6:7], AF.Identity),
                            "sC")
                    sqrt_done = cnt["sC"]

                    @blk.vector
                    def _(v, h=h, q=q, sqrt_done=sqrt_done):
                        wt(v, "sC", sqrt_done)
                        if h == 0 and q == 0 and dst_free is not None:
                            wt(v, dst_free[0], dst_free[1])
                        c0 = h * 1024 + 512 * q
                        for i in range(8):
                            ins_ = v.tensor_tensor(SBF0[:, 0:512],
                                                   src[i][:, c0:c0 + 512],
                                                   SF0[:, 0:512], OP.subtract)
                            ins_ = v.tensor_tensor(dst[i][:, c0:c0 + 512],
                                                   SBF0[:, 0:512],
                                                   SBF0[:, 512:1024], OP.mult)
                        inc(v.memset(stq[:, 7:8], 0.0), "sV")
                    qmarks.append(cnt["sV"])
            return cnt["sV"], qmarks

        xn1_ready, xn1_q = emit_ln(X, A, [[("dS", dS_x[i])] for i in range(8)])
        ln1_done_T = cnt["sT"]
        ck(1)
        tap("t_xn1", [a[:, :] for a in A], [("sV", xn1_ready)])

        @blk.gpsimd
        def _(g):
            for j in range(8):
                inc(g.dma_start(out=C[j][:, 0:1024],
                                in_=Wqkv[j * 128:(j + 1) * 128, 2048:3072]),
                    "dG", 16)

        @blk.gpsimd
        def _(g):
            wt(g, "sT", ln1_done_T)
            inc(g.dma_start(out=bvrow[0:1, 1024:2048],
                            in_=bqkv[2 * 1024:3 * 1024][None, :]), "dG", 16)
        dg_v = cnt["dG"]

        vt_done = []
        qk_cons = []
        qk_par = []
        stab_ready = []
        uc = [0]

        def emit_vtile(tt_):
            a = uc[0] % 2
            uc[0] += 1
            @blk.tensor
            def _(t, tt_=tt_, a=a):
                if tt_ == 0:
                    wt(t, "sV", ones_ready)
                    wt(t, "dG", dG_prolog)
                    wt(t, "dG", dg_v)
                wt(t, "sV", xn1_q[tt_ // 4])
                ps_acq(t, a)
                for j in range(8):
                    for pc in range(2):
                        ins_ = t.matmul(PS[a][:, pc * 512:(pc + 1) * 512],
                                        A[j][:, tt_ * 128:(tt_ + 1) * 128],
                                        C[j][:, pc * 512:(pc + 1) * 512],
                                        start=(j == 0), stop=False)
                for pc in range(2):
                    ins_ = t.matmul(PS[a][:, pc * 512:(pc + 1) * 512],
                                    onesr_bf[0:1, 0:128],
                                    bvrow[0:1, 1024 + pc * 512:1024 + (pc + 1) * 512],
                                    start=False, stop=True)
                inc(ins_, "sT")
            vt_done.append(cnt["sT"])

            @blk.scalar
            def _(s, tt_=tt_, a=a, m=cnt["sT"]):
                wt(s, "sT", m)
                pv = PS[a][:, 0:1024].rearrange("p (h e) -> p h e", e=64)
                dvw = Dv[tt_][:, :].rearrange("p (h f) -> p h f", f=65)[:, :, 0:64]
                ins_ = s.activation(dvw, pv, AF.Identity)
                ps_rel(ins_, a)

        def emit_qk(n):
            which, hp = divmod(n, 8)
            a = uc[0] % 2
            uc[0] += 1
            qk_par.append(a)
            @blk.gpsimd
            def _(g, n=n, hp=hp, which=which):
                if n >= 2:
                    wt(g, "sWc", qk_cons[n - 2])
                col0 = hp * 128 if which == 0 else 1024 + hp * 128
                inc(g.dma_start(
                    out=PT[n % 2][:, :].rearrange("p (c m) -> p c m", m=128),
                    in_=Wqkv[:, col0:col0 + 128].rearrange("(c p) m -> p c m", p=128)),
                    "dG", 16)
            dg_n = cnt["dG"]

            @blk.tensor
            def _(t, n=n, a=a, dg_n=dg_n):
                wt(t, "dG", dg_n)
                ps_acq(t, a)
                for pc in range(4):
                    if n < 2:
                        wt(t, "sV", xn1_q[pc])
                    for j in range(8):
                        ins_ = t.matmul(PS[a][:, pc * 512:(pc + 1) * 512],
                                        PT[n % 2][:, j * 128:(j + 1) * 128],
                                        A[j][:, pc * 512:(pc + 1) * 512],
                                        start=(j == 0), stop=(j == 7))
                inc(ins_, "sWc")
            qk_cons.append(cnt["sWc"])
            mm_done = cnt["sWc"]

            @blk.scalar
            def _(s, n=n, hp=hp, which=which, a=a, mm_done=mm_done):
                if n == 0:
                    wt(s, "dS", dS_prolog)
                wt(s, "sWc", mm_done)
                dst = B[hp] if which == 0 else C[hp]
                bias_col = hp if which == 0 else 8 + hp
                ins_ = s.activation(dst[:, :], PS[a][:, :], AF.Identity,
                                    bias=colC[:, bias_col:bias_col + 1])
                ps_rel(ins_, a)
                inc(s.activation(stq[:, 7:8], stq[:, 6:7], AF.Identity), "sC")

        for tt_ in range(16):
            emit_vtile(tt_)
            if tt_ < 8:
                emit_qk(tt_)
        v_done_T = cnt["sT"]

        for hp in range(8):
            emit_qk(8 + hp)
            qk_evac_done = cnt["sC"]

            ak = qk_par[8 + hp]
            @blk.vector
            def _(v, hp=hp, qk_evac_done=qk_evac_done):
                wt(v, "sC", qk_evac_done)
                inc(v.tensor_tensor(SBF0[:, :], B[hp][:, :], B[hp][:, :], OP.mult),
                    "sV")
            sqq = cnt["sV"]

            @blk.tensor
            def _(t, ak=ak, sqq=sqq):
                wt(t, "sV", sqq)
                ps_acq(t, ak)
                for pc in range(4):
                    ins_ = t.matmul(PS[ak][:, pc * 512:(pc + 1) * 512],
                                    ones_bf[:, :],
                                    SBF0[:, pc * 512:(pc + 1) * 512],
                                    start=True, stop=True)
                inc(ins_, "sT")
            mmq = cnt["sT"]

            @blk.vector
            def _(v, hp=hp, mmq=mmq):
                wt(v, "sT", mmq)
                inc(v.tensor_tensor(SBF0[:, :], C[hp][:, :], C[hp][:, :], OP.mult),
                    "sV")
            sqk = cnt["sV"]

            @blk.vector
            def _(v, ak=ak, mmq=mmq):
                wt(v, "sT", mmq)
                ins_ = v.reduce_max(stq[:, 0:1], PS[ak][:, :], axis=AX.X)
                ps_rel(ins_, ak)

            @blk.tensor
            def _(t, ak=ak, sqk=sqk):
                wt(t, "sV", sqk)
                ps_acq(t, ak)
                for pc in range(4):
                    ins_ = t.matmul(PS[ak][:, pc * 512:(pc + 1) * 512],
                                    ones_bf[:, :],
                                    SBF0[:, pc * 512:(pc + 1) * 512],
                                    start=True, stop=True)
                inc(ins_, "sT")
            mmk = cnt["sT"]

            @blk.vector
            def _(v, ak=ak, mmk=mmk):
                wt(v, "sT", mmk)
                ins_ = v.reduce_max(stq[:, 1:2], PS[ak][:, :], axis=AX.X)
                ps_rel(ins_, ak)
                inc(v.memset(stq[:, 6:7], 0.0), "sV")
            mul_done = cnt["sV"]

            @blk.scalar
            def _(s, hp=hp, mul_done=mul_done):
                wt(s, "sV", mul_done)
                ins_ = s.activation(stq[:, 4:5], stq[:, 0:1], AF.Ln)
                ins_ = s.activation(stq[:, 5:6], stq[:, 1:2], AF.Ln)
                ins_ = s.activation(stq[:, 2:3], stq[:, 4:5], AF.Identity,
                                    bias=stq[:, 5:6])
                ins_ = s.activation(stq[:, 3:4], stq[:, 2:3], AF.Exp, scale=0.5)
                inc(s.activation(stq[:, 7:8], stq[:, 6:7], AF.Identity), "sC")
            sqrt_done = cnt["sC"]

            @blk.vector
            def _(v, hp=hp, sqrt_done=sqrt_done):
                wt(v, "sC", sqrt_done)
                inc(v.tensor_scalar_mul(negc[:, hp:hp + 1], stq[:, 3:4], -0.125),
                    "sStab")
            stab_ready.append(cnt["sStab"])

        wc_after_qk = cnt["sWc"]
        tap("t_q", [b_[:, :] for b_ in B], [("sStab", stab_ready[7])])
        tap("t_k", [c_[:, :] for c_ in C], [("sStab", stab_ready[7])])
        tap("t_v", [d_[:, :] for d_ in Dv], [("sStab", stab_ready[7]),
                                             ("sPS0", cnt["sPS0"]),
                                             ("sPS1", cnt["sPS1"])])
        tap("t_negc", [negc[:, :]], [("sStab", stab_ready[7])])

        ck(2)
        sS0 = cnt["sS"]; sPV0 = cnt["sPV"]; sE0 = cnt["sE"]
        sRaw0 = cnt["sRaw"]; sDR0 = cnt["sDR"]; sNm0 = cnt["sNm"]

        @blk.tensor
        def _(t):
            ps_acq(t, 0)
            ps_acq(t, 1)

            def emit_pv(m):
                gp, kk = divmod(m, 16)
                hpp = gp % 8
                pp = gp % 2
                wt(t, "sE", sE0 + m + 1)
                if kk == 0:
                    wt(t, "sNm", sNm0 + gp - 1)
                ins_ = t.matmul(ps1[0:65, pp * 1024:pp * 1024 + 512],
                                Dv[kk][:, 130 * hpp:130 * hpp + 65],
                                PT[kk % 2][:, 0:512],
                                start=(kk == 0), stop=(kk == 15))
                ins_ = t.matmul(ps1[0:65, pp * 1024 + 512:pp * 1024 + 1024],
                                Dv[kk][:, 130 * hpp + 65:130 * hpp + 130],
                                PT[kk % 2][:, 512:1024],
                                start=(kk == 0), stop=(kk == 15))
                inc(ins_, "sPV")

            def emit_denrep(gp):
                pp = gp % 2
                wt(t, "sRaw", sRaw0 + gp + 1)
                ins_ = t.matmul(ps1[0:64, pp * 1024:pp * 1024 + 512],
                                ones_bf[64:65, 0:64],
                                SBF0[64:65, pp * 1024:pp * 1024 + 512],
                                start=True, stop=True)
                ins_ = t.matmul(ps1[0:64, pp * 1024 + 512:pp * 1024 + 1024],
                                ones_bf[64:65, 0:64],
                                SBF0[64:65, pp * 1024 + 512:pp * 1024 + 1024],
                                start=True, stop=True)
                inc(ins_, "sDR")

            for n in range(512):
                g, kt = divmod(n, 16)
                st_, hp = g // 8, g % 8
                wt(t, "sE", sE0 + n - 1)
                slot = kt % 2
                ins_ = t.matmul(ps0[:, slot * 1024:slot * 1024 + 512],
                                C[hp][0:64, kt * 128:(kt + 1) * 128],
                                B[hp][0:64, st_ * 512:(st_ + 1) * 512],
                                start=True, stop=True, tile_position=(0, 0))
                ins_ = t.matmul(ps0[:, slot * 1024 + 512:slot * 1024 + 1024],
                                C[hp][64:128, kt * 128:(kt + 1) * 128],
                                B[hp][64:128, st_ * 512:(st_ + 1) * 512],
                                start=True, stop=True, tile_position=(64, 0))
                inc(ins_, "sS")
                if kt == 13 and g >= 1:
                    emit_denrep(g - 1)
                if n >= 2:
                    emit_pv(n - 2)
            emit_pv(510)
            emit_pv(511)
            emit_denrep(31)

        @blk.scalar
        def _(s):
            wt(s, "sWc", wc_after_qk)
            for g in range(32):
                st_, hp = g // 8, g % 8
                for kt in range(16):
                    wt(s, "sS", sS0 + 16 * g + kt + 1)
                    wt(s, "sPV", sPV0 + 16 * g + kt - 1)
                    if g < 8 and kt == 0:
                        wt(s, "sStab", stab_ready[hp])
                    slot = kt % 2
                    ins_ = s.activation(PT[slot][:, :],
                                        ps0[:, slot * 1024:slot * 1024 + 1024],
                                        AF.Exp, bias=negc[:, hp:hp + 1], scale=0.125)
                    inc(ins_, "sE")

        @blk.vector
        def _(v):
            for g in range(32):
                p2 = g % 2
                if g >= 1:
                    gp = g - 1
                    stp, hpp = gp // 8, gp % 8
                    pp = gp % 2
                    wt(v, "sDR", sDR0 + gp + 1)
                    ins_ = v.tensor_tensor(A[hpp][0:64, stp * 512:(stp + 1) * 512],
                                           SBF0[0:64, pp * 1024:pp * 1024 + 512],
                                           ps1[0:64, pp * 1024:pp * 1024 + 512],
                                           OP.mult)
                    ins_ = v.tensor_tensor(A[hpp][64:128, stp * 512:(stp + 1) * 512],
                                           SBF0[0:64, pp * 1024 + 512:pp * 1024 + 1024],
                                           ps1[0:64, pp * 1024 + 512:pp * 1024 + 1024],
                                           OP.mult)
                    inc(ins_, "sNm")
                wt(v, "sPV", sPV0 + 16 * g + 16)
                ins_ = v.tensor_copy(SBF0[0:64, p2 * 1024:p2 * 1024 + 512],
                                     ps1[0:64, p2 * 1024:p2 * 1024 + 512])
                ins_ = v.tensor_copy(SBF0[0:64, p2 * 1024 + 512:p2 * 1024 + 1024],
                                     ps1[0:64, p2 * 1024 + 512:p2 * 1024 + 1024])
                with nc.allow_low_precision(reason="softmax denominator recip "
                                            "stored bf16 as matmul operand"):
                    ins_ = v.reciprocal(SBF0[64:65, p2 * 1024:p2 * 1024 + 512],
                                        ps1[64:65, p2 * 1024:p2 * 1024 + 512])
                    ins_ = v.reciprocal(SBF0[64:65, p2 * 1024 + 512:p2 * 1024 + 1024],
                                        ps1[64:65, p2 * 1024 + 512:p2 * 1024 + 1024])
                inc(v.memset(stq[:, 6:7], 0.0), "sRaw")
            gp = 31
            stp, hpp = gp // 8, gp % 8
            pp = gp % 2
            wt(v, "sDR", sDR0 + 32)
            ins_ = v.tensor_tensor(A[hpp][0:64, stp * 512:(stp + 1) * 512],
                                   SBF0[0:64, pp * 1024:pp * 1024 + 512],
                                   ps1[0:64, pp * 1024:pp * 1024 + 512], OP.mult)
            ins_ = v.tensor_tensor(A[hpp][64:128, stp * 512:(stp + 1) * 512],
                                   SBF0[0:64, pp * 1024 + 512:pp * 1024 + 1024],
                                   ps1[0:64, pp * 1024 + 512:pp * 1024 + 1024],
                                   OP.mult)
            inc(ins_, "sNm")
            ps_rel(v.memset(stq[:, 2:3], 0.0), 0)
            ps_rel(v.memset(stq[:, 2:3], 0.0), 1)
        ck(3)
        attn_done = cnt["sNm"]
        tap("t_attn", [a[:, :] for a in A], [("sNm", attn_done)])

        def emit_proj_residual(lhs_fn, rhs, bias_base, w_stream, first_waits,
                               rhs_qmarks=None):
            done = []
            wc_marks = []
            for i in range(8):
                if w_stream is not None:
                    @blk.gpsimd
                    def _(g, i=i):
                        if i == 0:
                            for (nm, val) in first_waits:
                                wt(g, nm, val)
                        if i >= 2:
                            wt(g, "sWc", wc_marks[i - 2])
                        inc(g.dma_start(
                            out=PT[i % 2][:, :].rearrange("p (c m) -> p c m", m=128),
                            in_=w_stream(i).rearrange("(c p) m -> p c m", p=128)),
                            "dG", 16)
                    dg_i = cnt["dG"]
                else:
                    dg_i = 0

                @blk.tensor
                def _(t, i=i, dg_i=dg_i):
                    if i == 0:
                        for (nm, val) in first_waits:
                            wt(t, nm, val)
                    if w_stream is not None:
                        wt(t, "dG", dg_i)
                    a = i % 2
                    ps_acq(t, a)
                    for pc in range(4):
                        if rhs_qmarks is not None and i < 2:
                            wt(t, "sV", rhs_qmarks[pc])
                        for j in range(8):
                            ins_ = t.matmul(PS[a][:, pc * 512:(pc + 1) * 512],
                                            lhs_fn(i, j),
                                            rhs[j][:, pc * 512:(pc + 1) * 512],
                                            start=(j == 0), stop=(j == 7))
                    inc(ins_, "sWc")
                wc_marks.append(cnt["sWc"])
                mm_i = cnt["sWc"]

                @blk.vector
                def _(v, i=i, mm_i=mm_i):
                    wt(v, "sWc", mm_i)
                    a = i % 2
                    if bias_base is not None:
                        ins_ = v.scalar_tensor_tensor(
                            X[i][:, :], PS[a][:, :],
                            colC[:, bias_base + i:bias_base + i + 1],
                            X[i][:, :], OP.add, OP.add)
                        ins_ = v.memset(stq[:, 6:7], 0.0)
                    else:
                        ins_ = v.tensor_tensor(X[i][:, :], PS[a][:, :],
                                               X[i][:, :], OP.add)
                    ps_rel(ins_, a)
                done.append(("sPS%d" % (i % 2), cnt["sPS%d" % (i % 2)]))
            return done, ("sWc", wc_marks[-1])

        xdone, pj_mark = emit_proj_residual(
            lhs_fn=lambda i, j: PT[i % 2][:, j * 128:(j + 1) * 128],
            rhs=A, bias_base=16,
            w_stream=lambda i: Wout[:, i * 128:(i + 1) * 128],
            first_waits=[("sNm", attn_done), ("sPV", sPV0 + 512)])

        ck(4)
        tap("t_x4", [x[:, :] for x in X], xdone)
        @blk.sync
        def _(sy):
            wt(sy, "sNm", attn_done)
            inc(sy.dma_start(out=B[0][0:32, 0:512], in_=RfT_h), "dS", 16)
            inc(sy.dma_start(out=B[0][0:16, 512:1024], in_=RqfT_h), "dS", 16)
            inc(sy.dma_start(out=B[2][0:32, 0:512], in_=RpT_h), "dS", 16)
            inc(sy.dma_start(out=B[2][0:16, 896:1024], in_=RqpT_h), "dS", 16)
        dS_Rm = cnt["dS"]

        @blk.gpsimd
        def _(g):
            wt(g, "sNm", attn_done)
            for rr, G1h, G2h in ((0, G1R, G2R), (1, G1I, G2I)):
                for c in range(4):
                    k = 4 * rr + c
                    inc(g.dma_start(out=B[2][0:32, 512 + 32 * k:544 + 32 * k],
                                    in_=G1h[0, :, c, :]), "dG", 16)
                    inc(g.dma_start(out=B[2][0:16, 768 + 16 * k:784 + 16 * k],
                                    in_=G2h[c, :, 0, :]), "dG", 16)
        dg_g = cnt["dG"]

        def t1ap(c):
            return (B[1] if c < 2 else B[3])[0:32, (c % 2) * 512:(c % 2) * 512 + 512]

        def t1qap(c):
            return (B[5] if c < 2 else B[7])[0:16, (c % 2) * 512:(c % 2) * 512 + 512]

        for rr in range(2):
            for c in range(4):
                @blk.tensor
                def _(t, rr=rr, c=c):
                    if rr == 0 and c == 0:
                        wt(t, "dS", dS_Rm)
                        wt(t, "dG", dg_g)
                    k = 4 * rr + c
                    ps_acq(t, 0)
                    ins_ = t.matmul(ps0[0:32, 0:512],
                                    B[2][0:32, 512 + 32 * k:544 + 32 * k],
                                    B[0][0:32, 0:512], start=True, stop=True)
                    ins_ = t.matmul(ps0[0:16, 512:1024],
                                    B[2][0:16, 768 + 16 * k:784 + 16 * k],
                                    B[0][0:16, 512:1024], start=True, stop=True)
                    inc(ins_, "sWc")
                t1_done = cnt["sWc"]

                @blk.vector
                def _(v, c=c, t1_done=t1_done):
                    wt(v, "sWc", t1_done)
                    ins_ = v.tensor_copy(t1ap(c), ps0[0:32, 0:512])
                    ins_ = v.tensor_copy(t1qap(c), ps0[0:16, 512:1024])
                    ps_rel(ins_, 0)
            t1_all = cnt["sPS0"]

            for ic in range(4):
                for c in range(4):
                    a2 = (4 * ic + c) % 2
                    @blk.tensor
                    def _(t, ic=ic, c=c, a2=a2, t1_all=t1_all):
                        if c == 0:
                            wt(t, "sPS0", t1_all)
                        ps_acq(t, a2)
                        ins_ = t.matmul(PS[a2][:, 0:512],
                                        B[2][0:32, ic * 128:(ic + 1) * 128],
                                        t1ap(c), start=True, stop=True)
                        ins_ = t.matmul(PS[a2][:, 512:1024], B[2][0:16, 896:1024],
                                        t1qap(c), start=True, stop=True)
                        inc(ins_, "sT")
                    ab_done = cnt["sT"]

                    @blk.vector
                    def _(v, rr=rr, ic=ic, c=c, a2=a2, ab_done=ab_done):
                        wt(v, "sT", ab_done)
                        ins_ = v.tensor_copy(B[4][:, 0:512], PS[a2][:, 512:1024])
                        if c == 0:
                            ins_ = v.tensor_tensor(B[6][:, 0:512], PS[a2][:, 0:512],
                                                   B[4][:, 0:512], OP.mult)
                        else:
                            ins_ = v.tensor_tensor(B[6][:, 512:1024],
                                                   PS[a2][:, 0:512],
                                                   B[4][:, 0:512], OP.mult)
                            dst = (Dv[4 * rr + ic][:, 0:512] if c == 3 else
                                   B[6][:, 0:512])
                            ins_ = v.tensor_tensor(dst, B[6][:, 0:512],
                                                   B[6][:, 512:1024], OP.add)
                        ps_rel(ins_, a2)
                        if c == 3:
                            inc(v.memset(stq[:, 7:8], 0.0), "sV")
                acc_done = cnt["sV"]

                @blk.scalar
                def _(s, rr=rr, ic=ic, acc_done=acc_done):
                    wt(s, "sV", acc_done)
                    inc(s.activation(stq[:, 7:8], stq[:, 6:7], AF.Identity), "sC")
        wt_cast = cnt["sC"]

        @blk.vector
        def _(v, wt_cast=wt_cast):
            wt(v, "sC", wt_cast)
            for ic in range(4):
                ins_ = v.tensor_scalar_mul(Dv[8 + ic][:, 0:512],
                                           Dv[4 + ic][:, 0:512], -1.0)
            inc(ins_, "sV")
        ffn1_wbuilt = cnt["sV"]
        ck(41)

        def ffn_lhs(i, j):
            if j < 4:
                if i < 4:
                    return Dv[j][:, i * 128:(i + 1) * 128]
                return Dv[4 + j][:, (i - 4) * 128:(i - 3) * 128]
            if i < 4:
                return Dv[8 + (j - 4)][:, i * 128:(i + 1) * 128]
            return Dv[j - 4][:, (i - 4) * 128:(i - 3) * 128]

        tap("t_ffnw", [Dv[i][:, 0:512] for i in range(12)],
            [("sV", ffn1_wbuilt)])

        @blk.sync
        def _(sy):
            wt(sy, "sNm", attn_done)
            for j in range(8):
                inc(sy.dma_start(out=C[j][:, 0:1026],
                                 in_=CS_h[j * 128:(j + 1) * 128, :]), "dS", 16)
        dS_cs = cnt["dS"]

        @blk.gpsimd
        def _(g):
            wt(g, "sNm", attn_done)
            for j in range(8):
                inc(g.dma_start(out=B[j][:, 1024:2048],
                                in_=WkT[j * 128:(j + 1) * 128, :]), "dG", 16)
        dg_kstage = cnt["dG"]

        xn2_ready, xn2_q = emit_ln(X, A, [[xdone[i]] for i in range(8)], dst_free=pj_mark,
                            db=pj_mark)

        ck(5)
        fold_marks = {}
        ev_marks = []
        for i in range(8):
            @blk.tensor
            def _(t, i=i):
                if i == 0:
                    wt(t, "dG", dg_kstage)
                    wt(t, "dS", dS_cs)
                a = i % 2
                ps_acq(t, a)
                for j in range(8):
                    for pc in range(2):
                        ins_ = t.matmul(PS[a][:, pc * 512:(pc + 1) * 512],
                                        B[j][:, 1024 + i * 128:1024 + (i + 1) * 128],
                                        C[j][:, pc * 512:(pc + 1) * 512],
                                        start=(j == 0), stop=(j == 7))
                    ins_ = t.matmul(PS[a][:, 1024:1026],
                                    B[j][:, 1024 + i * 128:1024 + (i + 1) * 128],
                                    C[j][:, 1024:1026],
                                    start=(j == 0), stop=(j == 7))
                inc(ins_, "sWc")
            mmf = cnt["sWc"]
            if i == 7:
                fold_marks["k"] = mmf

            @blk.scalar
            def _(s, i=i, mmf=mmf):
                wt(s, "sWc", mmf)
                a = i % 2
                ins_ = s.activation(B[i][:, 0:1024], PS[a][:, 0:1024], AF.Identity)
                ins_ = s.activation(WNY[:, i:i + 1], PS[a][:, 1024:1025],
                                    AF.Identity)
                ps_rel(ins_, a)
            ev_marks.append(cnt["sPS%d" % (i % 2)])

        xdone, ffn1_mark = emit_proj_residual(
            lhs_fn=ffn_lhs, rhs=A, bias_base=24, w_stream=None,
            first_waits=[("sV", ffn1_wbuilt)], rhs_qmarks=xn2_q)

        ck(6)
        tap("t_x6", [x[:, :] for x in X], xdone)
        xn3_ready, xn3_q = emit_ln(X, A, [[xdone[i]] for i in range(8)], dst_free=ffn1_mark,
                            db=pj_mark)

        ck(7)
        @blk.gpsimd
        def _(g):
            wt(g, "sV", ffn1_wbuilt)
            for i in range(12):
                inc(g.dma_start(out=wsth[i * 128:(i + 1) * 128, :],
                                in_=Dv[i][:, 0:512]), "dG", 16)
        w_stash = cnt["dG"]

        @blk.gpsimd
        def _(g):
            wt(g, ffn1_mark[0], ffn1_mark[1])
            for j in range(8):
                inc(g.dma_start(out=Dv[8 + j][:, 0:1024],
                                in_=WvT[j * 128:(j + 1) * 128, :]), "dG", 16)
        dg_vstage = cnt["dG"]

        for i in range(8):
            @blk.tensor
            def _(t, i=i):
                if i == 0:
                    wt(t, "dG", dg_vstage)
                a = i % 2
                ps_acq(t, a)
                for j in range(8):
                    for pc in range(2):
                        ins_ = t.matmul(PS[a][:, pc * 512:(pc + 1) * 512],
                                        Dv[8 + j][:, i * 128:(i + 1) * 128],
                                        C[j][:, pc * 512:(pc + 1) * 512],
                                        start=(j == 0), stop=(j == 7))
                    ins_ = t.matmul(PS[a][:, 1024:1026],
                                    Dv[8 + j][:, i * 128:(i + 1) * 128],
                                    C[j][:, 1024:1026],
                                    start=(j == 0), stop=(j == 7))
                inc(ins_, "sWc")
            mmf = cnt["sWc"]
            if i == 7:
                fold_marks["v"] = mmf

            @blk.scalar
            def _(s, i=i, mmf=mmf):
                wt(s, "sWc", mmf)
                if i == 0:
                    wt(s, "dG", w_stash)
                    wt(s, ffn1_mark[0], ffn1_mark[1])
                a = i % 2
                ins_ = s.activation(Dv[i][:, 0:1024], PS[a][:, 0:1024], AF.Identity)
                ins_ = s.activation(WNY[:, 8 + i:9 + i], PS[a][:, 1024:1025],
                                    AF.Identity)
                ps_rel(ins_, a)
            ev_marks.append(cnt["sPS%d" % (i % 2)])

        @blk.tensor
        def _(t):
            wt(t, "dG", dG_prolog)
            for si, (cb0, sl0) in enumerate(((0, 40), (8, 49))):
                a = si % 2
                ps_acq(t, a)
                for cchunk in range(8):
                    for j in range(8):
                        ins_ = t.matmul(PS[a][:, cchunk:cchunk + 1],
                                        C[j][:, cchunk * 128:(cchunk + 1) * 128],
                                        colB[:, cb0 + j:cb0 + j + 1],
                                        start=(j == 0), stop=(j == 7))
                for j in range(8):
                    ins_ = t.matmul(PS[a][0:1, 8:9],
                                    C[j][:, 1024:1025],
                                    colB[:, cb0 + j:cb0 + j + 1],
                                    start=(j == 0), stop=(j == 7))
                inc(ins_, "sT")
        bias_mm = cnt["sT"]

        @blk.vector
        def _(v):
            wt(v, "sT", bias_mm)
            for si, sl0 in ((0, 40), (1, 49)):
                a = si % 2
                for cchunk in range(8):
                    ins_ = v.tensor_copy(colC[:, sl0 + cchunk:sl0 + cchunk + 1],
                                         PS[a][:, cchunk:cchunk + 1])
                ins_ = v.tensor_copy(colC[0:1, sl0 + 8:sl0 + 9], PS[a][0:1, 8:9])
                ps_rel(ins_, a)
        bias_cols_done = cnt["sPS1"]
        bias_cols_done0 = cnt["sPS0"]

        tap("t_wkcs", [b_[:, 0:1024] for b_ in B],
            [("sPS0", cnt["sPS0"]), ("sPS1", cnt["sPS1"])])
        tap("t_wvcs", [d_[:, 0:1024] for d_ in Dv[0:8]],
            [("sPS0", cnt["sPS0"]), ("sPS1", cnt["sPS1"])])
        tap("t_wny", [WNY[:, :]], [("sPS0", cnt["sPS0"]), ("sPS1", cnt["sPS1"])])
        tap("t_bcol", [colC[:, :]], [("sPS0", cnt["sPS0"]), ("sPS1", cnt["sPS1"])])
        q_marks = []
        wcq = []
        for i in range(8):
            @blk.gpsimd
            def _(g, i=i):
                if i == 0:
                    wt(g, "sPV", sPV0 + 512)
                    wt(g, "sPS0", bias_cols_done0)
                if i >= 2:
                    wt(g, "sWc", wcq[i - 2])
                inc(g.dma_start(
                    out=PT[i % 2][:, :].rearrange("p (c m) -> p c m", m=128),
                    in_=Wq_b[:, i * 128:(i + 1) * 128].rearrange("(c p) m -> p c m", p=128)),
                    "dG", 16)
            dg_q = cnt["dG"]

            @blk.tensor
            def _(t, i=i, dg_q=dg_q):
                wt(t, "dG", dg_q)
                a = i % 2
                ps_acq(t, a)
                for pc in range(4):
                    if i < 2:
                        wt(t, "sV", xn3_q[pc])
                    for j in range(8):
                        ins_ = t.matmul(PS[a][:, pc * 512:(pc + 1) * 512],
                                        PT[i % 2][:, j * 128:(j + 1) * 128],
                                        A[j][:, pc * 512:(pc + 1) * 512],
                                        start=(j == 0), stop=(j == 7))
                inc(ins_, "sWc")
            wcq.append(cnt["sWc"])
            mm_q = cnt["sWc"]

            @blk.scalar
            def _(s, i=i, mm_q=mm_q):
                wt(s, "sWc", mm_q)
                a = i % 2
                ins_ = s.activation(C[i][:, :], PS[a][:, :], AF.Identity,
                                    bias=colC[:, 32 + i:33 + i])
                ps_rel(ins_, a)
            q_marks.append(cnt["sPS%d" % (i % 2)])
        q_done_marks = list(q_marks)

        @blk.gpsimd
        def _(g):
            wt(g, "sWc", fold_marks["k"])
            for r in range(8):
                for pc in range(2):
                    inc(g.dma_start(
                        out=B[r][:, 1024 + pc * 512:1536 + pc * 512],
                        in_=IDFT_h[r * 128:(r + 1) * 128, pc * 512:(pc + 1) * 512]),
                        "dG", 16)

        tap("t_qb", [c_[:, :] for c_ in C],
            [("sPS0", cnt["sPS0"]), ("sPS1", cnt["sPS1"])])
        prod_marks = []
        sf0_free_marks = []
        _dbg2 = [0]
        for fc in range(4):
            for tq in range(4):
                it = 4 * fc + tq
                @blk.tensor
                def _(t, fc=fc, tq=tq, it=it):
                    if it < 4:
                        wt(t, "sV", xn3_q[tq])
                    ps_acq(t, 1)
                    for half, f0 in ((0, fc), (1, 4 + fc)):
                        for j in range(8):
                            ins_ = t.matmul(
                                ps1[:, half * 512:(half + 1) * 512],
                                Dv[j][:, f0 * 128:(f0 + 1) * 128],
                                A[j][:, tq * 512:(tq + 1) * 512],
                                start=(j == 0), stop=(j == 7))
                    inc(ins_, "sT")
                    ps_acq(t, 0)
                    for half, f0 in ((0, fc), (1, 4 + fc)):
                        for j in range(8):
                            ins_ = t.matmul(
                                ps0[:, half * 512:(half + 1) * 512],
                                B[j][:, f0 * 128:(f0 + 1) * 128],
                                A[j][:, tq * 512:(tq + 1) * 512],
                                start=(j == 0), stop=(j == 7))
                    inc(ins_, "sT")
                vmm = cnt["sT"] - 1
                kmm = cnt["sT"]

                @blk.scalar
                def _(s, fc=fc, vmm=vmm, it=it):
                    wt(s, "sT", vmm)
                    if it >= 1:
                        wt(s, "sV", sf0_free_marks[it - 1])
                    ins_ = s.activation(SF0[:, 0:512], ps1[:, 0:512], AF.Identity,
                                        bias=colC[:, 49 + fc:50 + fc])
                    ins_ = s.activation(SF0[:, 512:1024], ps1[:, 512:1024],
                                        AF.Identity, bias=colC[:, 53 + fc:54 + fc])
                    ps_rel(ins_, 1)
                vev = cnt["sPS1"]

                if DEBUG_TAPS and fc == 0 and tq == 0:
                    @blk.vector
                    def _(v, vev=vev, kmm=kmm):
                        wt(v, "sPS1", vev)
                        wt(v, "sT", kmm)
                        ins_ = v.tensor_copy(PT[0][:, :], ps0[:, 0:1024])
                        ins_ = v.tensor_copy(PT[1][:, :], SF0[:, 0:1024])
                        inc(ins_, "sV")
                    dbg_mark = cnt["sV"]
                    tap("t_k0", [PT[0][:, :]], [("sV", dbg_mark)])
                    tap("t_v0", [PT[1][:, :]], [("sV", dbg_mark)])
                    dbg_ds = cnt["dS"]

                @blk.vector
                def _(v, fc=fc, tq=tq, vev=vev, kmm=kmm):
                    if DEBUG_TAPS and fc == 0 and tq == 0:
                        wt(v, "dS", dbg_ds)
                    if DEBUG_TAPS and it >= 1:
                        wt(v, "dS", _dbg2[0])
                    wt(v, "sPS1", vev)
                    wt(v, "sT", kmm)
                    combos = [(0, 0, 0), (1, 1, 1), (0, 1, 2), (1, 0, 3)]
                    tmps = [ps0[:, 1024:1536], ps0[:, 1536:2048]]
                    red_q = []
                    for idx, (kh, vh, k) in enumerate(combos):
                        bcol = 40 + fc if kh == 0 else 44 + fc
                        if idx >= 2:
                            pk, ptmp = red_q.pop(0)
                            v.reduce_sum(RED[:, it * 4 + pk:it * 4 + pk + 1],
                                         ptmp, axis=AX.X)
                        v.scalar_tensor_tensor(
                            tmps[idx % 2], ps0[:, kh * 512:(kh + 1) * 512],
                            colC[:, bcol:bcol + 1],
                            SF0[:, vh * 512:(vh + 1) * 512],
                            OP.add, OP.mult)
                        red_q.append((k, tmps[idx % 2]))
                    inc(v.memset(stq[:, 6:7], 0.0), "sV")
                    for pk, ptmp in red_q:
                        ins_ = v.reduce_sum(RED[:, it * 4 + pk:it * 4 + pk + 1],
                                            ptmp, axis=AX.X)
                    ps_rel(ins_, 0)
                    inc(v.memset(stq[:, 7:8], 0.0), "sV")
                sf0_free_marks.append(cnt["sV"] - 1)
                prod_marks.append(cnt["sV"])
                if DEBUG_TAPS:
                    tap("t_sh%d" % it, [Shat[:, 0:12]], [("sV", prod_marks[it])])
                    _dbg2[0] = cnt["dS"]
        shat_main = cnt["sV"]
        last_vmm = vmm

        @blk.tensor
        def _(t):
            ps_acq(t, 1)
            for j in range(8):
                for pc in range(4):
                    ins_ = t.matmul(ps1[0:1, pc * 512:(pc + 1) * 512],
                                    WNY[:, 8 + j:9 + j],
                                    A[j][:, pc * 512:(pc + 1) * 512],
                                    start=(j == 0), stop=(j == 7))
            inc(ins_, "sT")
            ps_acq(t, 0)
            for j in range(8):
                for pc in range(4):
                    ins_ = t.matmul(ps0[0:1, pc * 512:(pc + 1) * 512],
                                    WNY[:, j:j + 1],
                                    A[j][:, pc * 512:(pc + 1) * 512],
                                    start=(j == 0), stop=(j == 7))
            inc(ins_, "sT")
        ny_v = cnt["sT"] - 1
        ny_k = cnt["sT"]

        @blk.scalar
        def _(s):
            wt(s, "sT", ny_v)
            ins_ = s.activation(SBF0[0:1, 0:2048], ps1[0:1, 0:2048], AF.Identity,
                                bias=colC[0:1, 57:58])
            ps_rel(s.activation(stq[:, 7:8], stq[:, 6:7], AF.Identity), 1)
        ny_vev = cnt["sPS1"]

        @blk.vector
        def _(v):
            wt(v, "sPS1", ny_vev)
            wt(v, "sT", ny_k)
            ins_ = v.scalar_tensor_tensor(ps1[0:1, 1024:2048], ps0[0:1, 0:1024],
                                          colC[0:1, 48:49], SBF0[0:1, 0:1024],
                                          OP.add, OP.mult)
            ins_ = v.reduce_sum(RED[0:1, 64:65], ps1[0:1, 1024:2048], axis=AX.X)
            ins_ = v.scalar_tensor_tensor(ps1[0:1, 1024:2048], ps0[0:1, 1024:2048],
                                          colC[0:1, 48:49], SBF0[0:1, 1024:2048],
                                          OP.add, OP.mult)
            ins_ = v.reduce_sum(RED[0:1, 65:66], ps1[0:1, 1024:2048], axis=AX.X)
            ps_rel(ins_, 0)
            inc(v.memset(stq[:, 6:7], 0.0), "sV")
        ny_red = cnt["sV"]

        @blk.gpsimd
        def _(g):
            wt(g, "sV", ny_red)
            inc(g.dma_start(out=SBF0[0:1, 0:1024], in_=IDFT_h[1024:1025, :]),
                "dG", 16)
        idft_pieces_dg = cnt["dG"]

        @blk.scalar
        def _(s):
            wt(s, "sV", ny_red)
            inc(s.activation(stq[:, 7:8], stq[:, 6:7], AF.Identity), "sC")
        hop1 = cnt["sC"]

        @blk.vector
        def _(v):
            wt(v, "sC", hop1)
            rv = RED[:, 0:64].rearrange("p (i k) -> p i k", k=4)
            ins_ = v.tensor_tensor(Shat[:, 8:24], rv[:, :, 0:1], rv[:, :, 1:2],
                                   OP.subtract)
            ins_ = v.tensor_tensor(Shat[:, 24:40], rv[:, :, 2:3], rv[:, :, 3:4],
                                   OP.add)
            inc(v.memset(stq[:, 6:7], 0.0), "sV")
        bulk1 = cnt["sV"]

        @blk.scalar
        def _(s):
            wt(s, "sV", bulk1)
            inc(s.activation(stq[:, 7:8], stq[:, 6:7], AF.Identity), "sC")
        hop2 = cnt["sC"]

        @blk.vector
        def _(v):
            wt(v, "sC", hop2)
            ins_ = v.reduce_sum(Shat[:, 0:4],
                                Shat[:, 8:24].rearrange("p (f t) -> p f t", t=4),
                                axis=AX.X)
            ins_ = v.reduce_sum(Shat[:, 4:8],
                                Shat[:, 24:40].rearrange("p (f t) -> p f t", t=4),
                                axis=AX.X)
            inc(v.memset(stq[:, 6:7], 0.0), "sV")
        shat_done = cnt["sV"]

        @blk.scalar
        def _(s):
            wt(s, "sV", shat_done)
            ins_ = s.activation(Shat[0:1, 8:9], RED[0:1, 64:65], AF.Identity,
                                bias=RED[0:1, 65:66])
            inc(s.activation(stq[:, 7:8], stq[:, 6:7], AF.Identity), "sC")
        sny_done = cnt["sC"]

        tap("t_shat", [Shat[:, :]], [("sV", shat_done), ("sC", sny_done)])
        @blk.vector
        def _(v):
            wt(v, "sV", shat_done)
            wt(v, "sC", sny_done)
            inc(v.tensor_copy(colB[:, 0:9], Shat[:, 0:9]), "sV")
        shatbf_done = cnt["sV"]

        @blk.tensor
        def _(t):
            wt(t, "sV", shatbf_done)
            wt(t, "dG", idft_pieces_dg)
            ps_acq(t, 1)
            for pc in range(2):
                for r in range(9):
                    if r < 8:
                        lhs = colB[:, r:r + 1]
                        rhs = B[r][:, 1024 + pc * 512:1536 + pc * 512]
                    else:
                        lhs = colB[0:1, 8:9]
                        rhs = SBF0[0:1, pc * 512:(pc + 1) * 512]
                    ins_ = t.matmul(ps1[0:1, pc * 512:(pc + 1) * 512],
                                    lhs, rhs, start=(r == 0), stop=(r == 8))
            inc(ins_, "sT")
        s_time_done = cnt["sT"]

        @blk.scalar
        def _(s):
            wt(s, "sT", s_time_done)
            ins_ = s.activation(s_bf[0:1, 0:1024], ps1[0:1, 0:1024], AF.Identity)
            ins_ = s.activation(s_bf[0:1, 1024:2048], ps1[0:1, 0:1024], AF.Identity)
            inc(s.activation(stq[:, 7:8], stq[:, 6:7], AF.Identity), "sC")
            ps_rel(s.activation(stq[:, 7:8], stq[:, 6:7], AF.Identity), 1)
        sbf_done = cnt["sC"]

        @blk.sync
        def _(sy):
            wt(sy, "sC", sbf_done)
            wt(sy, "sWc", fold_marks["v"])
            inc(sy.dma_start(out=s2h, in_=s_bf[0:1, :]), "dS", 16)
        s2_w = cnt["dS"]

        @blk.sync
        def _(sy):
            wt(sy, "dS", s2_w)
            for j in range(8):
                win = bass.AP(s2h.tensor, j * 128, [[1, 128], [1, 1024]])
                inc(sy.dma_start(out=B[j][:, 1024:2048], in_=win), "dS", 16)
        circ_done = cnt["dS"]

        tap("t_sbf", [s_bf[0:1, :]], [("dS", circ_done)])
        tap("t_circ", [b_[:, 1024:2048] for b_ in B], [("dS", circ_done)])
        fin_marks = []
        for mc in range(8):
            @blk.tensor
            def _(t, mc=mc):
                if mc == 0:
                    wt(t, "dS", circ_done)
                a = mc % 2
                ps_acq(t, a)
                for j in range(8):
                    for pc in range(4):
                        ins_ = t.matmul(PS[a][:, pc * 512:(pc + 1) * 512],
                                        B[j][:, 1024 + mc * 128:1024 + (mc + 1) * 128],
                                        C[j][:, pc * 512:(pc + 1) * 512],
                                        start=(j == 0), stop=(j == 7))
                inc(ins_, "sT")
            mmf = cnt["sT"]

            @blk.vector
            def _(v, mc=mc, mmf=mmf):
                wt(v, "sT", mmf)
                a = mc % 2
                ins_ = v.tensor_tensor(X[mc][:, :], PS[a][:, :], X[mc][:, :], OP.add)
                ps_rel(ins_, a)
            fin_marks.append(("sPS%d" % (mc % 2), cnt["sPS%d" % (mc % 2)]))
        final_mm_done = cnt["sT"]

        @blk.gpsimd
        def _(g):
            wt(g, "dG", w_stash)
            wt(g, "sT", last_vmm)
            for i in range(8):
                inc(g.dma_start(out=Dv[i][:, 0:512],
                                in_=wsth[i * 128:(i + 1) * 128, :]), "dG", 16)
        w_reload = cnt["dG"]

        ck(9)
        xn4_ready, xn4_q = emit_ln(X, A, [[fin_marks[i]] for i in range(8)],
                            dst_free=("sT", ny_k), db=("sWc", wcq[-1]))

        ck(95)
        def ffn_lhs2(i, j):
            if j < 4:
                if i < 4:
                    return Dv[j][:, i * 128:(i + 1) * 128]
                return Dv[4 + j][:, (i - 4) * 128:(i - 3) * 128]
            if i < 4:
                return Dv[8 + (j - 4)][:, i * 128:(i + 1) * 128]
            return Dv[j - 4][:, (i - 4) * 128:(i - 3) * 128]

        xdone_f, _mk2 = emit_proj_residual(
            lhs_fn=ffn_lhs2, rhs=A, bias_base=24, w_stream=None,
            first_waits=[("dG", w_reload)], rhs_qmarks=xn4_q)

        @blk.sync
        def _(sy):
            for i in range(8):
                wt(sy, xdone_f[i][0], xdone_f[i][1])
                inc(sy.dma_start(out=outT[i * 128:(i + 1) * 128, :],
                                 in_=X[i][:, :]), "sST", 16)
            sy.wait_ge(sems["sST"], cnt["sST"])
            if DEBUG_TAPS:
                sy.wait_ge(sems["dS"], cnt["dS"])

    return nc


_NC_CACHE = []


def kernel(**inputs):
    consts = _host_consts()
    x = np.ascontiguousarray(inputs["L0_attn_a_data"], dtype=np.float32)
    f32 = lambda a: np.ascontiguousarray(a, dtype=np.float32)
    base = {
        "Wqkv": f32(inputs["L0_attn_a_W_qkv"]),
        "bqkv": f32(inputs["L0_attn_a_b_qkv"]),
        "Wout": f32(inputs["L0_attn_a_W_out"]),
        "bout": f32(inputs["L0_attn_a_b_out"]),
        "Wq_b": f32(inputs["L0_attn_b_Wq"]),
        "bq_b": f32(inputs["L0_attn_b_bq"]),
        "WkT": np.ascontiguousarray(np.asarray(inputs["L0_attn_b_Wk"], dtype=np.float32).T),
        "bk_b": f32(inputs["L0_attn_b_bk"]),
        "WvT": np.ascontiguousarray(np.asarray(inputs["L0_attn_b_Wv"], dtype=np.float32).T),
        "bv_b": f32(inputs["L0_attn_b_bv"]),
        "G1R": f32(inputs["L0_ffn_G1_R"]),
        "G2R": f32(inputs["L0_ffn_G2_R"]),
        "G1I": f32(inputs["L0_ffn_G1_I"]),
        "G2I": f32(inputs["L0_ffn_G2_I"]),
        "fbias": np.concatenate([f32(inputs["L0_ffn_bias_real"]),
                                 f32(inputs["L0_ffn_bias_imag"])]),
    }
    base.update(consts)
    nc = build_nc()
    in_maps = []
    for b in range(8):
        m = dict(base)
        m["xT"] = np.ascontiguousarray(x[b].T)
        in_maps.append(m)
    res = run_bass_kernel_spmd(nc, in_maps, core_ids=list(range(8)))
    out = np.stack([np.asarray(res.results[b]["outT"]).T for b in range(8)])
    return np.ascontiguousarray(out, dtype=np.float32)


# version 21
# speedup vs baseline: 1.0291x; 1.0031x over previous
import numpy as np
import ml_dtypes

import concourse.bass as bass
import concourse.mybir as mybir
from concourse.bass_utils import run_bass_kernel_spmd

F32 = mybir.dt.float32
BF16 = mybir.dt.bfloat16
BF = ml_dtypes.bfloat16
AF = mybir.ActivationFunctionType
OP = mybir.AluOpType
AX = mybir.AxisListType

D = 1024
T = 2048
DEBUG_TAPS = False
PHASE_LIM = 99
OLD_RECIPS = True


class _Stop(Exception):
    pass


def _host_consts():
    d = np.arange(D)
    f = np.arange(512)
    ang = 2 * np.pi * np.outer(d, f) / D
    CS = np.zeros((D, 1026), np.float32)
    CS[:, 0:512] = np.cos(ang)
    CS[:, 512:1024] = -np.sin(ang)
    CS[:, 1024] = np.cos(np.pi * d)
    IDFT = np.zeros((1026, D), np.float32)
    coef = np.where(f == 0, 1.0, 2.0) / D
    IDFT[0:512] = coef[:, None] * np.cos(ang.T)
    IDFT[512:1024] = (-2.0 / D) * np.sin(ang.T)
    IDFT[1024] = np.cos(np.pi * d) / D
    be = np.arange(512)
    RfT = np.zeros((32, 512), np.float32); RfT[be // 16, be] = 1
    RqfT = np.zeros((16, 512), np.float32); RqfT[be % 16, be] = 1
    RpT = np.zeros((32, 512), np.float32)
    RpT[(be // 128) * 8 + (be % 128) // 16, be] = 1
    p = np.arange(128)
    RqpT = np.zeros((16, 128), np.float32); RqpT[p % 16, p] = 1
    return {
        "CONST_CS": CS.astype(BF),
        "CONST_IDFT": IDFT.astype(np.float32),
        "CONST_RfT": RfT.astype(BF),
        "CONST_RqfT": RqfT.astype(BF),
        "CONST_RpT": RpT.astype(BF),
        "CONST_RqpT": RqpT.astype(BF),
    }


def build_nc():
    from contextlib import ExitStack
    nc = bass.Bass("TRN2", target_bir_lowering=False, detect_race_conditions=False)

    def din(name, shape, dt=F32):
        return nc.dram_tensor(name, shape, dt, kind="ExternalInput").ap()

    xT = din("xT", [D, T])
    Wqkv = din("Wqkv", [D, 3 * D])
    bqkv = din("bqkv", [3 * D])
    Wout = din("Wout", [D, D])
    bout = din("bout", [D])
    Wq_b = din("Wq_b", [D, D])
    bq_b = din("bq_b", [D])
    WkT = din("WkT", [D, D])
    bk_b = din("bk_b", [D])
    WvT = din("WvT", [D, D])
    bv_b = din("bv_b", [D])
    G1R = din("G1R", [1, 32, 4, 32])
    G2R = din("G2R", [4, 16, 1, 16])
    G1I = din("G1I", [1, 32, 4, 32])
    G2I = din("G2I", [4, 16, 1, 16])
    fbias = din("fbias", [D])
    CS_h = din("CONST_CS", [D, 1026], BF16)
    IDFT_h = din("CONST_IDFT", [1026, D])
    RfT_h = din("CONST_RfT", [32, 512], BF16)
    RqfT_h = din("CONST_RqfT", [16, 512], BF16)
    RpT_h = din("CONST_RpT", [32, 512], BF16)
    RqpT_h = din("CONST_RqpT", [16, 128], BF16)

    outT = nc.dram_tensor("outT", [D, T], F32, kind="ExternalOutput").ap()
    s2h = nc.dram_tensor("s2h", [2 * D], BF16, kind="Internal").ap()
    wsth = nc.dram_tensor("wsth", [12 * 128, 512], BF16, kind="Internal").ap()

    import kernel as _self

    with ExitStack() as ctx:
        def sbuf(name, shape, dt):
            return ctx.enter_context(nc.sbuf_tensor(name, shape, dt))

        X = [sbuf(f"X{i}", [128, T], F32) for i in range(8)]
        A = [sbuf(f"Ax{i}", [128, T], BF16) for i in range(8)]
        B = [sbuf(f"Bx{i}", [128, T], BF16) for i in range(8)]
        C = [sbuf(f"Cx{i}", [128, T], BF16) for i in range(8)]
        Dv = [sbuf(f"Dx{i}", [128, 1040], BF16) for i in range(16)]
        SBF0 = sbuf("SBF0", [128, T], BF16)
        PT = [sbuf("PT0", [128, 1024], BF16), sbuf("PT1", [128, 1024], BF16)]
        SF0 = sbuf("SF0", [128, 1024], F32)
        WVH = PT
        ones_bf = sbuf("ones_bf", [128, 128], BF16)
        ones_f = sbuf("ones_f", [128, 128], F32)
        ones1f = sbuf("ones1f", [1, 64], F32)
        colC = sbuf("colC", [128, 72], F32)
        colB = sbuf("colB", [128, 16], BF16)
        negc = sbuf("negc", [128, 8], F32)
        stq = sbuf("stq", [128, 8], F32)
        Shat = sbuf("Shat", [128, 48], F32)
        RED = sbuf("RED", [128, 68], F32)
        WNY = sbuf("WNY", [128, 16], BF16)
        recips = SF0
        s_bf = SBF0
        bvrow = SBF0
        onesr_bf = ones_bf
        RfT_s = B[0]
        RqfT_s = B[0]
        RpT_s = B[0]
        RqpT_s = B[0]
        g1s = B[2]
        g2s = B[2]
        T1 = B[1]
        T1q = B[3]
        brep = B[2]

        ps0 = ctx.enter_context(nc.psum_tensor("ps0", [128, T], F32))
        ps1 = ctx.enter_context(nc.psum_tensor("ps1", [128, T], F32))
        PS = [ps0, ps1]

        semn = ["dS", "dG", "dSc", "sT", "sC", "sV", "sS", "sPV", "sE", "sRaw",
                "sDR", "sNm", "sStab", "sWc", "sST", "sPS0", "sPS1", "sRc"]
        sems = {n: ctx.enter_context(nc.semaphore(n)) for n in semn}
        cnt = {n: 0 for n in semn}

        blk_real = ctx.enter_context(nc.Block())

        class _BlkProxy:
            stop = False
            def __getattr__(self, eng):
                if self.stop:
                    return lambda f: None
                return getattr(blk_real, eng)
        blk = _BlkProxy()

        def ck(n):
            if _self.PHASE_LIM <= n:
                blk.__class__.stop = True

        def inc(ins_, name, amt=1):
            cnt[name] += amt
            ins_.then_inc(sems[name], amt)
            return cnt[name]

        def wt(e, name, val):
            if val > 0:
                e.wait_ge(sems[name], val)

        psw = [0, 0]

        def ps_acq(e, a):
            if psw[a] > 0:
                e.wait_ge(sems["sPS%d" % a], psw[a])
            psw[a] += 1

        def ps_rel(ins_, a):
            inc(ins_, "sPS%d" % a)

        tap_seen = {}
        def tap(name, aps, waits):
            if not DEBUG_TAPS:
                return
            if name in tap_seen:
                name = name + str(tap_seen[name])
            tap_seen[name] = tap_seen.get(name, 0) + 1
            rows = aps[0].shape[0]
            cols = aps[0].shape[1]
            dt_ = aps[0].dtype
            h = nc.dram_tensor(name, [len(aps), rows, cols], dt_,
                               kind="ExternalOutput").ap()
            @blk.sync
            def _(sy):
                for (nm, val) in waits:
                    wt(sy, nm, val)
                for i, ap_ in enumerate(aps):
                    inc(sy.dma_start(out=h[i], in_=ap_), "dS", 16)

        @blk.scalar
        def _(s):
            for i in (1, 3, 5, 7):
                inc(s.dma_start(out=X[i][:, :],
                                in_=xT[i * 128:(i + 1) * 128, :]), "dSc", 16)

        @blk.sync
        def _(sy):
            for i in (0, 2, 4, 6):
                inc(sy.dma_start(out=X[i][:, :],
                                 in_=xT[i * 128:(i + 1) * 128, :]), "dS", 16)
            for c in range(8):
                inc(sy.dma_start(out=colC[:, c:c + 1],
                                 in_=bqkv[c * 128:(c + 1) * 128][:, None]), "dS", 16)
            for c in range(8):
                inc(sy.dma_start(out=colC[:, 8 + c:9 + c],
                                 in_=bqkv[1024 + c * 128:1024 + (c + 1) * 128][:, None]), "dS", 16)
            for c in range(8):
                inc(sy.dma_start(out=colC[:, 16 + c:17 + c],
                                 in_=bout[c * 128:(c + 1) * 128][:, None]), "dS", 16)
            for c in range(8):
                inc(sy.dma_start(out=colC[:, 24 + c:25 + c],
                                 in_=fbias[c * 128:(c + 1) * 128][:, None]), "dS", 16)
            for c in range(8):
                inc(sy.dma_start(out=colC[:, 32 + c:33 + c],
                                 in_=bq_b[c * 128:(c + 1) * 128][:, None]), "dS", 16)
        dS_x = [("dS" if i % 2 == 0 else "dSc", 16 * (i // 2 + 1))
                for i in range(8)]
        dS_prolog = cnt["dS"]

        @blk.gpsimd
        def _(g):
            for c in range(8):
                inc(g.dma_start(out=colB[:, c:c + 1],
                                in_=bk_b[c * 128:(c + 1) * 128][:, None]), "dG", 16)
            for c in range(8):
                inc(g.dma_start(out=colB[:, 8 + c:9 + c],
                                in_=bv_b[c * 128:(c + 1) * 128][:, None]), "dG", 16)
        dG_prolog = cnt["dG"]

        @blk.vector
        def _(v):
            ins_ = v.memset(ones_bf[:, :], 1.0)
            ins_ = v.memset(ones_f[:, :], 1.0)
            ins_ = v.memset(ones1f[:, :], 1.0)
            ins_ = v.memset(colC[:, :], 0.0)
            ins_ = v.memset(stq[:, :], 0.0)
            ins_ = v.memset(Shat[:, :], 0.0)
            ins_ = v.memset(negc[:, :], 0.0)
            for t in range(16):
                dview = Dv[t][:, :].rearrange("p (h f) -> p h f", f=65)
                ins_ = v.memset(dview[:, :, 64:65], 1.0)
            inc(ins_, "sV")
        ones_ready = cnt["sV"]

        def emit_ln(src, dst, chunk_ready, dst_free=None, db=None):
            sq_done = []
            st_done = []
            for i in range(8):
                use_pt = db is not None and i % 2 == 1
                @blk.vector
                def _(v, i=i, use_pt=use_pt):
                    for (nm, val) in chunk_ready[i]:
                        wt(v, nm, val)
                    if i == 1 and db is not None:
                        wt(v, db[0], db[1])
                    lim = 2 if db is not None else 1
                    if i >= lim:
                        wt(v, "sT", st_done[i - lim])
                    if use_pt:
                        ins_ = v.tensor_tensor(PT[0][:, :], src[i][:, 0:1024],
                                               src[i][:, 0:1024], OP.mult)
                        inc(v.tensor_tensor(PT[1][:, :], src[i][:, 1024:2048],
                                            src[i][:, 1024:2048], OP.mult), "sV")
                    else:
                        inc(v.tensor_tensor(SBF0[:, :], src[i][:, :], src[i][:, :],
                                            OP.mult), "sV")
                sq_done.append(cnt["sV"])

                @blk.tensor
                def _(t, i=i, use_pt=use_pt):
                    if i == 0:
                        wt(t, "sV", ones_ready)
                        ps_acq(t, 0)
                        ps_acq(t, 1)
                    wt(t, "sV", sq_done[i])
                    st = (i == 0); sp = (i == 7)
                    for h in range(2):
                        for p in range(2):
                            ins_ = t.matmul(
                                PS[h][:, p * 512:(p + 1) * 512], ones_f[:, :],
                                src[i][:, h * 1024 + p * 512:h * 1024 + (p + 1) * 512],
                                start=st, stop=sp)
                    for h in range(2):
                        for p in range(2):
                            sqs = (PT[h][:, p * 512:(p + 1) * 512] if use_pt else
                                   SBF0[:, h * 1024 + p * 512:h * 1024 + (p + 1) * 512])
                            ins_ = t.matmul(
                                PS[h][:, 1024 + p * 512:1024 + (p + 1) * 512],
                                ones_bf[:, :], sqs,
                                start=st, stop=sp)
                    inc(ins_, "sT")
                st_done.append(cnt["sT"])

            qmarks = []
            for h in range(2):
                for q in range(2):
                    @blk.vector
                    def _(v, h=h, q=q):
                        if h == 0 and q == 0:
                            wt(v, "sT", st_done[7])
                        pa = PS[h]
                        ins_ = v.tensor_scalar(SF0[:, 0:512],
                                               pa[:, 512 * q:512 * q + 512],
                                               1.0 / 1024, None, OP.mult)
                        ins_ = v.tensor_scalar(SF0[:, 512:1024],
                                               pa[:, 1024 + 512 * q:1536 + 512 * q],
                                               1.0 / 1024, 1e-5, OP.mult, OP.add)
                        ins_ = v.tensor_tensor(pa[:, 512 * q:512 * q + 512],
                                               SF0[:, 0:512], SF0[:, 0:512], OP.mult)
                        ins_ = v.tensor_tensor(SF0[:, 512:1024], SF0[:, 512:1024],
                                               pa[:, 512 * q:512 * q + 512],
                                               OP.subtract)
                        inc(v.memset(stq[:, 6:7], 0.0), "sV")
                        if q == 1:
                            ps_rel(ins_, h)
                    rec_done = cnt["sV"]

                    @blk.scalar
                    def _(s, h=h, q=q, rec_done=rec_done):
                        wt(s, "sV", rec_done)
                        ins_ = s.activation(SF0[:, 512:1024],
                                            SF0[:, 512:1024], AF.Ln)
                        ins_ = s.activation(SF0[:, 512:1024],
                                            SF0[:, 512:1024],
                                            AF.Exp, scale=-0.5)
                        ins_ = s.activation(SBF0[:, 512:1024],
                                            SF0[:, 512:1024], AF.Identity)
                        inc(s.activation(stq[:, 7:8], stq[:, 6:7], AF.Identity),
                            "sC")
                    sqrt_done = cnt["sC"]

                    @blk.vector
                    def _(v, h=h, q=q, sqrt_done=sqrt_done):
                        wt(v, "sC", sqrt_done)
                        if h == 0 and q == 0 and dst_free is not None:
                            wt(v, dst_free[0], dst_free[1])
                        c0 = h * 1024 + 512 * q
                        for i in range(8):
                            ins_ = v.tensor_tensor(SBF0[:, 0:512],
                                                   src[i][:, c0:c0 + 512],
                                                   SF0[:, 0:512], OP.subtract)
                            ins_ = v.tensor_tensor(dst[i][:, c0:c0 + 512],
                                                   SBF0[:, 0:512],
                                                   SBF0[:, 512:1024], OP.mult)
                        inc(v.memset(stq[:, 7:8], 0.0), "sV")
                    qmarks.append(cnt["sV"])
            return cnt["sV"], qmarks

        xn1_ready, xn1_q = emit_ln(X, A, [[dS_x[i]] for i in range(8)])
        ln1_done_T = cnt["sT"]
        ck(1)
        tap("t_xn1", [a[:, :] for a in A], [("sV", xn1_ready)])

        @blk.gpsimd
        def _(g):
            for j in range(8):
                inc(g.dma_start(out=C[j][:, 0:1024],
                                in_=Wqkv[j * 128:(j + 1) * 128, 2048:3072]),
                    "dG", 16)

        @blk.gpsimd
        def _(g):
            wt(g, "sT", ln1_done_T)
            inc(g.dma_start(out=bvrow[0:1, 1024:2048],
                            in_=bqkv[2 * 1024:3 * 1024][None, :]), "dG", 16)
        dg_v = cnt["dG"]

        vt_done = []
        @blk.tensor
        def _(t):
            wt(t, "sV", ones_ready)
            wt(t, "dG", dG_prolog)
            wt(t, "dG", dg_v)
            for tt_ in range(16):
                wt(t, "sV", xn1_q[tt_ // 4])
                a = tt_ % 2
                ps_acq(t, a)
                for j in range(8):
                    for pc in range(2):
                        ins_ = t.matmul(PS[a][:, pc * 512:(pc + 1) * 512],
                                        A[j][:, tt_ * 128:(tt_ + 1) * 128],
                                        C[j][:, pc * 512:(pc + 1) * 512],
                                        start=(j == 0), stop=False)
                for pc in range(2):
                    ins_ = t.matmul(PS[a][:, pc * 512:(pc + 1) * 512],
                                    onesr_bf[0:1, 0:128],
                                    bvrow[0:1, 1024 + pc * 512:1024 + (pc + 1) * 512],
                                    start=False, stop=True)
                inc(ins_, "sT")
                vt_done.append(cnt["sT"])

        for tt_ in range(16):
            @blk.scalar
            def _(s, tt_=tt_):
                wt(s, "sT", vt_done[tt_])
                a = tt_ % 2
                pv = PS[a][:, 0:1024].rearrange("p (h e) -> p h e", e=64)
                dvw = Dv[tt_][:, :].rearrange("p (h f) -> p h f", f=65)[:, :, 0:64]
                ins_ = s.activation(dvw, pv, AF.Identity)
                ps_rel(ins_, a)
        v_done_T = cnt["sT"]

        qk_cons = []
        stab_ready = []
        for hp in range(8):
            for which in range(2):
                n = 2 * hp + which
                @blk.gpsimd
                def _(g, n=n, hp=hp, which=which):
                    if n >= 2:
                        wt(g, "sWc", qk_cons[n - 2])
                    dc = hp if which == 0 else 8 + hp
                    col0 = dc * 128 if dc < 8 else 1024 + (dc - 8) * 128
                    inc(g.dma_start(
                        out=PT[n % 2][:, :].rearrange("p (c m) -> p c m", m=128),
                        in_=Wqkv[:, col0:col0 + 128].rearrange("(c p) m -> p c m", p=128)),
                        "dG", 16)
                dg_n = cnt["dG"]

                @blk.tensor
                def _(t, n=n, dg_n=dg_n):
                    wt(t, "dG", dg_n)
                    a = n % 2
                    ps_acq(t, a)
                    for j in range(8):
                        for pc in range(4):
                            ins_ = t.matmul(PS[a][:, pc * 512:(pc + 1) * 512],
                                            PT[n % 2][:, j * 128:(j + 1) * 128],
                                            A[j][:, pc * 512:(pc + 1) * 512],
                                            start=(j == 0), stop=(j == 7))
                    inc(ins_, "sWc")
                qk_cons.append(cnt["sWc"])
                mm_done = cnt["sWc"]

                @blk.scalar
                def _(s, n=n, hp=hp, which=which, mm_done=mm_done):
                    if n == 0:
                        wt(s, "dS", dS_prolog)
                    wt(s, "sWc", mm_done)
                    a = n % 2
                    dst = B[hp] if which == 0 else C[hp]
                    bias_col = hp if which == 0 else 8 + hp
                    ins_ = s.activation(dst[:, :], PS[a][:, :], AF.Identity,
                                        bias=colC[:, bias_col:bias_col + 1])
                    ps_rel(ins_, a)
                    inc(s.activation(stq[:, 7:8], stq[:, 6:7], AF.Identity), "sC")
            qk_evac_done = cnt["sC"]

            @blk.vector
            def _(v, hp=hp, qk_evac_done=qk_evac_done):
                wt(v, "sC", qk_evac_done)
                inc(v.tensor_tensor(SBF0[:, :], B[hp][:, :], B[hp][:, :], OP.mult),
                    "sV")
            sqq = cnt["sV"]

            @blk.tensor
            def _(t, sqq=sqq):
                wt(t, "sV", sqq)
                ps_acq(t, 0)
                for pc in range(4):
                    ins_ = t.matmul(PS[0][:, pc * 512:(pc + 1) * 512], ones_bf[:, :],
                                    SBF0[:, pc * 512:(pc + 1) * 512],
                                    start=True, stop=True)
                inc(ins_, "sT")
            mmq = cnt["sT"]

            @blk.vector
            def _(v, hp=hp, mmq=mmq):
                wt(v, "sT", mmq)
                inc(v.tensor_tensor(SBF0[:, :], C[hp][:, :], C[hp][:, :], OP.mult),
                    "sV")
            sqk = cnt["sV"]

            @blk.tensor
            def _(t, sqk=sqk):
                wt(t, "sV", sqk)
                ps_acq(t, 1)
                for pc in range(4):
                    ins_ = t.matmul(PS[1][:, pc * 512:(pc + 1) * 512], ones_bf[:, :],
                                    SBF0[:, pc * 512:(pc + 1) * 512],
                                    start=True, stop=True)
                inc(ins_, "sT")
            mmk = cnt["sT"]

            @blk.vector
            def _(v, hp=hp, mmq=mmq, mmk=mmk):
                wt(v, "sT", mmq)
                ins_ = v.reduce_max(stq[:, 0:1], PS[0][:, :], axis=AX.X)
                ps_rel(ins_, 0)
                wt(v, "sT", mmk)
                ins_ = v.reduce_max(stq[:, 1:2], PS[1][:, :], axis=AX.X)
                ps_rel(ins_, 1)
                inc(v.memset(stq[:, 6:7], 0.0), "sV")
            mul_done = cnt["sV"]

            @blk.scalar
            def _(s, hp=hp, mul_done=mul_done):
                wt(s, "sV", mul_done)
                ins_ = s.activation(stq[:, 4:5], stq[:, 0:1], AF.Ln)
                ins_ = s.activation(stq[:, 5:6], stq[:, 1:2], AF.Ln)
                ins_ = s.activation(stq[:, 2:3], stq[:, 4:5], AF.Identity,
                                    bias=stq[:, 5:6])
                ins_ = s.activation(stq[:, 3:4], stq[:, 2:3], AF.Exp, scale=0.5)
                inc(s.activation(stq[:, 7:8], stq[:, 6:7], AF.Identity), "sC")
            sqrt_done = cnt["sC"]

            @blk.vector
            def _(v, hp=hp, sqrt_done=sqrt_done):
                wt(v, "sC", sqrt_done)
                inc(v.tensor_scalar_mul(negc[:, hp:hp + 1], stq[:, 3:4], -0.125),
                    "sStab")
            stab_ready.append(cnt["sStab"])

        wc_after_qk = cnt["sWc"]
        tap("t_q", [b_[:, :] for b_ in B], [("sStab", stab_ready[7])])
        tap("t_k", [c_[:, :] for c_ in C], [("sStab", stab_ready[7])])
        tap("t_v", [d_[:, :] for d_ in Dv], [("sStab", stab_ready[7]),
                                             ("sPS0", cnt["sPS0"]),
                                             ("sPS1", cnt["sPS1"])])
        tap("t_negc", [negc[:, :]], [("sStab", stab_ready[7])])

        ck(2)
        sS0 = cnt["sS"]; sPV0 = cnt["sPV"]; sE0 = cnt["sE"]
        sRaw0 = cnt["sRaw"]; sDR0 = cnt["sDR"]; sNm0 = cnt["sNm"]

        @blk.tensor
        def _(t):
            ps_acq(t, 0)
            ps_acq(t, 1)

            def emit_pv(m):
                gp, kk = divmod(m, 16)
                hpp = gp % 8
                pp = gp % 2
                wt(t, "sE", sE0 + m + 1)
                if kk == 0:
                    wt(t, "sNm", sNm0 + gp - 1)
                ins_ = t.matmul(ps1[0:65, pp * 1024:pp * 1024 + 512],
                                Dv[kk][:, 130 * hpp:130 * hpp + 65],
                                PT[kk % 2][:, 0:512],
                                start=(kk == 0), stop=(kk == 15))
                ins_ = t.matmul(ps1[0:65, pp * 1024 + 512:pp * 1024 + 1024],
                                Dv[kk][:, 130 * hpp + 65:130 * hpp + 130],
                                PT[kk % 2][:, 512:1024],
                                start=(kk == 0), stop=(kk == 15))
                inc(ins_, "sPV")

            def emit_denrep(gp):
                pp = gp % 2
                wt(t, "sRaw", sRaw0 + gp + 1)
                ins_ = t.matmul(ps1[0:64, pp * 1024:pp * 1024 + 512],
                                ones_bf[64:65, 0:64],
                                SBF0[64:65, pp * 1024:pp * 1024 + 512],
                                start=True, stop=True)
                ins_ = t.matmul(ps1[0:64, pp * 1024 + 512:pp * 1024 + 1024],
                                ones_bf[64:65, 0:64],
                                SBF0[64:65, pp * 1024 + 512:pp * 1024 + 1024],
                                start=True, stop=True)
                inc(ins_, "sDR")

            for n in range(512):
                g, kt = divmod(n, 16)
                st_, hp = g // 8, g % 8
                wt(t, "sE", sE0 + n - 1)
                slot = kt % 2
                ins_ = t.matmul(ps0[:, slot * 1024:slot * 1024 + 512],
                                C[hp][0:64, kt * 128:(kt + 1) * 128],
                                B[hp][0:64, st_ * 512:(st_ + 1) * 512],
                                start=True, stop=True, tile_position=(0, 0))
                ins_ = t.matmul(ps0[:, slot * 1024 + 512:slot * 1024 + 1024],
                                C[hp][64:128, kt * 128:(kt + 1) * 128],
                                B[hp][64:128, st_ * 512:(st_ + 1) * 512],
                                start=True, stop=True, tile_position=(64, 0))
                inc(ins_, "sS")
                if kt == 13 and g >= 1:
                    emit_denrep(g - 1)
                if n >= 2:
                    emit_pv(n - 2)
            emit_pv(510)
            emit_pv(511)
            emit_denrep(31)

        @blk.scalar
        def _(s):
            wt(s, "sWc", wc_after_qk)
            for g in range(32):
                st_, hp = g // 8, g % 8
                for kt in range(16):
                    wt(s, "sS", sS0 + 16 * g + kt + 1)
                    wt(s, "sPV", sPV0 + 16 * g + kt - 1)
                    if g < 8 and kt == 0:
                        wt(s, "sStab", stab_ready[hp])
                    slot = kt % 2
                    ins_ = s.activation(PT[slot][:, :],
                                        ps0[:, slot * 1024:slot * 1024 + 1024],
                                        AF.Exp, bias=negc[:, hp:hp + 1], scale=0.125)
                    inc(ins_, "sE")

        @blk.vector
        def _(v):
            for g in range(32):
                p2 = g % 2
                if g >= 1:
                    gp = g - 1
                    stp, hpp = gp // 8, gp % 8
                    pp = gp % 2
                    wt(v, "sDR", sDR0 + gp + 1)
                    ins_ = v.tensor_tensor(A[hpp][0:64, stp * 512:(stp + 1) * 512],
                                           SBF0[0:64, pp * 1024:pp * 1024 + 512],
                                           ps1[0:64, pp * 1024:pp * 1024 + 512],
                                           OP.mult)
                    ins_ = v.tensor_tensor(A[hpp][64:128, stp * 512:(stp + 1) * 512],
                                           SBF0[0:64, pp * 1024 + 512:pp * 1024 + 1024],
                                           ps1[0:64, pp * 1024 + 512:pp * 1024 + 1024],
                                           OP.mult)
                    inc(ins_, "sNm")
                wt(v, "sPV", sPV0 + 16 * g + 16)
                ins_ = v.tensor_copy(SBF0[0:64, p2 * 1024:p2 * 1024 + 512],
                                     ps1[0:64, p2 * 1024:p2 * 1024 + 512])
                ins_ = v.tensor_copy(SBF0[0:64, p2 * 1024 + 512:p2 * 1024 + 1024],
                                     ps1[0:64, p2 * 1024 + 512:p2 * 1024 + 1024])
                with nc.allow_low_precision(reason="softmax denominator recip "
                                            "stored bf16 as matmul operand"):
                    ins_ = v.reciprocal(SBF0[64:65, p2 * 1024:p2 * 1024 + 512],
                                        ps1[64:65, p2 * 1024:p2 * 1024 + 512])
                    ins_ = v.reciprocal(SBF0[64:65, p2 * 1024 + 512:p2 * 1024 + 1024],
                                        ps1[64:65, p2 * 1024 + 512:p2 * 1024 + 1024])
                inc(v.memset(stq[:, 6:7], 0.0), "sRaw")
            gp = 31
            stp, hpp = gp // 8, gp % 8
            pp = gp % 2
            wt(v, "sDR", sDR0 + 32)
            ins_ = v.tensor_tensor(A[hpp][0:64, stp * 512:(stp + 1) * 512],
                                   SBF0[0:64, pp * 1024:pp * 1024 + 512],
                                   ps1[0:64, pp * 1024:pp * 1024 + 512], OP.mult)
            ins_ = v.tensor_tensor(A[hpp][64:128, stp * 512:(stp + 1) * 512],
                                   SBF0[0:64, pp * 1024 + 512:pp * 1024 + 1024],
                                   ps1[0:64, pp * 1024 + 512:pp * 1024 + 1024],
                                   OP.mult)
            inc(ins_, "sNm")
            ps_rel(v.memset(stq[:, 2:3], 0.0), 0)
            ps_rel(v.memset(stq[:, 2:3], 0.0), 1)
        ck(3)
        attn_done = cnt["sNm"]
        tap("t_attn", [a[:, :] for a in A], [("sNm", attn_done)])

        def emit_proj_residual(lhs_fn, rhs, bias_base, w_stream, first_waits,
                               rhs_qmarks=None):
            done = []
            wc_marks = []
            for i in range(8):
                if w_stream is not None:
                    @blk.gpsimd
                    def _(g, i=i):
                        if i == 0:
                            for (nm, val) in first_waits:
                                wt(g, nm, val)
                        if i >= 2:
                            wt(g, "sWc", wc_marks[i - 2])
                        inc(g.dma_start(
                            out=PT[i % 2][:, :].rearrange("p (c m) -> p c m", m=128),
                            in_=w_stream(i).rearrange("(c p) m -> p c m", p=128)),
                            "dG", 16)
                    dg_i = cnt["dG"]
                else:
                    dg_i = 0

                @blk.tensor
                def _(t, i=i, dg_i=dg_i):
                    if i == 0:
                        for (nm, val) in first_waits:
                            wt(t, nm, val)
                    if w_stream is not None:
                        wt(t, "dG", dg_i)
                    a = i % 2
                    ps_acq(t, a)
                    for pc in range(4):
                        if rhs_qmarks is not None and i < 2:
                            wt(t, "sV", rhs_qmarks[pc])
                        for j in range(8):
                            ins_ = t.matmul(PS[a][:, pc * 512:(pc + 1) * 512],
                                            lhs_fn(i, j),
                                            rhs[j][:, pc * 512:(pc + 1) * 512],
                                            start=(j == 0), stop=(j == 7))
                    inc(ins_, "sWc")
                wc_marks.append(cnt["sWc"])
                mm_i = cnt["sWc"]

                @blk.vector
                def _(v, i=i, mm_i=mm_i):
                    wt(v, "sWc", mm_i)
                    a = i % 2
                    if bias_base is not None:
                        ins_ = v.scalar_tensor_tensor(
                            X[i][:, :], PS[a][:, :],
                            colC[:, bias_base + i:bias_base + i + 1],
                            X[i][:, :], OP.add, OP.add)
                        ins_ = v.memset(stq[:, 6:7], 0.0)
                    else:
                        ins_ = v.tensor_tensor(X[i][:, :], PS[a][:, :],
                                               X[i][:, :], OP.add)
                    ps_rel(ins_, a)
                done.append(("sPS%d" % (i % 2), cnt["sPS%d" % (i % 2)]))
            return done, ("sWc", wc_marks[-1])

        xdone, pj_mark = emit_proj_residual(
            lhs_fn=lambda i, j: PT[i % 2][:, j * 128:(j + 1) * 128],
            rhs=A, bias_base=16,
            w_stream=lambda i: Wout[:, i * 128:(i + 1) * 128],
            first_waits=[("sNm", attn_done), ("sPV", sPV0 + 512)])

        ck(4)
        tap("t_x4", [x[:, :] for x in X], xdone)
        @blk.sync
        def _(sy):
            wt(sy, "sNm", attn_done)
            inc(sy.dma_start(out=B[0][0:32, 0:512], in_=RfT_h), "dS", 16)
            inc(sy.dma_start(out=B[0][0:16, 512:1024], in_=RqfT_h), "dS", 16)
            inc(sy.dma_start(out=B[2][0:32, 0:512], in_=RpT_h), "dS", 16)
            inc(sy.dma_start(out=B[2][0:16, 896:1024], in_=RqpT_h), "dS", 16)
        dS_Rm = cnt["dS"]

        @blk.gpsimd
        def _(g):
            wt(g, "sNm", attn_done)
            for rr, G1h, G2h in ((0, G1R, G2R), (1, G1I, G2I)):
                for c in range(4):
                    k = 4 * rr + c
                    inc(g.dma_start(out=B[2][0:32, 512 + 32 * k:544 + 32 * k],
                                    in_=G1h[0, :, c, :]), "dG", 16)
                    inc(g.dma_start(out=B[2][0:16, 768 + 16 * k:784 + 16 * k],
                                    in_=G2h[c, :, 0, :]), "dG", 16)
        dg_g = cnt["dG"]

        def t1ap(c):
            return (B[1] if c < 2 else B[3])[0:32, (c % 2) * 512:(c % 2) * 512 + 512]

        def t1qap(c):
            return (B[5] if c < 2 else B[7])[0:16, (c % 2) * 512:(c % 2) * 512 + 512]

        for rr in range(2):
            for c in range(4):
                @blk.tensor
                def _(t, rr=rr, c=c):
                    if rr == 0 and c == 0:
                        wt(t, "dS", dS_Rm)
                        wt(t, "dG", dg_g)
                    k = 4 * rr + c
                    ps_acq(t, 0)
                    ins_ = t.matmul(ps0[0:32, 0:512],
                                    B[2][0:32, 512 + 32 * k:544 + 32 * k],
                                    B[0][0:32, 0:512], start=True, stop=True)
                    ins_ = t.matmul(ps0[0:16, 512:1024],
                                    B[2][0:16, 768 + 16 * k:784 + 16 * k],
                                    B[0][0:16, 512:1024], start=True, stop=True)
                    inc(ins_, "sWc")
                t1_done = cnt["sWc"]

                @blk.vector
                def _(v, c=c, t1_done=t1_done):
                    wt(v, "sWc", t1_done)
                    ins_ = v.tensor_copy(t1ap(c), ps0[0:32, 0:512])
                    ins_ = v.tensor_copy(t1qap(c), ps0[0:16, 512:1024])
                    ps_rel(ins_, 0)
            t1_all = cnt["sPS0"]

            for ic in range(4):
                for c in range(4):
                    a2 = (4 * ic + c) % 2
                    @blk.tensor
                    def _(t, ic=ic, c=c, a2=a2, t1_all=t1_all):
                        if c == 0:
                            wt(t, "sPS0", t1_all)
                        ps_acq(t, a2)
                        ins_ = t.matmul(PS[a2][:, 0:512],
                                        B[2][0:32, ic * 128:(ic + 1) * 128],
                                        t1ap(c), start=True, stop=True)
                        ins_ = t.matmul(PS[a2][:, 512:1024], B[2][0:16, 896:1024],
                                        t1qap(c), start=True, stop=True)
                        inc(ins_, "sT")
                    ab_done = cnt["sT"]

                    @blk.vector
                    def _(v, rr=rr, ic=ic, c=c, a2=a2, ab_done=ab_done):
                        wt(v, "sT", ab_done)
                        ins_ = v.tensor_copy(B[4][:, 0:512], PS[a2][:, 512:1024])
                        if c == 0:
                            ins_ = v.tensor_tensor(B[6][:, 0:512], PS[a2][:, 0:512],
                                                   B[4][:, 0:512], OP.mult)
                        else:
                            ins_ = v.tensor_tensor(B[6][:, 512:1024],
                                                   PS[a2][:, 0:512],
                                                   B[4][:, 0:512], OP.mult)
                            dst = (Dv[4 * rr + ic][:, 0:512] if c == 3 else
                                   B[6][:, 0:512])
                            ins_ = v.tensor_tensor(dst, B[6][:, 0:512],
                                                   B[6][:, 512:1024], OP.add)
                        ps_rel(ins_, a2)
                        if c == 3:
                            inc(v.memset(stq[:, 7:8], 0.0), "sV")
                acc_done = cnt["sV"]

                @blk.scalar
                def _(s, rr=rr, ic=ic, acc_done=acc_done):
                    wt(s, "sV", acc_done)
                    inc(s.activation(stq[:, 7:8], stq[:, 6:7], AF.Identity), "sC")
        wt_cast = cnt["sC"]

        @blk.vector
        def _(v, wt_cast=wt_cast):
            wt(v, "sC", wt_cast)
            for ic in range(4):
                ins_ = v.tensor_scalar_mul(Dv[8 + ic][:, 0:512],
                                           Dv[4 + ic][:, 0:512], -1.0)
            inc(ins_, "sV")
        ffn1_wbuilt = cnt["sV"]
        ck(41)

        def ffn_lhs(i, j):
            if j < 4:
                if i < 4:
                    return Dv[j][:, i * 128:(i + 1) * 128]
                return Dv[4 + j][:, (i - 4) * 128:(i - 3) * 128]
            if i < 4:
                return Dv[8 + (j - 4)][:, i * 128:(i + 1) * 128]
            return Dv[j - 4][:, (i - 4) * 128:(i - 3) * 128]

        tap("t_ffnw", [Dv[i][:, 0:512] for i in range(12)],
            [("sV", ffn1_wbuilt)])

        @blk.sync
        def _(sy):
            wt(sy, "sNm", attn_done)
            for j in range(8):
                inc(sy.dma_start(out=C[j][:, 0:1026],
                                 in_=CS_h[j * 128:(j + 1) * 128, :]), "dS", 16)
        dS_cs = cnt["dS"]

        @blk.gpsimd
        def _(g):
            wt(g, "sNm", attn_done)
            for j in range(8):
                inc(g.dma_start(out=B[j][:, 1024:2048],
                                in_=WkT[j * 128:(j + 1) * 128, :]), "dG", 16)
        dg_kstage = cnt["dG"]

        xn2_ready, xn2_q = emit_ln(X, A, [[xdone[i]] for i in range(8)], dst_free=pj_mark,
                            db=pj_mark)

        ck(5)
        fold_marks = {}
        ev_marks = []
        for i in range(8):
            @blk.tensor
            def _(t, i=i):
                if i == 0:
                    wt(t, "dG", dg_kstage)
                    wt(t, "dS", dS_cs)
                a = i % 2
                ps_acq(t, a)
                for j in range(8):
                    for pc in range(2):
                        ins_ = t.matmul(PS[a][:, pc * 512:(pc + 1) * 512],
                                        B[j][:, 1024 + i * 128:1024 + (i + 1) * 128],
                                        C[j][:, pc * 512:(pc + 1) * 512],
                                        start=(j == 0), stop=(j == 7))
                    ins_ = t.matmul(PS[a][:, 1024:1026],
                                    B[j][:, 1024 + i * 128:1024 + (i + 1) * 128],
                                    C[j][:, 1024:1026],
                                    start=(j == 0), stop=(j == 7))
                inc(ins_, "sWc")
            mmf = cnt["sWc"]
            if i == 7:
                fold_marks["k"] = mmf

            @blk.scalar
            def _(s, i=i, mmf=mmf):
                wt(s, "sWc", mmf)
                a = i % 2
                ins_ = s.activation(B[i][:, 0:1024], PS[a][:, 0:1024], AF.Identity)
                ins_ = s.activation(WNY[:, i:i + 1], PS[a][:, 1024:1025],
                                    AF.Identity)
                ps_rel(ins_, a)
            ev_marks.append(cnt["sPS%d" % (i % 2)])

        xdone, ffn1_mark = emit_proj_residual(
            lhs_fn=ffn_lhs, rhs=A, bias_base=24, w_stream=None,
            first_waits=[("sV", ffn1_wbuilt)], rhs_qmarks=xn2_q)

        ck(6)
        tap("t_x6", [x[:, :] for x in X], xdone)
        xn3_ready, xn3_q = emit_ln(X, A, [[xdone[i]] for i in range(8)], dst_free=ffn1_mark,
                            db=pj_mark)

        ck(7)
        @blk.gpsimd
        def _(g):
            wt(g, "sV", ffn1_wbuilt)
            for i in range(12):
                inc(g.dma_start(out=wsth[i * 128:(i + 1) * 128, :],
                                in_=Dv[i][:, 0:512]), "dG", 16)
        w_stash = cnt["dG"]

        @blk.gpsimd
        def _(g):
            wt(g, ffn1_mark[0], ffn1_mark[1])
            for j in range(8):
                inc(g.dma_start(out=Dv[8 + j][:, 0:1024],
                                in_=WvT[j * 128:(j + 1) * 128, :]), "dG", 16)
        dg_vstage = cnt["dG"]

        for i in range(8):
            @blk.tensor
            def _(t, i=i):
                if i == 0:
                    wt(t, "dG", dg_vstage)
                a = i % 2
                ps_acq(t, a)
                for j in range(8):
                    for pc in range(2):
                        ins_ = t.matmul(PS[a][:, pc * 512:(pc + 1) * 512],
                                        Dv[8 + j][:, i * 128:(i + 1) * 128],
                                        C[j][:, pc * 512:(pc + 1) * 512],
                                        start=(j == 0), stop=(j == 7))
                    ins_ = t.matmul(PS[a][:, 1024:1026],
                                    Dv[8 + j][:, i * 128:(i + 1) * 128],
                                    C[j][:, 1024:1026],
                                    start=(j == 0), stop=(j == 7))
                inc(ins_, "sWc")
            mmf = cnt["sWc"]
            if i == 7:
                fold_marks["v"] = mmf

            @blk.scalar
            def _(s, i=i, mmf=mmf):
                wt(s, "sWc", mmf)
                if i == 0:
                    wt(s, "dG", w_stash)
                    wt(s, ffn1_mark[0], ffn1_mark[1])
                a = i % 2
                ins_ = s.activation(Dv[i][:, 0:1024], PS[a][:, 0:1024], AF.Identity)
                ins_ = s.activation(WNY[:, 8 + i:9 + i], PS[a][:, 1024:1025],
                                    AF.Identity)
                ps_rel(ins_, a)
            ev_marks.append(cnt["sPS%d" % (i % 2)])

        @blk.tensor
        def _(t):
            wt(t, "dG", dG_prolog)
            for si, (cb0, sl0) in enumerate(((0, 40), (8, 49))):
                a = si % 2
                ps_acq(t, a)
                for cchunk in range(8):
                    for j in range(8):
                        ins_ = t.matmul(PS[a][:, cchunk:cchunk + 1],
                                        C[j][:, cchunk * 128:(cchunk + 1) * 128],
                                        colB[:, cb0 + j:cb0 + j + 1],
                                        start=(j == 0), stop=(j == 7))
                for j in range(8):
                    ins_ = t.matmul(PS[a][0:1, 8:9],
                                    C[j][:, 1024:1025],
                                    colB[:, cb0 + j:cb0 + j + 1],
                                    start=(j == 0), stop=(j == 7))
                inc(ins_, "sT")
        bias_mm = cnt["sT"]

        @blk.vector
        def _(v):
            wt(v, "sT", bias_mm)
            for si, sl0 in ((0, 40), (1, 49)):
                a = si % 2
                for cchunk in range(8):
                    ins_ = v.tensor_copy(colC[:, sl0 + cchunk:sl0 + cchunk + 1],
                                         PS[a][:, cchunk:cchunk + 1])
                ins_ = v.tensor_copy(colC[0:1, sl0 + 8:sl0 + 9], PS[a][0:1, 8:9])
                ps_rel(ins_, a)
        bias_cols_done = cnt["sPS1"]
        bias_cols_done0 = cnt["sPS0"]

        tap("t_wkcs", [b_[:, 0:1024] for b_ in B],
            [("sPS0", cnt["sPS0"]), ("sPS1", cnt["sPS1"])])
        tap("t_wvcs", [d_[:, 0:1024] for d_ in Dv[0:8]],
            [("sPS0", cnt["sPS0"]), ("sPS1", cnt["sPS1"])])
        tap("t_wny", [WNY[:, :]], [("sPS0", cnt["sPS0"]), ("sPS1", cnt["sPS1"])])
        tap("t_bcol", [colC[:, :]], [("sPS0", cnt["sPS0"]), ("sPS1", cnt["sPS1"])])
        q_marks = []
        wcq = []
        for i in range(8):
            @blk.gpsimd
            def _(g, i=i):
                if i == 0:
                    wt(g, "sPV", sPV0 + 512)
                    wt(g, "sPS0", bias_cols_done0)
                if i >= 2:
                    wt(g, "sWc", wcq[i - 2])
                inc(g.dma_start(
                    out=PT[i % 2][:, :].rearrange("p (c m) -> p c m", m=128),
                    in_=Wq_b[:, i * 128:(i + 1) * 128].rearrange("(c p) m -> p c m", p=128)),
                    "dG", 16)
            dg_q = cnt["dG"]

            @blk.tensor
            def _(t, i=i, dg_q=dg_q):
                wt(t, "dG", dg_q)
                a = i % 2
                ps_acq(t, a)
                for pc in range(4):
                    if i < 2:
                        wt(t, "sV", xn3_q[pc])
                    for j in range(8):
                        ins_ = t.matmul(PS[a][:, pc * 512:(pc + 1) * 512],
                                        PT[i % 2][:, j * 128:(j + 1) * 128],
                                        A[j][:, pc * 512:(pc + 1) * 512],
                                        start=(j == 0), stop=(j == 7))
                inc(ins_, "sWc")
            wcq.append(cnt["sWc"])
            mm_q = cnt["sWc"]

            @blk.scalar
            def _(s, i=i, mm_q=mm_q):
                wt(s, "sWc", mm_q)
                a = i % 2
                ins_ = s.activation(C[i][:, :], PS[a][:, :], AF.Identity,
                                    bias=colC[:, 32 + i:33 + i])
                ps_rel(ins_, a)
            q_marks.append(cnt["sPS%d" % (i % 2)])
        q_done_marks = list(q_marks)

        @blk.gpsimd
        def _(g):
            wt(g, "sWc", fold_marks["k"])
            for r in range(8):
                for pc in range(2):
                    inc(g.dma_start(
                        out=B[r][:, 1024 + pc * 512:1536 + pc * 512],
                        in_=IDFT_h[r * 128:(r + 1) * 128, pc * 512:(pc + 1) * 512]),
                        "dG", 16)

        tap("t_qb", [c_[:, :] for c_ in C],
            [("sPS0", cnt["sPS0"]), ("sPS1", cnt["sPS1"])])
        prod_marks = []
        sf0_free_marks = []
        _dbg2 = [0]
        for fc in range(4):
            for tq in range(4):
                it = 4 * fc + tq
                @blk.tensor
                def _(t, fc=fc, tq=tq, it=it):
                    if it < 4:
                        wt(t, "sV", xn3_q[tq])
                    ps_acq(t, 1)
                    for half, f0 in ((0, fc), (1, 4 + fc)):
                        for j in range(8):
                            ins_ = t.matmul(
                                ps1[:, half * 512:(half + 1) * 512],
                                Dv[j][:, f0 * 128:(f0 + 1) * 128],
                                A[j][:, tq * 512:(tq + 1) * 512],
                                start=(j == 0), stop=(j == 7))
                    inc(ins_, "sT")
                    ps_acq(t, 0)
                    for half, f0 in ((0, fc), (1, 4 + fc)):
                        for j in range(8):
                            ins_ = t.matmul(
                                ps0[:, half * 512:(half + 1) * 512],
                                B[j][:, f0 * 128:(f0 + 1) * 128],
                                A[j][:, tq * 512:(tq + 1) * 512],
                                start=(j == 0), stop=(j == 7))
                    inc(ins_, "sT")
                vmm = cnt["sT"] - 1
                kmm = cnt["sT"]

                @blk.scalar
                def _(s, fc=fc, vmm=vmm, it=it):
                    wt(s, "sT", vmm)
                    if it >= 1:
                        wt(s, "sV", sf0_free_marks[it - 1])
                    ins_ = s.activation(SF0[:, 0:512], ps1[:, 0:512], AF.Identity,
                                        bias=colC[:, 49 + fc:50 + fc])
                    ins_ = s.activation(SF0[:, 512:1024], ps1[:, 512:1024],
                                        AF.Identity, bias=colC[:, 53 + fc:54 + fc])
                    ps_rel(ins_, 1)
                vev = cnt["sPS1"]

                if DEBUG_TAPS and fc == 0 and tq == 0:
                    @blk.vector
                    def _(v, vev=vev, kmm=kmm):
                        wt(v, "sPS1", vev)
                        wt(v, "sT", kmm)
                        ins_ = v.tensor_copy(PT[0][:, :], ps0[:, 0:1024])
                        ins_ = v.tensor_copy(PT[1][:, :], SF0[:, 0:1024])
                        inc(ins_, "sV")
                    dbg_mark = cnt["sV"]
                    tap("t_k0", [PT[0][:, :]], [("sV", dbg_mark)])
                    tap("t_v0", [PT[1][:, :]], [("sV", dbg_mark)])
                    dbg_ds = cnt["dS"]

                @blk.vector
                def _(v, fc=fc, tq=tq, vev=vev, kmm=kmm):
                    if DEBUG_TAPS and fc == 0 and tq == 0:
                        wt(v, "dS", dbg_ds)
                    if DEBUG_TAPS and it >= 1:
                        wt(v, "dS", _dbg2[0])
                    wt(v, "sPS1", vev)
                    wt(v, "sT", kmm)
                    combos = [(0, 0, 0), (1, 1, 1), (0, 1, 2), (1, 0, 3)]
                    tmps = [ps0[:, 1024:1536], ps0[:, 1536:2048]]
                    red_q = []
                    for idx, (kh, vh, k) in enumerate(combos):
                        bcol = 40 + fc if kh == 0 else 44 + fc
                        if idx >= 2:
                            pk, ptmp = red_q.pop(0)
                            v.reduce_sum(RED[:, it * 4 + pk:it * 4 + pk + 1],
                                         ptmp, axis=AX.X)
                        v.scalar_tensor_tensor(
                            tmps[idx % 2], ps0[:, kh * 512:(kh + 1) * 512],
                            colC[:, bcol:bcol + 1],
                            SF0[:, vh * 512:(vh + 1) * 512],
                            OP.add, OP.mult)
                        red_q.append((k, tmps[idx % 2]))
                    inc(v.memset(stq[:, 6:7], 0.0), "sV")
                    for pk, ptmp in red_q:
                        ins_ = v.reduce_sum(RED[:, it * 4 + pk:it * 4 + pk + 1],
                                            ptmp, axis=AX.X)
                    ps_rel(ins_, 0)
                    inc(v.memset(stq[:, 7:8], 0.0), "sV")
                sf0_free_marks.append(cnt["sV"] - 1)
                prod_marks.append(cnt["sV"])
                if DEBUG_TAPS:
                    tap("t_sh%d" % it, [Shat[:, 0:12]], [("sV", prod_marks[it])])
                    _dbg2[0] = cnt["dS"]
        shat_main = cnt["sV"]
        last_vmm = vmm

        @blk.tensor
        def _(t):
            ps_acq(t, 1)
            for j in range(8):
                for pc in range(4):
                    ins_ = t.matmul(ps1[0:1, pc * 512:(pc + 1) * 512],
                                    WNY[:, 8 + j:9 + j],
                                    A[j][:, pc * 512:(pc + 1) * 512],
                                    start=(j == 0), stop=(j == 7))
            inc(ins_, "sT")
            ps_acq(t, 0)
            for j in range(8):
                for pc in range(4):
                    ins_ = t.matmul(ps0[0:1, pc * 512:(pc + 1) * 512],
                                    WNY[:, j:j + 1],
                                    A[j][:, pc * 512:(pc + 1) * 512],
                                    start=(j == 0), stop=(j == 7))
            inc(ins_, "sT")
        ny_v = cnt["sT"] - 1
        ny_k = cnt["sT"]

        @blk.scalar
        def _(s):
            wt(s, "sT", ny_v)
            ins_ = s.activation(SBF0[0:1, 0:2048], ps1[0:1, 0:2048], AF.Identity,
                                bias=colC[0:1, 57:58])
            ps_rel(s.activation(stq[:, 7:8], stq[:, 6:7], AF.Identity), 1)
        ny_vev = cnt["sPS1"]

        @blk.vector
        def _(v):
            wt(v, "sPS1", ny_vev)
            wt(v, "sT", ny_k)
            ins_ = v.scalar_tensor_tensor(ps1[0:1, 1024:2048], ps0[0:1, 0:1024],
                                          colC[0:1, 48:49], SBF0[0:1, 0:1024],
                                          OP.add, OP.mult)
            ins_ = v.reduce_sum(RED[0:1, 64:65], ps1[0:1, 1024:2048], axis=AX.X)
            ins_ = v.scalar_tensor_tensor(ps1[0:1, 1024:2048], ps0[0:1, 1024:2048],
                                          colC[0:1, 48:49], SBF0[0:1, 1024:2048],
                                          OP.add, OP.mult)
            ins_ = v.reduce_sum(RED[0:1, 65:66], ps1[0:1, 1024:2048], axis=AX.X)
            ps_rel(ins_, 0)
            inc(v.memset(stq[:, 6:7], 0.0), "sV")
        ny_red = cnt["sV"]

        @blk.gpsimd
        def _(g):
            wt(g, "sV", ny_red)
            inc(g.dma_start(out=SBF0[0:1, 0:1024], in_=IDFT_h[1024:1025, :]),
                "dG", 16)
        idft_pieces_dg = cnt["dG"]

        @blk.scalar
        def _(s):
            wt(s, "sV", ny_red)
            inc(s.activation(stq[:, 7:8], stq[:, 6:7], AF.Identity), "sC")
        hop1 = cnt["sC"]

        @blk.vector
        def _(v):
            wt(v, "sC", hop1)
            rv = RED[:, 0:64].rearrange("p (i k) -> p i k", k=4)
            ins_ = v.tensor_tensor(Shat[:, 8:24], rv[:, :, 0:1], rv[:, :, 1:2],
                                   OP.subtract)
            ins_ = v.tensor_tensor(Shat[:, 24:40], rv[:, :, 2:3], rv[:, :, 3:4],
                                   OP.add)
            inc(v.memset(stq[:, 6:7], 0.0), "sV")
        bulk1 = cnt["sV"]

        @blk.scalar
        def _(s):
            wt(s, "sV", bulk1)
            inc(s.activation(stq[:, 7:8], stq[:, 6:7], AF.Identity), "sC")
        hop2 = cnt["sC"]

        @blk.vector
        def _(v):
            wt(v, "sC", hop2)
            ins_ = v.reduce_sum(Shat[:, 0:4],
                                Shat[:, 8:24].rearrange("p (f t) -> p f t", t=4),
                                axis=AX.X)
            ins_ = v.reduce_sum(Shat[:, 4:8],
                                Shat[:, 24:40].rearrange("p (f t) -> p f t", t=4),
                                axis=AX.X)
            inc(v.memset(stq[:, 6:7], 0.0), "sV")
        shat_done = cnt["sV"]

        @blk.scalar
        def _(s):
            wt(s, "sV", shat_done)
            ins_ = s.activation(Shat[0:1, 8:9], RED[0:1, 64:65], AF.Identity,
                                bias=RED[0:1, 65:66])
            inc(s.activation(stq[:, 7:8], stq[:, 6:7], AF.Identity), "sC")
        sny_done = cnt["sC"]

        tap("t_shat", [Shat[:, :]], [("sV", shat_done), ("sC", sny_done)])
        @blk.vector
        def _(v):
            wt(v, "sV", shat_done)
            wt(v, "sC", sny_done)
            inc(v.tensor_copy(colB[:, 0:9], Shat[:, 0:9]), "sV")
        shatbf_done = cnt["sV"]

        @blk.tensor
        def _(t):
            wt(t, "sV", shatbf_done)
            wt(t, "dG", idft_pieces_dg)
            ps_acq(t, 1)
            for pc in range(2):
                for r in range(9):
                    if r < 8:
                        lhs = colB[:, r:r + 1]
                        rhs = B[r][:, 1024 + pc * 512:1536 + pc * 512]
                    else:
                        lhs = colB[0:1, 8:9]
                        rhs = SBF0[0:1, pc * 512:(pc + 1) * 512]
                    ins_ = t.matmul(ps1[0:1, pc * 512:(pc + 1) * 512],
                                    lhs, rhs, start=(r == 0), stop=(r == 8))
            inc(ins_, "sT")
        s_time_done = cnt["sT"]

        @blk.scalar
        def _(s):
            wt(s, "sT", s_time_done)
            ins_ = s.activation(s_bf[0:1, 0:1024], ps1[0:1, 0:1024], AF.Identity)
            ins_ = s.activation(s_bf[0:1, 1024:2048], ps1[0:1, 0:1024], AF.Identity)
            inc(s.activation(stq[:, 7:8], stq[:, 6:7], AF.Identity), "sC")
            ps_rel(s.activation(stq[:, 7:8], stq[:, 6:7], AF.Identity), 1)
        sbf_done = cnt["sC"]

        @blk.sync
        def _(sy):
            wt(sy, "sC", sbf_done)
            wt(sy, "sWc", fold_marks["v"])
            inc(sy.dma_start(out=s2h, in_=s_bf[0:1, :]), "dS", 16)
        s2_w = cnt["dS"]

        @blk.sync
        def _(sy):
            wt(sy, "dS", s2_w)
            for j in range(8):
                win = bass.AP(s2h.tensor, j * 128, [[1, 128], [1, 1024]])
                inc(sy.dma_start(out=B[j][:, 1024:2048], in_=win), "dS", 16)
        circ_done = cnt["dS"]

        tap("t_sbf", [s_bf[0:1, :]], [("dS", circ_done)])
        tap("t_circ", [b_[:, 1024:2048] for b_ in B], [("dS", circ_done)])
        fin_marks = []
        for mc in range(8):
            @blk.tensor
            def _(t, mc=mc):
                if mc == 0:
                    wt(t, "dS", circ_done)
                a = mc % 2
                ps_acq(t, a)
                for j in range(8):
                    for pc in range(4):
                        ins_ = t.matmul(PS[a][:, pc * 512:(pc + 1) * 512],
                                        B[j][:, 1024 + mc * 128:1024 + (mc + 1) * 128],
                                        C[j][:, pc * 512:(pc + 1) * 512],
                                        start=(j == 0), stop=(j == 7))
                inc(ins_, "sT")
            mmf = cnt["sT"]

            @blk.vector
            def _(v, mc=mc, mmf=mmf):
                wt(v, "sT", mmf)
                a = mc % 2
                ins_ = v.tensor_tensor(X[mc][:, :], PS[a][:, :], X[mc][:, :], OP.add)
                ps_rel(ins_, a)
            fin_marks.append(("sPS%d" % (mc % 2), cnt["sPS%d" % (mc % 2)]))
        final_mm_done = cnt["sT"]

        @blk.gpsimd
        def _(g):
            wt(g, "dG", w_stash)
            wt(g, "sT", last_vmm)
            for i in range(8):
                inc(g.dma_start(out=Dv[i][:, 0:512],
                                in_=wsth[i * 128:(i + 1) * 128, :]), "dG", 16)
        w_reload = cnt["dG"]

        ck(9)
        xn4_ready, xn4_q = emit_ln(X, A, [[fin_marks[i]] for i in range(8)],
                            dst_free=("sT", ny_k), db=("sWc", wcq[-1]))

        ck(95)
        def ffn_lhs2(i, j):
            if j < 4:
                if i < 4:
                    return Dv[j][:, i * 128:(i + 1) * 128]
                return Dv[4 + j][:, (i - 4) * 128:(i - 3) * 128]
            if i < 4:
                return Dv[8 + (j - 4)][:, i * 128:(i + 1) * 128]
            return Dv[j - 4][:, (i - 4) * 128:(i - 3) * 128]

        xdone_f, _mk2 = emit_proj_residual(
            lhs_fn=ffn_lhs2, rhs=A, bias_base=24, w_stream=None,
            first_waits=[("dG", w_reload)], rhs_qmarks=xn4_q)

        @blk.sync
        def _(sy):
            for i in range(8):
                wt(sy, xdone_f[i][0], xdone_f[i][1])
                inc(sy.dma_start(out=outT[i * 128:(i + 1) * 128, :],
                                 in_=X[i][:, :]), "sST", 16)
            sy.wait_ge(sems["sST"], cnt["sST"])
            if DEBUG_TAPS:
                sy.wait_ge(sems["dS"], cnt["dS"])

    return nc


_NC_CACHE = []


def kernel(**inputs):
    consts = _host_consts()
    x = np.ascontiguousarray(inputs["L0_attn_a_data"], dtype=np.float32)
    f32 = lambda a: np.ascontiguousarray(a, dtype=np.float32)
    base = {
        "Wqkv": f32(inputs["L0_attn_a_W_qkv"]),
        "bqkv": f32(inputs["L0_attn_a_b_qkv"]),
        "Wout": f32(inputs["L0_attn_a_W_out"]),
        "bout": f32(inputs["L0_attn_a_b_out"]),
        "Wq_b": f32(inputs["L0_attn_b_Wq"]),
        "bq_b": f32(inputs["L0_attn_b_bq"]),
        "WkT": np.ascontiguousarray(np.asarray(inputs["L0_attn_b_Wk"], dtype=np.float32).T),
        "bk_b": f32(inputs["L0_attn_b_bk"]),
        "WvT": np.ascontiguousarray(np.asarray(inputs["L0_attn_b_Wv"], dtype=np.float32).T),
        "bv_b": f32(inputs["L0_attn_b_bv"]),
        "G1R": f32(inputs["L0_ffn_G1_R"]),
        "G2R": f32(inputs["L0_ffn_G2_R"]),
        "G1I": f32(inputs["L0_ffn_G1_I"]),
        "G2I": f32(inputs["L0_ffn_G2_I"]),
        "fbias": np.concatenate([f32(inputs["L0_ffn_bias_real"]),
                                 f32(inputs["L0_ffn_bias_imag"])]),
    }
    base.update(consts)
    nc = build_nc()
    in_maps = []
    for b in range(8):
        m = dict(base)
        m["xT"] = np.ascontiguousarray(x[b].T)
        in_maps.append(m)
    res = run_bass_kernel_spmd(nc, in_maps, core_ids=list(range(8)))
    out = np.stack([np.asarray(res.results[b]["outT"]).T for b in range(8)])
    return np.ascontiguousarray(out, dtype=np.float32)


# version 22
# speedup vs baseline: 1.0522x; 1.0224x over previous
import numpy as np
import ml_dtypes

import concourse.bass as bass
import concourse.mybir as mybir
from concourse.bass_utils import run_bass_kernel_spmd

F32 = mybir.dt.float32
BF16 = mybir.dt.bfloat16
BF = ml_dtypes.bfloat16
AF = mybir.ActivationFunctionType
OP = mybir.AluOpType
AX = mybir.AxisListType

D = 1024
T = 2048
DEBUG_TAPS = False
PHASE_LIM = 99
OLD_RECIPS = True


class _Stop(Exception):
    pass


def _host_consts():
    d = np.arange(D)
    f = np.arange(512)
    ang = 2 * np.pi * np.outer(d, f) / D
    CS = np.zeros((D, 1026), np.float32)
    CS[:, 0:512] = np.cos(ang)
    CS[:, 512:1024] = -np.sin(ang)
    CS[:, 1024] = np.cos(np.pi * d)
    IDFT = np.zeros((1026, D), np.float32)
    coef = np.where(f == 0, 1.0, 2.0) / D
    IDFT[0:512] = coef[:, None] * np.cos(ang.T)
    IDFT[512:1024] = (-2.0 / D) * np.sin(ang.T)
    IDFT[1024] = np.cos(np.pi * d) / D
    be = np.arange(512)
    RfT = np.zeros((32, 512), np.float32); RfT[be // 16, be] = 1
    RqfT = np.zeros((16, 512), np.float32); RqfT[be % 16, be] = 1
    RpT = np.zeros((32, 512), np.float32)
    RpT[(be // 128) * 8 + (be % 128) // 16, be] = 1
    p = np.arange(128)
    RqpT = np.zeros((16, 128), np.float32); RqpT[p % 16, p] = 1
    return {
        "CONST_CS": CS.astype(BF),
        "CONST_IDFT": IDFT.astype(np.float32),
        "CONST_RfT": RfT.astype(BF),
        "CONST_RqfT": RqfT.astype(BF),
        "CONST_RpT": RpT.astype(BF),
        "CONST_RqpT": RqpT.astype(BF),
    }


def build_nc():
    from contextlib import ExitStack
    nc = bass.Bass("TRN2", target_bir_lowering=False, detect_race_conditions=False)

    def din(name, shape, dt=F32):
        return nc.dram_tensor(name, shape, dt, kind="ExternalInput").ap()

    xT = din("xT", [D, T])
    Wqkv = din("Wqkv", [D, 3 * D])
    bqkv = din("bqkv", [3 * D])
    Wout = din("Wout", [D, D])
    bout = din("bout", [D])
    Wq_b = din("Wq_b", [D, D])
    bq_b = din("bq_b", [D])
    WkT = din("WkT", [D, D])
    bk_b = din("bk_b", [D])
    WvT = din("WvT", [D, D])
    bv_b = din("bv_b", [D])
    G1R = din("G1R", [1, 32, 4, 32])
    G2R = din("G2R", [4, 16, 1, 16])
    G1I = din("G1I", [1, 32, 4, 32])
    G2I = din("G2I", [4, 16, 1, 16])
    fbias = din("fbias", [D])
    CS_h = din("CONST_CS", [D, 1026], BF16)
    IDFT_h = din("CONST_IDFT", [1026, D])
    RfT_h = din("CONST_RfT", [32, 512], BF16)
    RqfT_h = din("CONST_RqfT", [16, 512], BF16)
    RpT_h = din("CONST_RpT", [32, 512], BF16)
    RqpT_h = din("CONST_RqpT", [16, 128], BF16)

    outT = nc.dram_tensor("outT", [D, T], F32, kind="ExternalOutput").ap()
    s2h = nc.dram_tensor("s2h", [2 * D], BF16, kind="Internal").ap()
    wsth = nc.dram_tensor("wsth", [12 * 128, 512], BF16, kind="Internal").ap()

    import kernel as _self

    with ExitStack() as ctx:
        def sbuf(name, shape, dt):
            return ctx.enter_context(nc.sbuf_tensor(name, shape, dt))

        X = [sbuf(f"X{i}", [128, T], F32) for i in range(8)]
        A = [sbuf(f"Ax{i}", [128, T], BF16) for i in range(8)]
        B = [sbuf(f"Bx{i}", [128, T], BF16) for i in range(8)]
        C = [sbuf(f"Cx{i}", [128, T], BF16) for i in range(8)]
        Dv = [sbuf(f"Dx{i}", [128, 1040], BF16) for i in range(16)]
        SBF0 = sbuf("SBF0", [128, T], BF16)
        PT = [sbuf("PT0", [128, 1024], BF16), sbuf("PT1", [128, 1024], BF16)]
        SF0 = sbuf("SF0", [128, 1024], F32)
        WVH = PT
        ones_bf = sbuf("ones_bf", [128, 128], BF16)
        ones_f = sbuf("ones_f", [128, 128], F32)
        ones1f = sbuf("ones1f", [1, 64], F32)
        colC = sbuf("colC", [128, 72], F32)
        colB = sbuf("colB", [128, 16], BF16)
        negc = sbuf("negc", [128, 8], F32)
        stq = sbuf("stq", [128, 8], F32)
        Shat = sbuf("Shat", [128, 48], F32)
        RED = sbuf("RED", [128, 68], F32)
        WNY = sbuf("WNY", [128, 16], BF16)
        recips = SF0
        s_bf = SBF0
        bvrow = SBF0
        onesr_bf = ones_bf
        RfT_s = B[0]
        RqfT_s = B[0]
        RpT_s = B[0]
        RqpT_s = B[0]
        g1s = B[2]
        g2s = B[2]
        T1 = B[1]
        T1q = B[3]
        brep = B[2]

        ps0 = ctx.enter_context(nc.psum_tensor("ps0", [128, T], F32))
        ps1 = ctx.enter_context(nc.psum_tensor("ps1", [128, T], F32))
        PS = [ps0, ps1]

        semn = ["dS", "dG", "dSc", "sT", "sC", "sV", "sS", "sPV", "sE", "sRaw",
                "sDR", "sNm", "sStab", "sWc", "sST", "sPS0", "sPS1", "sRc"]
        sems = {n: ctx.enter_context(nc.semaphore(n)) for n in semn}
        cnt = {n: 0 for n in semn}

        blk_real = ctx.enter_context(nc.Block())

        class _BlkProxy:
            stop = False
            def __getattr__(self, eng):
                if self.stop:
                    return lambda f: None
                return getattr(blk_real, eng)
        blk = _BlkProxy()

        def ck(n):
            if _self.PHASE_LIM <= n:
                blk.__class__.stop = True

        def inc(ins_, name, amt=1):
            cnt[name] += amt
            ins_.then_inc(sems[name], amt)
            return cnt[name]

        def wt(e, name, val):
            if val > 0:
                e.wait_ge(sems[name], val)

        psw = [0, 0]

        def ps_acq(e, a):
            if psw[a] > 0:
                e.wait_ge(sems["sPS%d" % a], psw[a])
            psw[a] += 1

        def ps_rel(ins_, a):
            inc(ins_, "sPS%d" % a)

        tap_seen = {}
        def tap(name, aps, waits):
            if not DEBUG_TAPS:
                return
            if name in tap_seen:
                name = name + str(tap_seen[name])
            tap_seen[name] = tap_seen.get(name, 0) + 1
            rows = aps[0].shape[0]
            cols = aps[0].shape[1]
            dt_ = aps[0].dtype
            h = nc.dram_tensor(name, [len(aps), rows, cols], dt_,
                               kind="ExternalOutput").ap()
            @blk.sync
            def _(sy):
                for (nm, val) in waits:
                    wt(sy, nm, val)
                for i, ap_ in enumerate(aps):
                    inc(sy.dma_start(out=h[i], in_=ap_), "dS", 16)

        @blk.scalar
        def _(s):
            for i in (1, 3, 5, 7):
                inc(s.dma_start(out=X[i][:, :],
                                in_=xT[i * 128:(i + 1) * 128, :]), "dSc", 16)

        @blk.sync
        def _(sy):
            for i in (0, 2, 4, 6):
                inc(sy.dma_start(out=X[i][:, :],
                                 in_=xT[i * 128:(i + 1) * 128, :]), "dS", 16)
            for c in range(8):
                inc(sy.dma_start(out=colC[:, c:c + 1],
                                 in_=bqkv[c * 128:(c + 1) * 128][:, None]), "dS", 16)
            for c in range(8):
                inc(sy.dma_start(out=colC[:, 8 + c:9 + c],
                                 in_=bqkv[1024 + c * 128:1024 + (c + 1) * 128][:, None]), "dS", 16)
            for c in range(8):
                inc(sy.dma_start(out=colC[:, 16 + c:17 + c],
                                 in_=bout[c * 128:(c + 1) * 128][:, None]), "dS", 16)
            for c in range(8):
                inc(sy.dma_start(out=colC[:, 24 + c:25 + c],
                                 in_=fbias[c * 128:(c + 1) * 128][:, None]), "dS", 16)
            for c in range(8):
                inc(sy.dma_start(out=colC[:, 32 + c:33 + c],
                                 in_=bq_b[c * 128:(c + 1) * 128][:, None]), "dS", 16)
        dS_x = [("dS" if i % 2 == 0 else "dSc", 16 * (i // 2 + 1))
                for i in range(8)]
        dS_prolog = cnt["dS"]

        @blk.gpsimd
        def _(g):
            for c in range(8):
                inc(g.dma_start(out=colB[:, c:c + 1],
                                in_=bk_b[c * 128:(c + 1) * 128][:, None]), "dG", 16)
            for c in range(8):
                inc(g.dma_start(out=colB[:, 8 + c:9 + c],
                                in_=bv_b[c * 128:(c + 1) * 128][:, None]), "dG", 16)
        dG_prolog = cnt["dG"]

        @blk.vector
        def _(v):
            ins_ = v.memset(ones_bf[:, :], 1.0)
            ins_ = v.memset(ones_f[:, :], 1.0)
            ins_ = v.memset(ones1f[:, :], 1.0)
            ins_ = v.memset(colC[:, :], 0.0)
            ins_ = v.memset(stq[:, :], 0.0)
            ins_ = v.memset(Shat[:, :], 0.0)
            ins_ = v.memset(negc[:, :], 0.0)
            for t in range(16):
                dview = Dv[t][:, :].rearrange("p (h f) -> p h f", f=65)
                ins_ = v.memset(dview[:, :, 64:65], 1.0)
            inc(ins_, "sV")
        ones_ready = cnt["sV"]

        def emit_ln(src, dst, chunk_ready, dst_free=None, db=None):
            sq_done = []
            st_done = []
            for i in range(8):
                use_pt = db is not None and i % 2 == 1
                @blk.vector
                def _(v, i=i, use_pt=use_pt):
                    for (nm, val) in chunk_ready[i]:
                        wt(v, nm, val)
                    if i == 1 and db is not None:
                        wt(v, db[0], db[1])
                    lim = 2 if db is not None else 1
                    if i >= lim:
                        wt(v, "sT", st_done[i - lim])
                    if use_pt:
                        ins_ = v.tensor_tensor(PT[0][:, :], src[i][:, 0:1024],
                                               src[i][:, 0:1024], OP.mult)
                        inc(v.tensor_tensor(PT[1][:, :], src[i][:, 1024:2048],
                                            src[i][:, 1024:2048], OP.mult), "sV")
                    else:
                        inc(v.tensor_tensor(SBF0[:, :], src[i][:, :], src[i][:, :],
                                            OP.mult), "sV")
                sq_done.append(cnt["sV"])

                @blk.tensor
                def _(t, i=i, use_pt=use_pt):
                    if i == 0:
                        wt(t, "sV", ones_ready)
                        ps_acq(t, 0)
                        ps_acq(t, 1)
                    wt(t, "sV", sq_done[i])
                    st = (i == 0); sp = (i == 7)
                    for h in range(2):
                        for p in range(2):
                            ins_ = t.matmul(
                                PS[h][:, p * 512:(p + 1) * 512], ones_f[:, :],
                                src[i][:, h * 1024 + p * 512:h * 1024 + (p + 1) * 512],
                                start=st, stop=sp)
                    for h in range(2):
                        for p in range(2):
                            sqs = (PT[h][:, p * 512:(p + 1) * 512] if use_pt else
                                   SBF0[:, h * 1024 + p * 512:h * 1024 + (p + 1) * 512])
                            ins_ = t.matmul(
                                PS[h][:, 1024 + p * 512:1024 + (p + 1) * 512],
                                ones_bf[:, :], sqs,
                                start=st, stop=sp)
                    inc(ins_, "sT")
                st_done.append(cnt["sT"])

            qmarks = []
            for h in range(2):
                for q in range(2):
                    @blk.vector
                    def _(v, h=h, q=q):
                        if h == 0 and q == 0:
                            wt(v, "sT", st_done[7])
                        pa = PS[h]
                        ins_ = v.tensor_scalar(SF0[:, 0:512],
                                               pa[:, 512 * q:512 * q + 512],
                                               1.0 / 1024, None, OP.mult)
                        ins_ = v.tensor_scalar(SF0[:, 512:1024],
                                               pa[:, 1024 + 512 * q:1536 + 512 * q],
                                               1.0 / 1024, 1e-5, OP.mult, OP.add)
                        ins_ = v.tensor_tensor(pa[:, 512 * q:512 * q + 512],
                                               SF0[:, 0:512], SF0[:, 0:512], OP.mult)
                        ins_ = v.tensor_tensor(SF0[:, 512:1024], SF0[:, 512:1024],
                                               pa[:, 512 * q:512 * q + 512],
                                               OP.subtract)
                        inc(v.memset(stq[:, 6:7], 0.0), "sV")
                        if q == 1:
                            ps_rel(ins_, h)
                    rec_done = cnt["sV"]

                    @blk.scalar
                    def _(s, h=h, q=q, rec_done=rec_done):
                        wt(s, "sV", rec_done)
                        ins_ = s.activation(SF0[:, 512:1024],
                                            SF0[:, 512:1024], AF.Ln)
                        ins_ = s.activation(SF0[:, 512:1024],
                                            SF0[:, 512:1024],
                                            AF.Exp, scale=-0.5)
                        ins_ = s.activation(SBF0[:, 512:1024],
                                            SF0[:, 512:1024], AF.Identity)
                        inc(s.activation(stq[:, 7:8], stq[:, 6:7], AF.Identity),
                            "sC")
                    sqrt_done = cnt["sC"]

                    @blk.vector
                    def _(v, h=h, q=q, sqrt_done=sqrt_done):
                        wt(v, "sC", sqrt_done)
                        if h == 0 and q == 0 and dst_free is not None:
                            wt(v, dst_free[0], dst_free[1])
                        c0 = h * 1024 + 512 * q
                        for i in range(8):
                            ins_ = v.tensor_tensor(SBF0[:, 0:512],
                                                   src[i][:, c0:c0 + 512],
                                                   SF0[:, 0:512], OP.subtract)
                            ins_ = v.tensor_tensor(dst[i][:, c0:c0 + 512],
                                                   SBF0[:, 0:512],
                                                   SBF0[:, 512:1024], OP.mult)
                        inc(v.memset(stq[:, 7:8], 0.0), "sV")
                    qmarks.append(cnt["sV"])
            return cnt["sV"], qmarks

        xn1_ready, xn1_q = emit_ln(X, A, [[dS_x[i]] for i in range(8)])
        ln1_done_T = cnt["sT"]
        ck(1)
        tap("t_xn1", [a[:, :] for a in A], [("sV", xn1_ready)])

        @blk.gpsimd
        def _(g):
            for j in range(8):
                inc(g.dma_start(out=C[j][:, 0:1024],
                                in_=Wqkv[j * 128:(j + 1) * 128, 2048:3072]),
                    "dG", 16)

        @blk.gpsimd
        def _(g):
            wt(g, "sT", ln1_done_T)
            inc(g.dma_start(out=bvrow[0:1, 1024:2048],
                            in_=bqkv[2 * 1024:3 * 1024][None, :]), "dG", 16)
        dg_v = cnt["dG"]

        vt_done = []
        @blk.tensor
        def _(t):
            wt(t, "sV", ones_ready)
            wt(t, "dG", dG_prolog)
            wt(t, "dG", dg_v)
            for tt_ in range(16):
                wt(t, "sV", xn1_q[tt_ // 4])
                a = tt_ % 2
                ps_acq(t, a)
                for j in range(8):
                    for pc in range(2):
                        ins_ = t.matmul(PS[a][:, pc * 512:(pc + 1) * 512],
                                        A[j][:, tt_ * 128:(tt_ + 1) * 128],
                                        C[j][:, pc * 512:(pc + 1) * 512],
                                        start=(j == 0), stop=False)
                for pc in range(2):
                    ins_ = t.matmul(PS[a][:, pc * 512:(pc + 1) * 512],
                                    onesr_bf[0:1, 0:128],
                                    bvrow[0:1, 1024 + pc * 512:1024 + (pc + 1) * 512],
                                    start=False, stop=True)
                inc(ins_, "sT")
                vt_done.append(cnt["sT"])

        for tt_ in range(16):
            @blk.scalar
            def _(s, tt_=tt_):
                wt(s, "sT", vt_done[tt_])
                a = tt_ % 2
                pv = PS[a][:, 0:1024].rearrange("p (h e) -> p h e", e=64)
                dvw = Dv[tt_][:, :].rearrange("p (h f) -> p h f", f=65)[:, :, 0:64]
                ins_ = s.activation(dvw, pv, AF.Identity)
                ps_rel(ins_, a)
        v_done_T = cnt["sT"]

        qk_cons = []
        stab_ready = []
        for hp in range(8):
            for which in range(2):
                n = 2 * hp + which
                @blk.gpsimd
                def _(g, n=n, hp=hp, which=which):
                    if n >= 2:
                        wt(g, "sWc", qk_cons[n - 2])
                    dc = hp if which == 0 else 8 + hp
                    col0 = dc * 128 if dc < 8 else 1024 + (dc - 8) * 128
                    inc(g.dma_start(
                        out=PT[n % 2][:, :].rearrange("p (c m) -> p c m", m=128),
                        in_=Wqkv[:, col0:col0 + 128].rearrange("(c p) m -> p c m", p=128)),
                        "dG", 16)
                dg_n = cnt["dG"]

                @blk.tensor
                def _(t, n=n, dg_n=dg_n):
                    wt(t, "dG", dg_n)
                    a = n % 2
                    ps_acq(t, a)
                    for j in range(8):
                        for pc in range(4):
                            ins_ = t.matmul(PS[a][:, pc * 512:(pc + 1) * 512],
                                            PT[n % 2][:, j * 128:(j + 1) * 128],
                                            A[j][:, pc * 512:(pc + 1) * 512],
                                            start=(j == 0), stop=(j == 7))
                    inc(ins_, "sWc")
                qk_cons.append(cnt["sWc"])
                mm_done = cnt["sWc"]

                @blk.scalar
                def _(s, n=n, hp=hp, which=which, mm_done=mm_done):
                    if n == 0:
                        wt(s, "dS", dS_prolog)
                    wt(s, "sWc", mm_done)
                    a = n % 2
                    dst = B[hp] if which == 0 else C[hp]
                    bias_col = hp if which == 0 else 8 + hp
                    ins_ = s.activation(dst[:, :], PS[a][:, :], AF.Identity,
                                        bias=colC[:, bias_col:bias_col + 1])
                    ps_rel(ins_, a)
                    inc(s.activation(stq[:, 7:8], stq[:, 6:7], AF.Identity), "sC")
            qk_evac_done = cnt["sC"]

            @blk.vector
            def _(v, hp=hp, qev=qk_evac_done - 1):
                wt(v, "sC", qev)
                inc(v.tensor_tensor(SBF0[:, :], B[hp][:, :], B[hp][:, :], OP.mult),
                    "sV")
            sqq = cnt["sV"]

            @blk.tensor
            def _(t, sqq=sqq):
                wt(t, "sV", sqq)
                ps_acq(t, 0)
                for pc in range(4):
                    ins_ = t.matmul(PS[0][:, pc * 512:(pc + 1) * 512], ones_bf[:, :],
                                    SBF0[:, pc * 512:(pc + 1) * 512],
                                    start=True, stop=True)
                inc(ins_, "sT")
            mmq = cnt["sT"]

            @blk.vector
            def _(v, hp=hp, mmq=mmq, kev=qk_evac_done):
                wt(v, "sT", mmq)
                wt(v, "sC", kev)
                inc(v.tensor_tensor(SBF0[:, :], C[hp][:, :], C[hp][:, :], OP.mult),
                    "sV")
            sqk = cnt["sV"]

            @blk.vector
            def _(v, mmq=mmq):
                wt(v, "sT", mmq)
                ins_ = v.reduce_max(stq[:, 0:1], PS[0][:, :], axis=AX.X)
                ps_rel(ins_, 0)

            @blk.tensor
            def _(t, sqk=sqk):
                wt(t, "sV", sqk)
                ps_acq(t, 1)
                for pc in range(4):
                    ins_ = t.matmul(PS[1][:, pc * 512:(pc + 1) * 512], ones_bf[:, :],
                                    SBF0[:, pc * 512:(pc + 1) * 512],
                                    start=True, stop=True)
                inc(ins_, "sT")
            mmk = cnt["sT"]

            @blk.vector
            def _(v, mmk=mmk):
                wt(v, "sT", mmk)
                ins_ = v.reduce_max(stq[:, 1:2], PS[1][:, :], axis=AX.X)
                ps_rel(ins_, 1)
                inc(v.memset(stq[:, 6:7], 0.0), "sV")
            mul_done = cnt["sV"]

            @blk.scalar
            def _(s, hp=hp, mul_done=mul_done):
                wt(s, "sV", mul_done)
                ins_ = s.activation(stq[:, 4:5], stq[:, 0:1], AF.Ln)
                ins_ = s.activation(stq[:, 5:6], stq[:, 1:2], AF.Ln)
                ins_ = s.activation(stq[:, 2:3], stq[:, 4:5], AF.Identity,
                                    bias=stq[:, 5:6])
                ins_ = s.activation(stq[:, 3:4], stq[:, 2:3], AF.Exp, scale=0.5)
                inc(s.activation(stq[:, 7:8], stq[:, 6:7], AF.Identity), "sC")
            sqrt_done = cnt["sC"]

            @blk.vector
            def _(v, hp=hp, sqrt_done=sqrt_done):
                wt(v, "sC", sqrt_done)
                inc(v.tensor_scalar_mul(negc[:, hp:hp + 1], stq[:, 3:4], -0.125),
                    "sStab")
            stab_ready.append(cnt["sStab"])

        wc_after_qk = cnt["sWc"]
        tap("t_q", [b_[:, :] for b_ in B], [("sStab", stab_ready[7])])
        tap("t_k", [c_[:, :] for c_ in C], [("sStab", stab_ready[7])])
        tap("t_v", [d_[:, :] for d_ in Dv], [("sStab", stab_ready[7]),
                                             ("sPS0", cnt["sPS0"]),
                                             ("sPS1", cnt["sPS1"])])
        tap("t_negc", [negc[:, :]], [("sStab", stab_ready[7])])

        ck(2)
        sS0 = cnt["sS"]; sPV0 = cnt["sPV"]; sE0 = cnt["sE"]
        sRaw0 = cnt["sRaw"]; sDR0 = cnt["sDR"]; sNm0 = cnt["sNm"]

        @blk.tensor
        def _(t):
            ps_acq(t, 0)
            ps_acq(t, 1)

            def emit_pv(m):
                gp, kk = divmod(m, 16)
                hpp = gp % 8
                pp = gp % 2
                wt(t, "sE", sE0 + m + 1)
                if kk == 0:
                    wt(t, "sNm", sNm0 + gp - 1)
                ins_ = t.matmul(ps1[0:65, pp * 1024:pp * 1024 + 512],
                                Dv[kk][:, 130 * hpp:130 * hpp + 65],
                                PT[kk % 2][:, 0:512],
                                start=(kk == 0), stop=(kk == 15))
                ins_ = t.matmul(ps1[0:65, pp * 1024 + 512:pp * 1024 + 1024],
                                Dv[kk][:, 130 * hpp + 65:130 * hpp + 130],
                                PT[kk % 2][:, 512:1024],
                                start=(kk == 0), stop=(kk == 15))
                inc(ins_, "sPV")

            def emit_denrep(gp):
                pp = gp % 2
                wt(t, "sRaw", sRaw0 + gp + 1)
                ins_ = t.matmul(ps1[0:64, pp * 1024:pp * 1024 + 512],
                                ones_bf[64:65, 0:64],
                                SBF0[64:65, pp * 1024:pp * 1024 + 512],
                                start=True, stop=True)
                ins_ = t.matmul(ps1[0:64, pp * 1024 + 512:pp * 1024 + 1024],
                                ones_bf[64:65, 0:64],
                                SBF0[64:65, pp * 1024 + 512:pp * 1024 + 1024],
                                start=True, stop=True)
                inc(ins_, "sDR")

            for n in range(512):
                g, kt = divmod(n, 16)
                st_, hp = g // 8, g % 8
                wt(t, "sE", sE0 + n - 1)
                slot = kt % 2
                ins_ = t.matmul(ps0[:, slot * 1024:slot * 1024 + 512],
                                C[hp][0:64, kt * 128:(kt + 1) * 128],
                                B[hp][0:64, st_ * 512:(st_ + 1) * 512],
                                start=True, stop=True, tile_position=(0, 0))
                ins_ = t.matmul(ps0[:, slot * 1024 + 512:slot * 1024 + 1024],
                                C[hp][64:128, kt * 128:(kt + 1) * 128],
                                B[hp][64:128, st_ * 512:(st_ + 1) * 512],
                                start=True, stop=True, tile_position=(64, 0))
                inc(ins_, "sS")
                if kt == 13 and g >= 1:
                    emit_denrep(g - 1)
                if n >= 2:
                    emit_pv(n - 2)
            emit_pv(510)
            emit_pv(511)
            emit_denrep(31)

        @blk.scalar
        def _(s):
            wt(s, "sWc", wc_after_qk)
            for g in range(32):
                st_, hp = g // 8, g % 8
                for kt in range(16):
                    wt(s, "sS", sS0 + 16 * g + kt + 1)
                    wt(s, "sPV", sPV0 + 16 * g + kt - 1)
                    if g < 8 and kt == 0:
                        wt(s, "sStab", stab_ready[hp])
                    slot = kt % 2
                    ins_ = s.activation(PT[slot][:, :],
                                        ps0[:, slot * 1024:slot * 1024 + 1024],
                                        AF.Exp, bias=negc[:, hp:hp + 1], scale=0.125)
                    inc(ins_, "sE")

        @blk.vector
        def _(v):
            for g in range(32):
                p2 = g % 2
                if g >= 1:
                    gp = g - 1
                    stp, hpp = gp // 8, gp % 8
                    pp = gp % 2
                    wt(v, "sDR", sDR0 + gp + 1)
                    ins_ = v.tensor_tensor(A[hpp][0:64, stp * 512:(stp + 1) * 512],
                                           SBF0[0:64, pp * 1024:pp * 1024 + 512],
                                           ps1[0:64, pp * 1024:pp * 1024 + 512],
                                           OP.mult)
                    ins_ = v.tensor_tensor(A[hpp][64:128, stp * 512:(stp + 1) * 512],
                                           SBF0[0:64, pp * 1024 + 512:pp * 1024 + 1024],
                                           ps1[0:64, pp * 1024 + 512:pp * 1024 + 1024],
                                           OP.mult)
                    inc(ins_, "sNm")
                wt(v, "sPV", sPV0 + 16 * g + 16)
                ins_ = v.tensor_copy(SBF0[0:64, p2 * 1024:p2 * 1024 + 512],
                                     ps1[0:64, p2 * 1024:p2 * 1024 + 512])
                ins_ = v.tensor_copy(SBF0[0:64, p2 * 1024 + 512:p2 * 1024 + 1024],
                                     ps1[0:64, p2 * 1024 + 512:p2 * 1024 + 1024])
                with nc.allow_low_precision(reason="softmax denominator recip "
                                            "stored bf16 as matmul operand"):
                    ins_ = v.reciprocal(SBF0[64:65, p2 * 1024:p2 * 1024 + 512],
                                        ps1[64:65, p2 * 1024:p2 * 1024 + 512])
                    ins_ = v.reciprocal(SBF0[64:65, p2 * 1024 + 512:p2 * 1024 + 1024],
                                        ps1[64:65, p2 * 1024 + 512:p2 * 1024 + 1024])
                inc(v.memset(stq[:, 6:7], 0.0), "sRaw")
            gp = 31
            stp, hpp = gp // 8, gp % 8
            pp = gp % 2
            wt(v, "sDR", sDR0 + 32)
            ins_ = v.tensor_tensor(A[hpp][0:64, stp * 512:(stp + 1) * 512],
                                   SBF0[0:64, pp * 1024:pp * 1024 + 512],
                                   ps1[0:64, pp * 1024:pp * 1024 + 512], OP.mult)
            ins_ = v.tensor_tensor(A[hpp][64:128, stp * 512:(stp + 1) * 512],
                                   SBF0[0:64, pp * 1024 + 512:pp * 1024 + 1024],
                                   ps1[0:64, pp * 1024 + 512:pp * 1024 + 1024],
                                   OP.mult)
            inc(ins_, "sNm")
            ps_rel(v.memset(stq[:, 2:3], 0.0), 0)
            ps_rel(v.memset(stq[:, 2:3], 0.0), 1)
        ck(3)
        attn_done = cnt["sNm"]
        tap("t_attn", [a[:, :] for a in A], [("sNm", attn_done)])

        def emit_proj_residual(lhs_fn, rhs, bias_base, w_stream, first_waits,
                               rhs_qmarks=None):
            done = []
            wc_marks = []
            for i in range(8):
                if w_stream is not None:
                    @blk.gpsimd
                    def _(g, i=i):
                        if i == 0:
                            for (nm, val) in first_waits:
                                wt(g, nm, val)
                        if i >= 2:
                            wt(g, "sWc", wc_marks[i - 2])
                        inc(g.dma_start(
                            out=PT[i % 2][:, :].rearrange("p (c m) -> p c m", m=128),
                            in_=w_stream(i).rearrange("(c p) m -> p c m", p=128)),
                            "dG", 16)
                    dg_i = cnt["dG"]
                else:
                    dg_i = 0

                @blk.tensor
                def _(t, i=i, dg_i=dg_i):
                    if i == 0:
                        for (nm, val) in first_waits:
                            wt(t, nm, val)
                    if w_stream is not None:
                        wt(t, "dG", dg_i)
                    a = i % 2
                    ps_acq(t, a)
                    for pc in range(4):
                        if rhs_qmarks is not None and i < 2:
                            wt(t, "sV", rhs_qmarks[pc])
                        for j in range(8):
                            ins_ = t.matmul(PS[a][:, pc * 512:(pc + 1) * 512],
                                            lhs_fn(i, j),
                                            rhs[j][:, pc * 512:(pc + 1) * 512],
                                            start=(j == 0), stop=(j == 7))
                    inc(ins_, "sWc")
                wc_marks.append(cnt["sWc"])
                mm_i = cnt["sWc"]

                @blk.vector
                def _(v, i=i, mm_i=mm_i):
                    wt(v, "sWc", mm_i)
                    a = i % 2
                    if bias_base is not None:
                        ins_ = v.scalar_tensor_tensor(
                            X[i][:, :], PS[a][:, :],
                            colC[:, bias_base + i:bias_base + i + 1],
                            X[i][:, :], OP.add, OP.add)
                        ins_ = v.memset(stq[:, 6:7], 0.0)
                    else:
                        ins_ = v.tensor_tensor(X[i][:, :], PS[a][:, :],
                                               X[i][:, :], OP.add)
                    ps_rel(ins_, a)
                done.append(("sPS%d" % (i % 2), cnt["sPS%d" % (i % 2)]))
            return done, ("sWc", wc_marks[-1])

        xdone, pj_mark = emit_proj_residual(
            lhs_fn=lambda i, j: PT[i % 2][:, j * 128:(j + 1) * 128],
            rhs=A, bias_base=16,
            w_stream=lambda i: Wout[:, i * 128:(i + 1) * 128],
            first_waits=[("sNm", attn_done), ("sPV", sPV0 + 512)])

        ck(4)
        tap("t_x4", [x[:, :] for x in X], xdone)
        @blk.sync
        def _(sy):
            wt(sy, "sNm", attn_done)
            inc(sy.dma_start(out=B[0][0:32, 0:512], in_=RfT_h), "dS", 16)
            inc(sy.dma_start(out=B[0][0:16, 512:1024], in_=RqfT_h), "dS", 16)
            inc(sy.dma_start(out=B[2][0:32, 0:512], in_=RpT_h), "dS", 16)
            inc(sy.dma_start(out=B[2][0:16, 896:1024], in_=RqpT_h), "dS", 16)
        dS_Rm = cnt["dS"]

        @blk.gpsimd
        def _(g):
            wt(g, "sNm", attn_done)
            for rr, G1h, G2h in ((0, G1R, G2R), (1, G1I, G2I)):
                for c in range(4):
                    k = 4 * rr + c
                    inc(g.dma_start(out=B[2][0:32, 512 + 32 * k:544 + 32 * k],
                                    in_=G1h[0, :, c, :]), "dG", 16)
                    inc(g.dma_start(out=B[2][0:16, 768 + 16 * k:784 + 16 * k],
                                    in_=G2h[c, :, 0, :]), "dG", 16)
        dg_g = cnt["dG"]

        def t1ap(c):
            return (B[1] if c < 2 else B[3])[0:32, (c % 2) * 512:(c % 2) * 512 + 512]

        def t1qap(c):
            return (B[5] if c < 2 else B[7])[0:16, (c % 2) * 512:(c % 2) * 512 + 512]

        for rr in range(2):
            for c in range(4):
                @blk.tensor
                def _(t, rr=rr, c=c):
                    if rr == 0 and c == 0:
                        wt(t, "dS", dS_Rm)
                        wt(t, "dG", dg_g)
                    k = 4 * rr + c
                    ps_acq(t, 0)
                    ins_ = t.matmul(ps0[0:32, 0:512],
                                    B[2][0:32, 512 + 32 * k:544 + 32 * k],
                                    B[0][0:32, 0:512], start=True, stop=True)
                    ins_ = t.matmul(ps0[0:16, 512:1024],
                                    B[2][0:16, 768 + 16 * k:784 + 16 * k],
                                    B[0][0:16, 512:1024], start=True, stop=True)
                    inc(ins_, "sWc")
                t1_done = cnt["sWc"]

                @blk.vector
                def _(v, c=c, t1_done=t1_done):
                    wt(v, "sWc", t1_done)
                    ins_ = v.tensor_copy(t1ap(c), ps0[0:32, 0:512])
                    ins_ = v.tensor_copy(t1qap(c), ps0[0:16, 512:1024])
                    ps_rel(ins_, 0)
            t1_all = cnt["sPS0"]

            for ic in range(4):
                for c in range(4):
                    a2 = (4 * ic + c) % 2
                    @blk.tensor
                    def _(t, ic=ic, c=c, a2=a2, t1_all=t1_all):
                        if c == 0:
                            wt(t, "sPS0", t1_all)
                        ps_acq(t, a2)
                        ins_ = t.matmul(PS[a2][:, 0:512],
                                        B[2][0:32, ic * 128:(ic + 1) * 128],
                                        t1ap(c), start=True, stop=True)
                        ins_ = t.matmul(PS[a2][:, 512:1024], B[2][0:16, 896:1024],
                                        t1qap(c), start=True, stop=True)
                        inc(ins_, "sT")
                    ab_done = cnt["sT"]

                    @blk.vector
                    def _(v, rr=rr, ic=ic, c=c, a2=a2, ab_done=ab_done):
                        wt(v, "sT", ab_done)
                        ins_ = v.tensor_copy(B[4][:, 0:512], PS[a2][:, 512:1024])
                        if c == 0:
                            ins_ = v.tensor_tensor(B[6][:, 0:512], PS[a2][:, 0:512],
                                                   B[4][:, 0:512], OP.mult)
                        else:
                            ins_ = v.tensor_tensor(B[6][:, 512:1024],
                                                   PS[a2][:, 0:512],
                                                   B[4][:, 0:512], OP.mult)
                            dst = (Dv[4 * rr + ic][:, 0:512] if c == 3 else
                                   B[6][:, 0:512])
                            ins_ = v.tensor_tensor(dst, B[6][:, 0:512],
                                                   B[6][:, 512:1024], OP.add)
                        ps_rel(ins_, a2)
                        if c == 3:
                            inc(v.memset(stq[:, 7:8], 0.0), "sV")
                acc_done = cnt["sV"]

                @blk.scalar
                def _(s, rr=rr, ic=ic, acc_done=acc_done):
                    wt(s, "sV", acc_done)
                    inc(s.activation(stq[:, 7:8], stq[:, 6:7], AF.Identity), "sC")
        wt_cast = cnt["sC"]

        @blk.vector
        def _(v, wt_cast=wt_cast):
            wt(v, "sC", wt_cast)
            for ic in range(4):
                ins_ = v.tensor_scalar_mul(Dv[8 + ic][:, 0:512],
                                           Dv[4 + ic][:, 0:512], -1.0)
            inc(ins_, "sV")
        ffn1_wbuilt = cnt["sV"]
        ck(41)

        def ffn_lhs(i, j):
            if j < 4:
                if i < 4:
                    return Dv[j][:, i * 128:(i + 1) * 128]
                return Dv[4 + j][:, (i - 4) * 128:(i - 3) * 128]
            if i < 4:
                return Dv[8 + (j - 4)][:, i * 128:(i + 1) * 128]
            return Dv[j - 4][:, (i - 4) * 128:(i - 3) * 128]

        tap("t_ffnw", [Dv[i][:, 0:512] for i in range(12)],
            [("sV", ffn1_wbuilt)])

        @blk.sync
        def _(sy):
            wt(sy, "sNm", attn_done)
            for j in range(8):
                inc(sy.dma_start(out=C[j][:, 0:1026],
                                 in_=CS_h[j * 128:(j + 1) * 128, :]), "dS", 16)
        dS_cs = cnt["dS"]

        @blk.gpsimd
        def _(g):
            wt(g, "sNm", attn_done)
            for j in range(8):
                inc(g.dma_start(out=B[j][:, 1024:2048],
                                in_=WkT[j * 128:(j + 1) * 128, :]), "dG", 16)
        dg_kstage = cnt["dG"]

        xn2_ready, xn2_q = emit_ln(X, A, [[xdone[i]] for i in range(8)], dst_free=pj_mark,
                            db=pj_mark)

        ck(5)
        fold_marks = {}
        ev_marks = []
        for i in range(8):
            @blk.tensor
            def _(t, i=i):
                if i == 0:
                    wt(t, "dG", dg_kstage)
                    wt(t, "dS", dS_cs)
                a = i % 2
                ps_acq(t, a)
                for j in range(8):
                    for pc in range(2):
                        ins_ = t.matmul(PS[a][:, pc * 512:(pc + 1) * 512],
                                        B[j][:, 1024 + i * 128:1024 + (i + 1) * 128],
                                        C[j][:, pc * 512:(pc + 1) * 512],
                                        start=(j == 0), stop=(j == 7))
                    ins_ = t.matmul(PS[a][:, 1024:1026],
                                    B[j][:, 1024 + i * 128:1024 + (i + 1) * 128],
                                    C[j][:, 1024:1026],
                                    start=(j == 0), stop=(j == 7))
                inc(ins_, "sWc")
            mmf = cnt["sWc"]
            if i == 7:
                fold_marks["k"] = mmf

            @blk.scalar
            def _(s, i=i, mmf=mmf):
                wt(s, "sWc", mmf)
                a = i % 2
                ins_ = s.activation(B[i][:, 0:1024], PS[a][:, 0:1024], AF.Identity)
                ins_ = s.activation(WNY[:, i:i + 1], PS[a][:, 1024:1025],
                                    AF.Identity)
                ps_rel(ins_, a)
            ev_marks.append(cnt["sPS%d" % (i % 2)])

        xdone, ffn1_mark = emit_proj_residual(
            lhs_fn=ffn_lhs, rhs=A, bias_base=24, w_stream=None,
            first_waits=[("sV", ffn1_wbuilt)], rhs_qmarks=xn2_q)

        ck(6)
        tap("t_x6", [x[:, :] for x in X], xdone)
        xn3_ready, xn3_q = emit_ln(X, A, [[xdone[i]] for i in range(8)], dst_free=ffn1_mark,
                            db=pj_mark)

        ck(7)
        @blk.gpsimd
        def _(g):
            wt(g, "sV", ffn1_wbuilt)
            for i in range(12):
                inc(g.dma_start(out=wsth[i * 128:(i + 1) * 128, :],
                                in_=Dv[i][:, 0:512]), "dG", 16)
        w_stash = cnt["dG"]

        @blk.gpsimd
        def _(g):
            wt(g, ffn1_mark[0], ffn1_mark[1])
            for j in range(8):
                inc(g.dma_start(out=Dv[8 + j][:, 0:1024],
                                in_=WvT[j * 128:(j + 1) * 128, :]), "dG", 16)
        dg_vstage = cnt["dG"]

        for i in range(8):
            @blk.tensor
            def _(t, i=i):
                if i == 0:
                    wt(t, "dG", dg_vstage)
                a = i % 2
                ps_acq(t, a)
                for j in range(8):
                    for pc in range(2):
                        ins_ = t.matmul(PS[a][:, pc * 512:(pc + 1) * 512],
                                        Dv[8 + j][:, i * 128:(i + 1) * 128],
                                        C[j][:, pc * 512:(pc + 1) * 512],
                                        start=(j == 0), stop=(j == 7))
                    ins_ = t.matmul(PS[a][:, 1024:1026],
                                    Dv[8 + j][:, i * 128:(i + 1) * 128],
                                    C[j][:, 1024:1026],
                                    start=(j == 0), stop=(j == 7))
                inc(ins_, "sWc")
            mmf = cnt["sWc"]
            if i == 7:
                fold_marks["v"] = mmf

            @blk.scalar
            def _(s, i=i, mmf=mmf):
                wt(s, "sWc", mmf)
                if i == 0:
                    wt(s, "dG", w_stash)
                    wt(s, ffn1_mark[0], ffn1_mark[1])
                a = i % 2
                ins_ = s.activation(Dv[i][:, 0:1024], PS[a][:, 0:1024], AF.Identity)
                ins_ = s.activation(WNY[:, 8 + i:9 + i], PS[a][:, 1024:1025],
                                    AF.Identity)
                ps_rel(ins_, a)
            ev_marks.append(cnt["sPS%d" % (i % 2)])

        @blk.tensor
        def _(t):
            wt(t, "dG", dG_prolog)
            for si, (cb0, sl0) in enumerate(((0, 40), (8, 49))):
                a = si % 2
                ps_acq(t, a)
                for cchunk in range(8):
                    for j in range(8):
                        ins_ = t.matmul(PS[a][:, cchunk:cchunk + 1],
                                        C[j][:, cchunk * 128:(cchunk + 1) * 128],
                                        colB[:, cb0 + j:cb0 + j + 1],
                                        start=(j == 0), stop=(j == 7))
                for j in range(8):
                    ins_ = t.matmul(PS[a][0:1, 8:9],
                                    C[j][:, 1024:1025],
                                    colB[:, cb0 + j:cb0 + j + 1],
                                    start=(j == 0), stop=(j == 7))
                inc(ins_, "sT")
        bias_mm = cnt["sT"]

        @blk.vector
        def _(v):
            wt(v, "sT", bias_mm)
            for si, sl0 in ((0, 40), (1, 49)):
                a = si % 2
                for cchunk in range(8):
                    ins_ = v.tensor_copy(colC[:, sl0 + cchunk:sl0 + cchunk + 1],
                                         PS[a][:, cchunk:cchunk + 1])
                ins_ = v.tensor_copy(colC[0:1, sl0 + 8:sl0 + 9], PS[a][0:1, 8:9])
                ps_rel(ins_, a)
        bias_cols_done = cnt["sPS1"]
        bias_cols_done0 = cnt["sPS0"]

        tap("t_wkcs", [b_[:, 0:1024] for b_ in B],
            [("sPS0", cnt["sPS0"]), ("sPS1", cnt["sPS1"])])
        tap("t_wvcs", [d_[:, 0:1024] for d_ in Dv[0:8]],
            [("sPS0", cnt["sPS0"]), ("sPS1", cnt["sPS1"])])
        tap("t_wny", [WNY[:, :]], [("sPS0", cnt["sPS0"]), ("sPS1", cnt["sPS1"])])
        tap("t_bcol", [colC[:, :]], [("sPS0", cnt["sPS0"]), ("sPS1", cnt["sPS1"])])
        q_marks = []
        wcq = []
        for i in range(8):
            @blk.gpsimd
            def _(g, i=i):
                if i == 0:
                    wt(g, "sPV", sPV0 + 512)
                    wt(g, "sPS0", bias_cols_done0)
                if i >= 2:
                    wt(g, "sWc", wcq[i - 2])
                inc(g.dma_start(
                    out=PT[i % 2][:, :].rearrange("p (c m) -> p c m", m=128),
                    in_=Wq_b[:, i * 128:(i + 1) * 128].rearrange("(c p) m -> p c m", p=128)),
                    "dG", 16)
            dg_q = cnt["dG"]

            @blk.tensor
            def _(t, i=i, dg_q=dg_q):
                wt(t, "dG", dg_q)
                a = i % 2
                ps_acq(t, a)
                for pc in range(4):
                    if i < 2:
                        wt(t, "sV", xn3_q[pc])
                    for j in range(8):
                        ins_ = t.matmul(PS[a][:, pc * 512:(pc + 1) * 512],
                                        PT[i % 2][:, j * 128:(j + 1) * 128],
                                        A[j][:, pc * 512:(pc + 1) * 512],
                                        start=(j == 0), stop=(j == 7))
                inc(ins_, "sWc")
            wcq.append(cnt["sWc"])
            mm_q = cnt["sWc"]

            @blk.scalar
            def _(s, i=i, mm_q=mm_q):
                wt(s, "sWc", mm_q)
                a = i % 2
                ins_ = s.activation(C[i][:, :], PS[a][:, :], AF.Identity,
                                    bias=colC[:, 32 + i:33 + i])
                ps_rel(ins_, a)
            q_marks.append(cnt["sPS%d" % (i % 2)])
        q_done_marks = list(q_marks)

        @blk.gpsimd
        def _(g):
            wt(g, "sWc", fold_marks["k"])
            for r in range(8):
                for pc in range(2):
                    inc(g.dma_start(
                        out=B[r][:, 1024 + pc * 512:1536 + pc * 512],
                        in_=IDFT_h[r * 128:(r + 1) * 128, pc * 512:(pc + 1) * 512]),
                        "dG", 16)

        tap("t_qb", [c_[:, :] for c_ in C],
            [("sPS0", cnt["sPS0"]), ("sPS1", cnt["sPS1"])])
        prod_marks = []
        sf0_free_marks = []
        _dbg2 = [0]
        for fc in range(4):
            for tq in range(4):
                it = 4 * fc + tq
                @blk.tensor
                def _(t, fc=fc, tq=tq, it=it):
                    if it < 4:
                        wt(t, "sV", xn3_q[tq])
                    ps_acq(t, 1)
                    for half, f0 in ((0, fc), (1, 4 + fc)):
                        for j in range(8):
                            ins_ = t.matmul(
                                ps1[:, half * 512:(half + 1) * 512],
                                Dv[j][:, f0 * 128:(f0 + 1) * 128],
                                A[j][:, tq * 512:(tq + 1) * 512],
                                start=(j == 0), stop=(j == 7))
                    inc(ins_, "sT")
                    ps_acq(t, 0)
                    for half, f0 in ((0, fc), (1, 4 + fc)):
                        for j in range(8):
                            ins_ = t.matmul(
                                ps0[:, half * 512:(half + 1) * 512],
                                B[j][:, f0 * 128:(f0 + 1) * 128],
                                A[j][:, tq * 512:(tq + 1) * 512],
                                start=(j == 0), stop=(j == 7))
                    inc(ins_, "sT")
                vmm = cnt["sT"] - 1
                kmm = cnt["sT"]

                @blk.scalar
                def _(s, fc=fc, vmm=vmm, it=it):
                    wt(s, "sT", vmm)
                    if it >= 1:
                        wt(s, "sV", sf0_free_marks[it - 1])
                    ins_ = s.activation(SF0[:, 0:512], ps1[:, 0:512], AF.Identity,
                                        bias=colC[:, 49 + fc:50 + fc])
                    ins_ = s.activation(SF0[:, 512:1024], ps1[:, 512:1024],
                                        AF.Identity, bias=colC[:, 53 + fc:54 + fc])
                    ps_rel(ins_, 1)
                vev = cnt["sPS1"]

                if DEBUG_TAPS and fc == 0 and tq == 0:
                    @blk.vector
                    def _(v, vev=vev, kmm=kmm):
                        wt(v, "sPS1", vev)
                        wt(v, "sT", kmm)
                        ins_ = v.tensor_copy(PT[0][:, :], ps0[:, 0:1024])
                        ins_ = v.tensor_copy(PT[1][:, :], SF0[:, 0:1024])
                        inc(ins_, "sV")
                    dbg_mark = cnt["sV"]
                    tap("t_k0", [PT[0][:, :]], [("sV", dbg_mark)])
                    tap("t_v0", [PT[1][:, :]], [("sV", dbg_mark)])
                    dbg_ds = cnt["dS"]

                @blk.vector
                def _(v, fc=fc, tq=tq, vev=vev, kmm=kmm):
                    if DEBUG_TAPS and fc == 0 and tq == 0:
                        wt(v, "dS", dbg_ds)
                    if DEBUG_TAPS and it >= 1:
                        wt(v, "dS", _dbg2[0])
                    wt(v, "sPS1", vev)
                    wt(v, "sT", kmm)
                    combos = [(0, 0, 0), (1, 1, 1), (0, 1, 2), (1, 0, 3)]
                    tmps = [ps0[:, 1024:1536], ps0[:, 1536:2048]]
                    red_q = []
                    for idx, (kh, vh, k) in enumerate(combos):
                        bcol = 40 + fc if kh == 0 else 44 + fc
                        if idx >= 2:
                            pk, ptmp = red_q.pop(0)
                            v.reduce_sum(RED[:, it * 4 + pk:it * 4 + pk + 1],
                                         ptmp, axis=AX.X)
                        v.scalar_tensor_tensor(
                            tmps[idx % 2], ps0[:, kh * 512:(kh + 1) * 512],
                            colC[:, bcol:bcol + 1],
                            SF0[:, vh * 512:(vh + 1) * 512],
                            OP.add, OP.mult)
                        red_q.append((k, tmps[idx % 2]))
                    inc(v.memset(stq[:, 6:7], 0.0), "sV")
                    for pk, ptmp in red_q:
                        ins_ = v.reduce_sum(RED[:, it * 4 + pk:it * 4 + pk + 1],
                                            ptmp, axis=AX.X)
                    ps_rel(ins_, 0)
                    inc(v.memset(stq[:, 7:8], 0.0), "sV")
                sf0_free_marks.append(cnt["sV"] - 1)
                prod_marks.append(cnt["sV"])
                if DEBUG_TAPS:
                    tap("t_sh%d" % it, [Shat[:, 0:12]], [("sV", prod_marks[it])])
                    _dbg2[0] = cnt["dS"]
        shat_main = cnt["sV"]
        last_vmm = vmm

        @blk.tensor
        def _(t):
            ps_acq(t, 1)
            for j in range(8):
                for pc in range(4):
                    ins_ = t.matmul(ps1[0:1, pc * 512:(pc + 1) * 512],
                                    WNY[:, 8 + j:9 + j],
                                    A[j][:, pc * 512:(pc + 1) * 512],
                                    start=(j == 0), stop=(j == 7))
            inc(ins_, "sT")
            ps_acq(t, 0)
            for j in range(8):
                for pc in range(4):
                    ins_ = t.matmul(ps0[0:1, pc * 512:(pc + 1) * 512],
                                    WNY[:, j:j + 1],
                                    A[j][:, pc * 512:(pc + 1) * 512],
                                    start=(j == 0), stop=(j == 7))
            inc(ins_, "sT")
        ny_v = cnt["sT"] - 1
        ny_k = cnt["sT"]

        @blk.scalar
        def _(s):
            wt(s, "sT", ny_v)
            ins_ = s.activation(SBF0[0:1, 0:2048], ps1[0:1, 0:2048], AF.Identity,
                                bias=colC[0:1, 57:58])
            ps_rel(s.activation(stq[:, 7:8], stq[:, 6:7], AF.Identity), 1)
        ny_vev = cnt["sPS1"]

        @blk.vector
        def _(v):
            wt(v, "sPS1", ny_vev)
            wt(v, "sT", ny_k)
            ins_ = v.scalar_tensor_tensor(ps1[0:1, 1024:2048], ps0[0:1, 0:1024],
                                          colC[0:1, 48:49], SBF0[0:1, 0:1024],
                                          OP.add, OP.mult)
            ins_ = v.reduce_sum(RED[0:1, 64:65], ps1[0:1, 1024:2048], axis=AX.X)
            ins_ = v.scalar_tensor_tensor(ps1[0:1, 1024:2048], ps0[0:1, 1024:2048],
                                          colC[0:1, 48:49], SBF0[0:1, 1024:2048],
                                          OP.add, OP.mult)
            ins_ = v.reduce_sum(RED[0:1, 65:66], ps1[0:1, 1024:2048], axis=AX.X)
            ps_rel(ins_, 0)
            inc(v.memset(stq[:, 6:7], 0.0), "sV")
        ny_red = cnt["sV"]

        @blk.gpsimd
        def _(g):
            wt(g, "sV", ny_red)
            inc(g.dma_start(out=SBF0[0:1, 0:1024], in_=IDFT_h[1024:1025, :]),
                "dG", 16)
        idft_pieces_dg = cnt["dG"]

        @blk.scalar
        def _(s):
            wt(s, "sV", ny_red)
            inc(s.activation(stq[:, 7:8], stq[:, 6:7], AF.Identity), "sC")
        hop1 = cnt["sC"]

        @blk.vector
        def _(v):
            wt(v, "sC", hop1)
            rv = RED[:, 0:64].rearrange("p (i k) -> p i k", k=4)
            ins_ = v.tensor_tensor(Shat[:, 8:24], rv[:, :, 0:1], rv[:, :, 1:2],
                                   OP.subtract)
            ins_ = v.tensor_tensor(Shat[:, 24:40], rv[:, :, 2:3], rv[:, :, 3:4],
                                   OP.add)
            inc(v.memset(stq[:, 6:7], 0.0), "sV")
        bulk1 = cnt["sV"]

        @blk.scalar
        def _(s):
            wt(s, "sV", bulk1)
            inc(s.activation(stq[:, 7:8], stq[:, 6:7], AF.Identity), "sC")
        hop2 = cnt["sC"]

        @blk.vector
        def _(v):
            wt(v, "sC", hop2)
            ins_ = v.reduce_sum(Shat[:, 0:4],
                                Shat[:, 8:24].rearrange("p (f t) -> p f t", t=4),
                                axis=AX.X)
            ins_ = v.reduce_sum(Shat[:, 4:8],
                                Shat[:, 24:40].rearrange("p (f t) -> p f t", t=4),
                                axis=AX.X)
            inc(v.memset(stq[:, 6:7], 0.0), "sV")
        shat_done = cnt["sV"]

        @blk.scalar
        def _(s):
            wt(s, "sV", shat_done)
            ins_ = s.activation(Shat[0:1, 8:9], RED[0:1, 64:65], AF.Identity,
                                bias=RED[0:1, 65:66])
            inc(s.activation(stq[:, 7:8], stq[:, 6:7], AF.Identity), "sC")
        sny_done = cnt["sC"]

        tap("t_shat", [Shat[:, :]], [("sV", shat_done), ("sC", sny_done)])
        @blk.vector
        def _(v):
            wt(v, "sV", shat_done)
            wt(v, "sC", sny_done)
            inc(v.tensor_copy(colB[:, 0:9], Shat[:, 0:9]), "sV")
        shatbf_done = cnt["sV"]

        @blk.tensor
        def _(t):
            wt(t, "sV", shatbf_done)
            wt(t, "dG", idft_pieces_dg)
            ps_acq(t, 1)
            for pc in range(2):
                for r in range(9):
                    if r < 8:
                        lhs = colB[:, r:r + 1]
                        rhs = B[r][:, 1024 + pc * 512:1536 + pc * 512]
                    else:
                        lhs = colB[0:1, 8:9]
                        rhs = SBF0[0:1, pc * 512:(pc + 1) * 512]
                    ins_ = t.matmul(ps1[0:1, pc * 512:(pc + 1) * 512],
                                    lhs, rhs, start=(r == 0), stop=(r == 8))
            inc(ins_, "sT")
        s_time_done = cnt["sT"]

        @blk.scalar
        def _(s):
            wt(s, "sT", s_time_done)
            ins_ = s.activation(s_bf[0:1, 0:1024], ps1[0:1, 0:1024], AF.Identity)
            ins_ = s.activation(s_bf[0:1, 1024:2048], ps1[0:1, 0:1024], AF.Identity)
            inc(s.activation(stq[:, 7:8], stq[:, 6:7], AF.Identity), "sC")
            ps_rel(s.activation(stq[:, 7:8], stq[:, 6:7], AF.Identity), 1)
        sbf_done = cnt["sC"]

        @blk.sync
        def _(sy):
            wt(sy, "sC", sbf_done)
            wt(sy, "sWc", fold_marks["v"])
            inc(sy.dma_start(out=s2h, in_=s_bf[0:1, :]), "dS", 16)
        s2_w = cnt["dS"]

        @blk.sync
        def _(sy):
            wt(sy, "dS", s2_w)
            for j in range(8):
                win = bass.AP(s2h.tensor, j * 128, [[1, 128], [1, 1024]])
                inc(sy.dma_start(out=B[j][:, 1024:2048], in_=win), "dS", 16)
        circ_done = cnt["dS"]

        tap("t_sbf", [s_bf[0:1, :]], [("dS", circ_done)])
        tap("t_circ", [b_[:, 1024:2048] for b_ in B], [("dS", circ_done)])
        fin_marks = []
        for mc in range(8):
            @blk.tensor
            def _(t, mc=mc):
                if mc == 0:
                    wt(t, "dS", circ_done)
                a = mc % 2
                ps_acq(t, a)
                for j in range(8):
                    for pc in range(4):
                        ins_ = t.matmul(PS[a][:, pc * 512:(pc + 1) * 512],
                                        B[j][:, 1024 + mc * 128:1024 + (mc + 1) * 128],
                                        C[j][:, pc * 512:(pc + 1) * 512],
                                        start=(j == 0), stop=(j == 7))
                inc(ins_, "sT")
            mmf = cnt["sT"]

            @blk.vector
            def _(v, mc=mc, mmf=mmf):
                wt(v, "sT", mmf)
                a = mc % 2
                ins_ = v.tensor_tensor(X[mc][:, :], PS[a][:, :], X[mc][:, :], OP.add)
                ps_rel(ins_, a)
            fin_marks.append(("sPS%d" % (mc % 2), cnt["sPS%d" % (mc % 2)]))
        final_mm_done = cnt["sT"]

        @blk.gpsimd
        def _(g):
            wt(g, "dG", w_stash)
            wt(g, "sT", last_vmm)
            for i in range(8):
                inc(g.dma_start(out=Dv[i][:, 0:512],
                                in_=wsth[i * 128:(i + 1) * 128, :]), "dG", 16)
        w_reload = cnt["dG"]

        ck(9)
        xn4_ready, xn4_q = emit_ln(X, A, [[fin_marks[i]] for i in range(8)],
                            dst_free=("sT", ny_k), db=("sWc", wcq[-1]))

        ck(95)
        def ffn_lhs2(i, j):
            if j < 4:
                if i < 4:
                    return Dv[j][:, i * 128:(i + 1) * 128]
                return Dv[4 + j][:, (i - 4) * 128:(i - 3) * 128]
            if i < 4:
                return Dv[8 + (j - 4)][:, i * 128:(i + 1) * 128]
            return Dv[j - 4][:, (i - 4) * 128:(i - 3) * 128]

        xdone_f, _mk2 = emit_proj_residual(
            lhs_fn=ffn_lhs2, rhs=A, bias_base=24, w_stream=None,
            first_waits=[("dG", w_reload)], rhs_qmarks=xn4_q)

        @blk.sync
        def _(sy):
            for i in range(8):
                wt(sy, xdone_f[i][0], xdone_f[i][1])
                inc(sy.dma_start(out=outT[i * 128:(i + 1) * 128, :],
                                 in_=X[i][:, :]), "sST", 16)
            sy.wait_ge(sems["sST"], cnt["sST"])
            if DEBUG_TAPS:
                sy.wait_ge(sems["dS"], cnt["dS"])

    return nc


_NC_CACHE = []


def kernel(**inputs):
    consts = _host_consts()
    x = np.ascontiguousarray(inputs["L0_attn_a_data"], dtype=np.float32)
    f32 = lambda a: np.ascontiguousarray(a, dtype=np.float32)
    base = {
        "Wqkv": f32(inputs["L0_attn_a_W_qkv"]),
        "bqkv": f32(inputs["L0_attn_a_b_qkv"]),
        "Wout": f32(inputs["L0_attn_a_W_out"]),
        "bout": f32(inputs["L0_attn_a_b_out"]),
        "Wq_b": f32(inputs["L0_attn_b_Wq"]),
        "bq_b": f32(inputs["L0_attn_b_bq"]),
        "WkT": np.ascontiguousarray(np.asarray(inputs["L0_attn_b_Wk"], dtype=np.float32).T),
        "bk_b": f32(inputs["L0_attn_b_bk"]),
        "WvT": np.ascontiguousarray(np.asarray(inputs["L0_attn_b_Wv"], dtype=np.float32).T),
        "bv_b": f32(inputs["L0_attn_b_bv"]),
        "G1R": f32(inputs["L0_ffn_G1_R"]),
        "G2R": f32(inputs["L0_ffn_G2_R"]),
        "G1I": f32(inputs["L0_ffn_G1_I"]),
        "G2I": f32(inputs["L0_ffn_G2_I"]),
        "fbias": np.concatenate([f32(inputs["L0_ffn_bias_real"]),
                                 f32(inputs["L0_ffn_bias_imag"])]),
    }
    base.update(consts)
    nc = build_nc()
    in_maps = []
    for b in range(8):
        m = dict(base)
        m["xT"] = np.ascontiguousarray(x[b].T)
        in_maps.append(m)
    res = run_bass_kernel_spmd(nc, in_maps, core_ids=list(range(8)))
    out = np.stack([np.asarray(res.results[b]["outT"]).T for b in range(8)])
    return np.ascontiguousarray(out, dtype=np.float32)
